# version 07
# baseline (speedup 1.0000x reference)
import numpy as np
import ml_dtypes
import concourse.bass as bass
import concourse.mybir as mybir
from concourse.alu_op_type import AluOpType as OP
from concourse.bass_utils import run_bass_kernel_spmd
from contextlib import ExitStack

F32 = mybir.dt.float32
BF16 = mybir.dt.bfloat16
AF = mybir.ActivationFunctionType
AX = mybir.AxisListType

NCORE = 8
BPC = 4
S = 2048
D = 1024
SCH = S // 128
DCH = D // 128
FF = 4096
FCH = FF // 128
ET = 4096
ETCH = ET // 32 // 4
NEGBIG = -1.0e30
SC0 = float(128.0 ** -0.5)
SC1 = float(64.0 ** -0.5)
GC1 = 1.5957691216057308
GC2 = 0.07135481282426948

_CACHE = {}


class Cx:
    def __init__(self, nc, block, st):
        self.nc, self.block, self.st = nc, block, st
        self.c = {}
        self.s = {}
        for name in ("pe", "dve", "act", "ld", "gd", "cc"):
            self.s[name] = st.enter_context(nc.semaphore("sem_" + name))
            self.c[name] = 0

    def run(self, eng, fn):
        getattr(self.block, eng)(fn)

    def snap(self):
        return dict(self.c)

    def waits(self, e, snap, names=None):
        for n in (names if names is not None else list(snap)):
            if snap.get(n, 0) > 0:
                e.wait_ge(self.s[n], snap[n])

    def inc(self, instr, name, n=1):
        instr.then_inc(self.s[name], n)
        self.c[name] += n
        return self.c[name]

    def vstep(self, fn):
        def blk(e, fn=fn):
            if self.c["dve"] > 0:
                e.wait_ge(self.s["dve"], self.c["dve"])
            self.inc(fn(e), "dve")
        self.run("vector", blk)

    def astep(self, fn):
        def blk(e, fn=fn):
            if self.c["act"] > 0:
                e.wait_ge(self.s["act"], self.c["act"])
            self.inc(fn(e), "act")
        self.run("scalar", blk)

    def sb(self, name, shape, dt):
        return self.st.enter_context(self.nc.sbuf_tensor(name, shape, dt))

    def ps(self, name, shape, dt):
        return self.st.enter_context(self.nc.psum_tensor(name, shape, dt))


def cr_ap(dram, width):
    return dram.rearrange("(c p) w -> p c w", p=128)


def colblock_ap(dram, c0, w):
    return dram[:, c0:c0 + w].rearrange("(c p) w -> p c w", p=128)


def s3(tile_ap, w):
    return tile_ap.rearrange("p (c w) -> p c w", w=w)



def emit_threshold_search(cx, bar, search_dram, kk_dram, cc_in, cc_out,
                          thr8_sbuf, n_iter=40):
    W = 32768
    av = cx.sb("thr_a", [128, W], F32)
    dump = cx.sb("thr_dump", [128, W], BF16)
    lo = cx.sb("thr_lo", [128, 1], F32)
    hi = cx.sb("thr_hi", [128, 1], F32)
    mid = cx.sb("thr_m", [128, 1], F32)
    kkb = cx.sb("thr_kkb", [128, 1], F32)
    pcnt = cx.sb("thr_pcnt", [128, 1], F32)
    crow = cx.sb("thr_crow", [1, 128], F32)
    ctot = cx.sb("thr_ctot", [1, 1], F32)
    cntb = cx.sb("thr_cntb", [128, 1], F32)
    geb = cx.sb("thr_geb", [128, 1], F32)
    gnb = cx.sb("thr_gnb", [128, 1], F32)
    tt1 = cx.sb("thr_tt1", [128, 1], F32)
    tt2 = cx.sb("thr_tt2", [128, 1], F32)
    skk = cx.sb("thr_skk", [1, 1], F32)
    PF = cx.PB

    def sync_f(e):
        cx.waits(e, bar)
        cx.inc(e.dma_start(av[:, :], search_dram[:, :]), "ld", 16)
        cx.inc(e.dma_start(skk[:, :], kk_dram[:, :]), "ld", 16)
    cx.run("sync", sync_f)
    ld0 = cx.c["ld"]

    def act_p0(e):
        e.wait_ge(cx.s["ld"], ld0)
        cx.inc(e.activation(av[:, :], av[:, :], AF.Abs), "act")
    cx.run("scalar", act_p0)
    a0 = cx.c["act"]

    def dve_p1(e):
        e.wait_ge(cx.s["act"], a0)
        e.memset(lo[:, :], -1.0)
        cx.inc(e.reduce_max(pcnt[:, :], av[:, :], axis=AX.X), "dve")
    cx.run("vector", dve_p1)
    d0 = cx.c["dve"]

    def pe_p1(e):
        e.wait_ge(cx.s["dve"], d0)
        cx.inc(e.transpose(PF[0:1, 0:128], pcnt[:, :], cx.identf[:, :]), "pe")
    cx.run("tensor", pe_p1)
    p0_ = cx.c["pe"]

    def dve_p2(e):
        e.wait_ge(cx.s["pe"], p0_)
        cx.inc(e.tensor_copy(crow[:, :], PF[0:1, 0:128]), "dve")
    cx.run("vector", dve_p2)
    cx.vstep(lambda e: e.reduce_max(ctot[:, :], crow[:, :], axis=AX.X))
    d1 = cx.c["dve"]

    def pe_p2(e):
        e.wait_ge(cx.s["dve"], d1)
        e.matmul(PF[:, 128:129], cx.onesf[:, :], ctot[:, :], start=True, stop=True)
        cx.inc(e.matmul(PF[:, 129:130], cx.onesf[:, :], skk[:, :], start=True, stop=True), "pe")
    cx.run("tensor", pe_p2)
    p1_ = cx.c["pe"]

    def dve_p3(e):
        e.wait_ge(cx.s["pe"], p1_)
        e.tensor_copy(hi[:, :], PF[:, 128:129])
        cx.inc(e.tensor_copy(kkb[:, :], PF[:, 129:130]), "dve")
    cx.run("vector", dve_p3)

    for _ in range(n_iter):
        cx.vstep(lambda e: e.tensor_tensor(mid[:, :], lo[:, :], hi[:, :], OP.add))
        cx.vstep(lambda e: e.tensor_scalar(mid[:, :], mid[:, :], 0.5, None, OP.mult, OP.bypass))
        cx.vstep(lambda e: e.tensor_scalar(dump[:, :], av[:, :], mid[:, :], None,
                                           OP.is_le, OP.add, accum_out=pcnt[:, :]))
        da = cx.c["dve"]

        def pe_a(e, da=da):
            e.wait_ge(cx.s["dve"], da)
            cx.inc(e.transpose(PF[0:1, 0:128], pcnt[:, :], cx.identf[:, :]), "pe")
        cx.run("tensor", pe_a)
        pa_ = cx.c["pe"]

        def dve_b(e, pa_=pa_):
            e.wait_ge(cx.s["pe"], pa_)
            cx.inc(e.tensor_copy(crow[:, :], PF[0:1, 0:128]), "dve")
        cx.run("vector", dve_b)
        cx.vstep(lambda e: e.reduce_sum(ctot[:, :], crow[:, :], axis=AX.X))
        db = cx.c["dve"]

        def pe_b(e, db=db):
            e.wait_ge(cx.s["dve"], db)
            cx.inc(e.matmul(PF[:, 128:129], cx.onesf[:, :], ctot[:, :],
                            start=True, stop=True), "pe")
        cx.run("tensor", pe_b)
        pb_ = cx.c["pe"]

        def dve_c(e, pb_=pb_):
            e.wait_ge(cx.s["pe"], pb_)
            cx.inc(e.tensor_copy(cntb[:, :], PF[:, 128:129]), "dve")
        cx.run("vector", dve_c)
        cx.vstep(lambda e: e.tensor_scalar(geb[:, :], cntb[:, :], kkb[:, :], None,
                                           OP.is_ge, OP.bypass))
        cx.vstep(lambda e: e.tensor_scalar(gnb[:, :], geb[:, :], -1.0, 1.0, OP.mult, OP.add))

        def bl1(e):
            e.tensor_tensor(tt1[:, :], mid[:, :], geb[:, :], OP.mult)
            return e.tensor_tensor(tt2[:, :], hi[:, :], gnb[:, :], OP.mult)
        cx.vstep(bl1)
        cx.vstep(lambda e: e.tensor_tensor(hi[:, :], tt1[:, :], tt2[:, :], OP.add))

        def bl2(e):
            e.tensor_tensor(tt1[:, :], mid[:, :], gnb[:, :], OP.mult)
            return e.tensor_tensor(tt2[:, :], lo[:, :], geb[:, :], OP.mult)
        cx.vstep(bl2)
        cx.vstep(lambda e: e.tensor_tensor(lo[:, :], tt1[:, :], tt2[:, :], OP.add))

    dfin = cx.c["dve"]

    def pool_fin(e):
        e.wait_ge(cx.s["dve"], dfin)
        cx.inc(e.dma_start(cc_in[0:1, 0:1], hi[0:1, 0:1]), "gd", 16)
        e.wait_ge(cx.s["gd"], cx.c["gd"])
        cx.inc(e.collective_compute(
            "AllGather", mybir.AluOpType.bypass,
            replica_groups=[list(range(NCORE))],
            ins=[cc_in.ap().opt()], outs=[cc_out.ap().opt()]), "cc")
        e.wait_ge(cx.s["cc"], cx.c["cc"])
        cx.inc(e.dma_start(thr8_sbuf[:, :], cc_out[:, 0:1].rearrange("r one -> one r"),
                           allow_slow_non_contiguous=True), "gd", 16)
    cx.run("gpsimd", pool_fin)
    return cx.snap()


def build_thr_program():
    nc = bass.Bass("TRN2", target_bir_lowering=False)
    search = nc.dram_tensor("search", [128, 32768], F32, kind="ExternalInput")
    kk = nc.dram_tensor("kk", [1, 1], F32, kind="ExternalInput")
    identf_d = nc.dram_tensor("identf", [128, 128], F32, kind="ExternalInput")
    thr8 = nc.dram_tensor("thr8", [1, 8], F32, kind="ExternalOutput")
    cc_in = nc.dram_tensor("cc_in", [1, 128], F32)
    cc_out = nc.dram_tensor("cc_out", [8, 128], F32)
    with ExitStack() as st:
        block = st.enter_context(nc.Block())
        cx = Cx(nc, block, st)
        cx.identf = cx.sb("identf_s", [128, 128], F32)
        cx.onesf = cx.sb("onesf_s", [1, 128], F32)
        cx.PB = cx.ps("PB", [128, 1024], F32)
        thr8_s = cx.sb("thr8_s", [1, 8], F32)

        def sync0(e):
            cx.inc(e.dma_start(cx.identf[:, :], identf_d[:, :]), "ld", 16)
        cx.run("sync", sync0)
        l0_ = cx.c["ld"]

        def dve0(e):
            e.wait_ge(cx.s["ld"], l0_)
            cx.inc(e.memset(cx.onesf[:, :], 1.0), "dve")
        cx.run("vector", dve0)
        bar = emit_threshold_search(cx, cx.snap(), search, kk, cc_in, cc_out, thr8_s)

        def pool_out(e):
            e.wait_ge(cx.s["gd"], cx.c["gd"])
            cx.inc(e.dma_start(thr8[:, :], thr8_s[:, :]), "gd", 16)
            e.wait_ge(cx.s["gd"], cx.c["gd"])
        cx.run("gpsimd", pool_out)
    return nc



def alloc_common(cx):
    cx.t_sum = cx.sb("t_sum", [128, 1], F32)
    cx.t_ssq = cx.sb("t_ssq", [128, 1], F32)
    cx.t_mean = cx.sb("t_mean", [128, 1], F32)
    cx.t_m2 = cx.sb("t_m2", [128, 1], F32)
    cx.t_var = cx.sb("t_var", [128, 1], F32)
    cx.t_sd = cx.sb("t_sd", [128, 1], F32)
    cx.t_sdv = cx.sb("t_sdv", [128, 1], F32)
    cx.t_rstd = cx.sb("t_rstd", [128, 1], F32)
    cx.t_nrm = cx.sb("t_nrm", [128, 1], F32)
    cx.t_tau = cx.sb("t_tau", [128, 1], F32)
    cx.t_rt = cx.sb("t_rt", [128, 1], F32)
    cx.t_neg = cx.sb("t_neg", [128, 1], F32)
    cx.t_asum = cx.sb("t_asum", [128, 1], F32)
    cx.t_ra = cx.sb("t_ra", [128, 1], F32)
    cx.ident16 = cx.sb("ident16_s", [128, 128], BF16)
    cx.identf = cx.sb("identf_s", [128, 128], F32)
    cx.onesf = cx.sb("onesf_s", [1, 128], F32)
    cx.epsb = cx.sb("epsb_s", [128, 1], F32)
    cx.ones16 = cx.sb("ones16_s", [1, 128], BF16)
    cx.xch = cx.sb("xch", [128, 2 * D], F32)
    cx.dchunk = cx.sb("dchunk", [128, S], BF16)
    cx.t_dump = cx.dchunk[:, 0:D]
    cx.t_xn = cx.dchunk[:, D:2 * D]
    cx.PA = cx.ps("PA", [128, 2048], F32)
    cx.PB = cx.ps("PB", [128, 1024], F32)
    cx.PTa = cx.ps("PTa", [128, 1024], BF16)
    cx.PTb = cx.ps("PTb", [128, 1024], BF16)
    cx.BIG1 = cx.sb("BIG1", [128, 32768], BF16)
    cx.BIG2 = cx.sb("BIG2", [128, 32768], BF16)
    cx.M3 = cx.sb("M3", [128, 16384], BF16)
    cx.wbuf = cx.sb("wbuf", [128, 8192], BF16)


def emit_load_consts(cx, ident16_d, identf_d):
    def sync_c(e):
        cx.inc(e.dma_start(cx.ident16[:, :], ident16_d[:, :]), "ld", 16)
        cx.inc(e.dma_start(cx.identf[:, :], identf_d[:, :]), "ld", 16)
    cx.run("sync", sync_c)
    lc = cx.c["ld"]

    def dve_c(e):
        e.wait_ge(cx.s["ld"], lc)
        e.memset(cx.onesf[:, :], 1.0)
        e.memset(cx.epsb[:, :], 1e-5)
        cx.inc(e.memset(cx.ones16[:, :], 1.0), "dve")
    cx.run("vector", dve_c)
    return cx.snap()


def emit_vec_cols(cx, e, dram_vec, tile, ncols):
    for c in range(ncols):
        cx.inc(e.dma_start(tile[:, c:c + 1],
                           dram_vec[c * 128:(c + 1) * 128].rearrange("(p one) -> p one", one=1)), "ld", 16)


def emit_ln(cx, bar, xhome, xnT, xnp, xnpT, want_tau, router=False):
    for c in range(SCH):
        slot = (c % 2) * D
        xc = cx.xch[:, slot:slot + D]

        def sync_x(e, c=c, slot=slot):
            if c == 0:
                cx.waits(e, bar)
            else:
                e.wait_ge(cx.s["act"], cx.c["act"])
            cx.inc(e.dma_start(cx.xch[:, slot:slot + D],
                               xhome[c * 128:(c + 1) * 128, :]), "ld", 16)
        cx.run("sync", sync_x)
        lx = cx.c["ld"]

        def dve_1(e, lx=lx, xc=xc):
            e.wait_ge(cx.s["ld"], lx)
            cx.inc(e.reduce_sum(cx.t_sum[:, :], xc, axis=AX.X), "dve")
        cx.run("vector", dve_1)
        d1 = cx.c["dve"]

        def act_1(e, d1=d1, xc=xc):
            e.wait_ge(cx.s["dve"], d1)
            cx.inc(e.activation(cx.t_dump, xc, AF.Square, accum_out=cx.t_ssq[:, :]), "act")
        cx.run("scalar", act_1)
        a1 = cx.c["act"]

        def dve_2(e, a1=a1):
            e.wait_ge(cx.s["act"], a1)
            if cx.c["dve"] > 0:
                e.wait_ge(cx.s["dve"], cx.c["dve"])
            cx.inc(e.tensor_scalar(cx.t_mean[:, :], cx.t_sum[:, :], 1.0 / D, None,
                                   OP.mult, OP.bypass), "dve")
        cx.run("vector", dve_2)
        cx.vstep(lambda e: e.tensor_tensor(cx.t_m2[:, :], cx.t_mean[:, :], cx.t_mean[:, :], OP.mult))
        cx.vstep(lambda e: e.scalar_tensor_tensor(cx.t_var[:, :], cx.t_ssq[:, :], 1.0 / D,
                                                  cx.t_m2[:, :], OP.mult, OP.subtract))
        d2 = cx.c["dve"]

        def act_2(e, d2=d2):
            e.wait_ge(cx.s["dve"], d2)
            if cx.c["act"] > 0:
                e.wait_ge(cx.s["act"], cx.c["act"])
            ii = e.activation(cx.t_sd[:, :], cx.t_var[:, :], AF.Ln, bias=cx.epsb[:, :])
            if want_tau:
                ii = e.activation(cx.t_sdv[:, :], cx.t_var[:, :], AF.Ln)
            cx.inc(ii, "act")
        cx.run("scalar", act_2)
        cx.astep(lambda e: e.activation(cx.t_rstd[:, :], cx.t_sd[:, :], AF.Exp, scale=-0.5))
        if want_tau:
            cx.astep(lambda e: e.activation(cx.t_sdv[:, :], cx.t_sdv[:, :], AF.Exp, scale=0.5))
        a2 = cx.c["act"]

        def dve_3a(e, a2=a2):
            e.wait_ge(cx.s["act"], a2)
            e.wait_ge(cx.s["dve"], cx.c["dve"])
            cx.inc(e.nop(), "dve")
        cx.run("vector", dve_3a)

        def st_xn(e, c=c, xc=xc):
            ii = e.tensor_scalar(cx.t_xn[:, :], xc, cx.t_mean[:, :], cx.t_rstd[:, :],
                                 OP.subtract, OP.mult)
            if router:
                ii = e.tensor_copy(cx.r_mean[:, c:c + 1], cx.t_mean[:, :])
            return ii
        cx.vstep(st_xn)
        if router:
            cx.vstep(lambda e: e.tensor_tensor(cx.t_nrm[:, :], cx.t_rstd[:, :],
                                               cx.t_rstd[:, :], OP.mult))
            cx.vstep(lambda e: e.tensor_tensor(cx.t_nrm[:, :], cx.t_nrm[:, :],
                                               cx.t_var[:, :], OP.mult))
            dvr = cx.c["dve"]

            def act_vr(e, dvr=dvr):
                e.wait_ge(cx.s["dve"], dvr)
                if cx.c["act"] > 0:
                    e.wait_ge(cx.s["act"], cx.c["act"])
                cx.inc(e.activation(cx.t_nrm[:, :], cx.t_nrm[:, :], AF.Ln,
                                    bias=cx.epsb[:, :]), "act")
            cx.run("scalar", act_vr)
            cx.astep(lambda e, c=c: e.activation(cx.r_rstd[:, c:c + 1], cx.t_nrm[:, :],
                                                 AF.Exp, scale=-0.5))
        if want_tau:
            cx.vstep(lambda e: e.tensor_tensor(cx.t_nrm[:, :], cx.t_rstd[:, :],
                                               cx.t_sdv[:, :], OP.mult))
            cx.vstep(lambda e: e.tensor_scalar(cx.t_nrm[:, :], cx.t_nrm[:, :], 32.0, 1e-12,
                                               OP.mult, OP.max))
            cx.vstep(lambda e: e.reciprocal(cx.t_tau[:, :], cx.t_nrm[:, :]))
            cx.vstep(lambda e: e.tensor_tensor(cx.t_rt[:, :], cx.t_rstd[:, :],
                                               cx.t_tau[:, :], OP.mult))
            cx.vstep(lambda e: e.tensor_scalar(cx.t_rt[:, :], cx.t_rt[:, :], 32.0, None,
                                               OP.mult, OP.bypass))
            cx.vstep(lambda e, c=c, xc=xc: e.tensor_scalar(
                xnp[:, c * D:(c + 1) * D], xc, cx.t_mean[:, :], cx.t_rt[:, :],
                OP.subtract, OP.mult))
        d3 = cx.c["dve"]

        srcs = [(True, xnT)]
        if want_tau:
            srcs.append((False, xnpT))
        for use_txn, dstT in srcs:
            for dc in range(DCH):
                pt = cx.PTa if dc % 2 == 0 else cx.PTb
                src_ap = (cx.t_xn[:, dc * 128:(dc + 1) * 128] if use_txn else
                          xnp[:, c * D + dc * 128: c * D + dc * 128 + 128])

                def pe_t(e, pt=pt, src_ap=src_ap, d3=d3, dc=dc):
                    if dc == 0:
                        e.wait_ge(cx.s["dve"], d3)
                    if cx.c["dve"] > 1:
                        e.wait_ge(cx.s["dve"], cx.c["dve"] - 1)
                    cx.inc(e.transpose(pt[:, 0:128], src_ap, cx.ident16[:, :]), "pe")
                cx.run("tensor", pe_t)
                pn = cx.c["pe"]

                def dve_t(e, c=c, dc=dc, pt=pt, dstT=dstT, pn=pn):
                    e.wait_ge(cx.s["pe"], pn)
                    cx.inc(e.tensor_copy(dstT[:, dc * S + c * 128: dc * S + c * 128 + 128],
                                         pt[:, 0:128]), "dve")
                cx.run("vector", dve_t)

        if router:
            atr = cx.c["act"]

            def pe_r(e, c=c, atr=atr):
                e.wait_ge(cx.s["act"], atr)
                e.wait_ge(cx.s["dve"], cx.c["dve"])
                for dc in range(DCH):
                    mm = e.matmul(cx.PB[:, 0:8],
                                  xnT[:, dc * S + c * 128: dc * S + c * 128 + 128],
                                  cx.wrp[:, dc * 8:(dc + 1) * 8],
                                  start=(dc == 0), stop=(dc == DCH - 1))
                cx.inc(mm, "pe")
            cx.run("tensor", pe_r)
            pr = cx.c["pe"]

            def dve_r(e, c=c, pr=pr):
                e.wait_ge(cx.s["pe"], pr)
                e.wait_ge(cx.s["dve"], cx.c["dve"])
                cx.inc(e.tensor_scalar(cx.t_r8[:, :], cx.PB[:, 0:8], cx.r_rstd[:, c:c + 1], None,
                                       OP.mult, OP.bypass), "dve")
            cx.run("vector", dve_r)
            cx.vstep(lambda e: e.tensor_tensor(cx.t_r8[:, :], cx.t_r8[:, :], cx.cb8[:, :], OP.add))
            cx.vstep(lambda e: e.reduce_max(cx.t_neg[:, :], cx.t_r8[:, :], axis=AX.X, negate=True))
            dr = cx.c["dve"]

            def act_r(e, dr=dr):
                e.wait_ge(cx.s["dve"], dr)
                cx.inc(e.activation(cx.t_re[:, :], cx.t_r8[:, :], AF.Exp,
                                    bias=cx.t_neg[:, :], accum_out=cx.t_asum[:, :]), "act")
            cx.run("scalar", act_r)
            ar = cx.c["act"]

            def dve_r2(e, ar=ar):
                e.wait_ge(cx.s["act"], ar)
                e.wait_ge(cx.s["dve"], cx.c["dve"])
                cx.inc(e.reciprocal(cx.t_ra[:, :], cx.t_asum[:, :]), "dve")
            cx.run("vector", dve_r2)
            cx.vstep(lambda e, c=c: e.tensor_scalar(cx.p0[:, c:c + 1], cx.t_re[:, 0:1],
                                                    cx.t_ra[:, :], None, OP.mult, OP.bypass))
            cx.vstep(lambda e, c=c: e.tensor_scalar(cx.q0[:, c:c + 1], cx.p0[:, c:c + 1],
                                                    -1.0, 1.0, OP.mult, OP.add))

    return cx.snap()


def tap(cx, bar, name, tile, tap_outs, b):
    if name not in tap_outs or b != 0:
        return

    def pool_t(e):
        cx.waits(e, bar)
        cx.inc(e.dma_start(tap_outs[name][:, :], tile), "gd", 16)
        e.wait_ge(cx.s["gd"], cx.c["gd"])
    cx.run("gpsimd", pool_t)



def emit_l0_body(cx, tens, tap_outs):
    nc = cx.nc
    (src, Wq, bq, Wk, bk, Wv, bv, Wo, bo, gamma_x, gamma_slot, semb, w1, w2,
     x1, xpark_d, xnT_park, slots_park, sepT_d, mask0_d) = tens

    mask0 = cx.sb("mask0_s", [128, 128], BF16)
    bq_t = cx.sb("bq_t", [128, 8], F32)
    bk_t = cx.sb("bk_t", [128, 8], F32)
    gx_t = cx.sb("gx_t", [128, 8], F32)
    gs_t = cx.sb("gs_t", [128, 8], F32)
    bvrow = cx.sb("bvrow", [1, D], BF16)
    borow = cx.sb("borow", [1, D], BF16)
    m1_all = cx.sb("m1_all", [128, 32], F32)
    a1_all = cx.sb("a1_all", [128, 32], F32)
    lhalf = cx.sb("lhalf", [128, 2048], F32)
    sew = cx.wbuf[:, 0:1024]
    pexp = cx.dchunk[:, 1024:1152]
    pnrm = cx.dchunk[:, 1152:1280]
    ptile = cx.dchunk[:, 1280:1408]
    vchunk = cx.dchunk[:, 0:1024]
    sen_t = cx.dchunk[:, 1024:2048]

    xnT = cx.BIG2[:, 0:16384]
    xnpT = cx.BIG2[:, 16384:32768]
    xnp = cx.M3
    dthalf = cx.BIG1
    slotsT = cx.BIG2[:, 0:16384]
    hTq = cx.M3
    P_lo = cx.BIG1[:, 0:16384]
    P_hi = cx.BIG1[:, 16384:32768]
    nhalf = cx.BIG2
    obuf = cx.M3
    qT = cx.M3
    kT = cx.BIG2[:, 16384:32768]
    attnT = cx.BIG1[:, 0:16384]
    w1col = cx.wbuf[:, 0:2048]
    w2row = cx.wbuf[:, 2048:4096]

    sehalf = cx.M3
    seh1 = cx.BIG2[:, 0:16384]
    nbuf = cx.wbuf[:, 0:4096]
    ntb = cx.wbuf[:, 4096:8192]

    def P_(ej):
        return (P_lo[:, ej * D:(ej + 1) * D] if ej < 16 else
                P_hi[:, (ej - 16) * D:(ej - 15) * D])

    def sync_pre(e):
        if cx.c["cc"] > 0:
            e.wait_ge(cx.s["cc"], cx.c["cc"])
        cx.inc(e.dma_start(mask0[:, :], mask0_d[:, :]), "ld", 16)
        emit_vec_cols(cx, e, bq, bq_t, 8)
        emit_vec_cols(cx, e, bk, bk_t, 8)
        emit_vec_cols(cx, e, gamma_x, gx_t, 8)
        emit_vec_cols(cx, e, gamma_slot, gs_t, 8)
    cx.run("sync", sync_pre)

    def pool_pre(e):
        cx.inc(e.dma_start(bvrow[:, :], bv.rearrange("(one d) -> one d", one=1)), "gd", 16)
        cx.inc(e.dma_start(borow[:, :], bo.rearrange("(one d) -> one d", one=1)), "gd", 16)
    cx.run("gpsimd", pool_pre)
    lp, gp = cx.c["ld"], cx.c["gd"]

    def dve_pre(e):
        e.wait_ge(cx.s["ld"], lp)
        e.wait_ge(cx.s["gd"], gp)
        cx.inc(e.tensor_tensor(gs_t[:, :], gs_t[:, :], gx_t[:, :], OP.mult), "dve")
    cx.run("vector", dve_pre)
    bar = cx.snap()

    serows = cx.BIG1
    for ec in range(32):
        def sync_se(e, ec=ec):
            if ec == 0:
                cx.waits(e, bar, ["dve"])
            else:
                e.wait_ge(cx.s["act"], cx.c["act"])
                e.wait_ge(cx.s["dve"], cx.c["dve"])
            cx.inc(e.dma_start(cx.xch[:, 0:D], semb[ec * 128:(ec + 1) * 128, :]), "ld", 16)
        cx.run("sync", sync_se)
        lse = cx.c["ld"]

        def act_se(e, lse=lse):
            e.wait_ge(cx.s["ld"], lse)
            if cx.c["dve"] > 0:
                e.wait_ge(cx.s["dve"], cx.c["dve"])
            cx.inc(e.activation(cx.t_dump, cx.xch[:, 0:D], AF.Square,
                                accum_out=cx.t_ssq[:, :]), "act")
        cx.run("scalar", act_se)
        cx.astep(lambda e: e.activation(cx.t_sd[:, :], cx.t_ssq[:, :], AF.Ln))
        cx.astep(lambda e: e.activation(cx.t_sd[:, :], cx.t_sd[:, :], AF.Exp, scale=0.5))
        ase = cx.c["act"]

        def dve_se(e, ase=ase):
            e.wait_ge(cx.s["act"], ase)
            e.wait_ge(cx.s["dve"], cx.c["dve"])
            cx.inc(e.tensor_scalar(cx.t_sd[:, :], cx.t_sd[:, :], 1e-12, None,
                                   OP.max, OP.bypass), "dve")
        cx.run("vector", dve_se)
        cx.vstep(lambda e: e.reciprocal(cx.t_ra[:, :], cx.t_sd[:, :]))
        cx.vstep(lambda e: e.tensor_scalar(cx.t_ra[:, :], cx.t_ra[:, :], 32.0, None,
                                           OP.mult, OP.bypass))
        cx.vstep(lambda e: e.tensor_scalar(sen_t[:, :], cx.xch[:, 0:D], cx.t_ra[:, :], None,
                                           OP.mult, OP.bypass))
        dse = cx.c["dve"]

        for dc in range(DCH):
            pt = cx.PTa if dc % 2 == 0 else cx.PTb

            def pe_se(e, dc=dc, pt=pt, dse=dse):
                if dc == 0:
                    e.wait_ge(cx.s["dve"], dse)
                if dc >= 2:
                    e.wait_ge(cx.s["dve"], cx.c["dve"] - 1)
                cx.inc(e.transpose(pt[:, 0:128], sen_t[:, dc * 128:(dc + 1) * 128],
                                   cx.ident16[:, :]), "pe")
            cx.run("tensor", pe_se)
            pse = cx.c["pe"]

            def dve_se2(e, ec=ec, dc=dc, pt=pt, pse=pse):
                e.wait_ge(cx.s["pe"], pse)
                cx.inc(e.tensor_scalar(serows[:, dc * ET + ec * 128: dc * ET + ec * 128 + 128],
                                       pt[:, 0:128], gs_t[:, dc:dc + 1], None,
                                       OP.mult, OP.bypass), "dve")
            cx.run("vector", dve_se2)
    dse_all = cx.c["dve"]

    def pool_se(e):
        e.wait_ge(cx.s["dve"], dse_all)
        for dc in range(DCH):
            cx.inc(e.dma_start(sepT_d[dc * 128:(dc + 1) * 128, :],
                               serows[:, dc * ET:(dc + 1) * ET]), "gd", 16)
        e.wait_ge(cx.s["gd"], cx.c["gd"])
    cx.run("gpsimd", pool_se)
    if "sepT" in tap_outs:
        def pool_tsep(e):
            e.wait_ge(cx.s["gd"], cx.c["gd"])
            cx.inc(e.dma_start(tap_outs["sepT"][:, :], sepT_d[:, :]), "gd", 16)
            e.wait_ge(cx.s["gd"], cx.c["gd"])
        cx.run("gpsimd", pool_tsep)
    bar = cx.snap()

    for b in range(BPC):
        xhome = src[b * S:(b + 1) * S, :]
        bar = emit_ln(cx, bar, xhome, xnT, xnp, xnpT, want_tau=True)
        tap(cx, bar, "xnT", xnT[:, :], tap_outs, b)
        tap(cx, bar, "xnp", xnp[:, :], tap_outs, b)

        def pool_pk(e):
            cx.waits(e, bar)
            cx.inc(e.dma_start(cr_ap(xnT_park, S), s3(xnT[:, :], S)), "gd", 16)
        cx.run("gpsimd", pool_pk)
        bar = cx.snap()

        for half in range(2):
            for eci in range(16):
                ec = half * 16 + eci

                def sync_sew(e, ec=ec, eci=eci, half=half):
                    if eci == 0 and half == 0:
                        cx.waits(e, bar)
                    else:
                        e.wait_ge(cx.s["pe"], cx.c["pe"])
                    cx.inc(e.dma_start(s3(sew[:, :], 128), colblock_ap(sepT_d, ec * 128, 128)), "ld", 16)
                cx.run("sync", sync_sew)
                lsew = cx.c["ld"]

                def pe_d(e, lsew=lsew):
                    e.wait_ge(cx.s["ld"], lsew)
                    e.wait_ge(cx.s["act"], cx.c["act"])
                    for q4 in range(4):
                        for dc in range(DCH):
                            mm = e.matmul(cx.PA[:, q4 * 512:(q4 + 1) * 512],
                                          sew[:, dc * 128:(dc + 1) * 128],
                                          xnpT[:, dc * S + q4 * 512: dc * S + (q4 + 1) * 512],
                                          start=(dc == 0), stop=(dc == DCH - 1))
                    cx.inc(mm, "pe")
                cx.run("tensor", pe_d)
                pd_ = cx.c["pe"]

                def dve_d(e, pd_=pd_):
                    e.wait_ge(cx.s["pe"], pd_)
                    cx.inc(e.reduce_max(cx.t_neg[:, :], cx.PA[:, :], axis=AX.X, negate=True), "dve")
                cx.run("vector", dve_d)
                dd_ = cx.c["dve"]

                def act_d(e, ec=ec, dd_=dd_):
                    e.wait_ge(cx.s["dve"], dd_)
                    e.wait_ge(cx.s["gd"], cx.c["gd"])
                    cx.inc(e.activation(cx.dchunk[:, :], cx.PA[:, :], AF.Exp,
                                        bias=cx.t_neg[:, :], accum_out=a1_all[:, ec:ec + 1]), "act")
                cx.run("scalar", act_d)
                ad_ = cx.c["act"]

                def dve_d2(e, ec=ec, ad_=ad_):
                    e.wait_ge(cx.s["act"], ad_)
                    e.wait_ge(cx.s["dve"], cx.c["dve"])
                    e.tensor_scalar(m1_all[:, ec:ec + 1], cx.t_neg[:, :], -1.0, None,
                                    OP.mult, OP.bypass)
                    cx.inc(e.reciprocal(cx.t_ra[:, :], a1_all[:, ec:ec + 1]), "dve")
                cx.run("vector", dve_d2)
                cx.vstep(lambda e: e.tensor_scalar(cx.dchunk[:, :], cx.dchunk[:, :],
                                                   cx.t_ra[:, :], None, OP.mult, OP.bypass))
                dd2 = cx.c["dve"]

                for k in range(SCH):
                    pt = cx.PTa if k % 2 == 0 else cx.PTb

                    def pe_dt(e, k=k, pt=pt, dd2=dd2):
                        if k == 0:
                            e.wait_ge(cx.s["dve"], dd2)
                        if k >= 2:
                            e.wait_ge(cx.s["dve"], cx.c["dve"] - 1)
                        cx.inc(e.transpose(pt[:, 0:128], cx.dchunk[:, k * 128:(k + 1) * 128],
                                           cx.ident16[:, :]), "pe")
                    cx.run("tensor", pe_dt)
                    pdt = cx.c["pe"]

                    def dve_dt(e, k=k, eci=eci, pt=pt, pdt=pdt):
                        e.wait_ge(cx.s["pe"], pdt)
                        cx.inc(e.tensor_copy(
                            dthalf[:, k * S + eci * 128: k * S + eci * 128 + 128],
                            pt[:, 0:128]), "dve")
                    cx.run("vector", dve_dt)
            bar = cx.snap()

            for dc in range(DCH):
                def pe_u(e, dc=dc):
                    if dc == 0:
                        cx.waits(e, bar, ["dve", "act", "gd"])
                    else:
                        e.wait_ge(cx.s["dve"], cx.c["dve"])
                    for k in range(SCH):
                        for et4 in range(4):
                            mm = e.matmul(cx.PA[:, et4 * 512:(et4 + 1) * 512],
                                          xnp[:, k * D + dc * 128: k * D + dc * 128 + 128],
                                          dthalf[:, k * S + et4 * 512: k * S + (et4 + 1) * 512],
                                          start=(k == 0), stop=(k == SCH - 1))
                    cx.inc(mm, "pe")
                cx.run("tensor", pe_u)
                pu = cx.c["pe"]

                def dve_u(e, dc=dc, pu=pu):
                    e.wait_ge(cx.s["pe"], pu)
                    cx.inc(e.tensor_scalar(slotsT[:, dc * S: dc * S + S], cx.PA[:, :],
                                           gx_t[:, dc:dc + 1], None, OP.mult, OP.bypass), "dve")
                cx.run("vector", dve_u)
            du = cx.c["dve"]

            def pool_su(e, half=half, du=du):
                e.wait_ge(cx.s["dve"], du)
                cx.inc(e.dma_start(cr_ap(slots_park[half], S), s3(slotsT[:, :], S)), "gd", 16)
            cx.run("gpsimd", pool_su)
            bar = cx.snap()
            if half == 0:
                tap(cx, bar, "slotsT", slotsT[:, :], tap_outs, b)
        tap(cx, bar, "m1", m1_all[:, :], tap_outs, b)
        tap(cx, bar, "A1", a1_all[:, :], tap_outs, b)

        for half in range(2):
            def sync_sl(e, half=half):
                cx.waits(e, bar)
                e.wait_ge(cx.s["pe"], cx.c["pe"])
                cx.inc(e.dma_start(s3(slotsT[:, :], S), cr_ap(slots_park[half], S)), "ld", 16)
            cx.run("sync", sync_sl)
            lsl = cx.c["ld"]

            for qtr in range(4):
                for fc in range(FCH):
                    wslot = (fc % 2) * 1024

                    def pool_w1(e, fc=fc, qtr=qtr, half=half, wslot=wslot, lsl=lsl):
                        if fc >= 2:
                            e.wait_ge(cx.s["pe"], cx.c["pe"] - 1)
                        cx.inc(e.dma_start(s3(w1col[:, wslot:wslot + 1024], 128),
                                           colblock_ap(w1, fc * 128, 128)), "gd", 16)
                    cx.run("gpsimd", pool_w1)
                    gw1 = cx.c["gd"]
                    hps = cx.PB[:, (fc % 2) * 512:(fc % 2) * 512 + 512]

                    def pe_h(e, fc=fc, qtr=qtr, wslot=wslot, gw1=gw1, hps=hps, lsl=lsl):
                        e.wait_ge(cx.s["gd"], gw1)
                        if fc == 0:
                            e.wait_ge(cx.s["ld"], lsl)
                        if fc >= 2 and cx.c["act"] >= 1:
                            e.wait_ge(cx.s["act"], cx.c["act"] - 1)
                        for dc in range(DCH):
                            mm = e.matmul(hps,
                                          w1col[:, wslot + dc * 128: wslot + dc * 128 + 128],
                                          slotsT[:, dc * S + qtr * 512: dc * S + (qtr + 1) * 512],
                                          start=(dc == 0), stop=(dc == DCH - 1))
                        cx.inc(mm, "pe")
                    cx.run("tensor", pe_h)
                    ph = cx.c["pe"]

                    def act_h(e, fc=fc, ph=ph, hps=hps):
                        e.wait_ge(cx.s["pe"], ph)
                        cx.inc(e.activation(hTq[:, fc * 512:(fc + 1) * 512], hps, AF.Gelu), "act")
                    cx.run("scalar", act_h)
                ah = cx.c["act"]

                for pair in range(2):
                    for fc in range(FCH):
                        wslot = (fc % 2) * 1024

                        def pool_w2(e, fc=fc, wslot=wslot):
                            if fc >= 2:
                                e.wait_ge(cx.s["pe"], cx.c["pe"] - 1)
                            cx.inc(e.dma_start(w2row[:, wslot:wslot + 1024],
                                               w2[fc * 128:(fc + 1) * 128, :]), "gd", 16)
                        cx.run("gpsimd", pool_w2)
                        gw2 = cx.c["gd"]

                        def pe_p(e, fc=fc, pair=pair, gw2=gw2, wslot=wslot, ah=ah):
                            e.wait_ge(cx.s["gd"], gw2)
                            if fc == 0:
                                e.wait_ge(cx.s["act"], ah)
                                e.wait_ge(cx.s["dve"], cx.c["dve"])
                            for e1 in (pair * 2, pair * 2 + 1):
                                off = (e1 % 2) * 1024
                                for hh in range(2):
                                    mm = e.matmul(
                                        cx.PA[:, off + hh * 512: off + (hh + 1) * 512],
                                        hTq[:, fc * 512 + e1 * 128: fc * 512 + e1 * 128 + 128],
                                        w2row[:, wslot + hh * 512: wslot + (hh + 1) * 512],
                                        start=(fc == 0), stop=(fc == FCH - 1))
                            cx.inc(mm, "pe")
                        cx.run("tensor", pe_p)
                    pp = cx.c["pe"]

                    def dve_p(e, qtr=qtr, half=half, pair=pair, pp=pp):
                        e.wait_ge(cx.s["pe"], pp)
                        for e1 in (pair * 2, pair * 2 + 1):
                            off = (e1 % 2) * 1024
                            ej = half * 16 + qtr * 4 + e1
                            ii = e.tensor_copy(P_(ej), cx.PA[:, off:off + 1024])
                        cx.inc(ii, "dve")
                    cx.run("vector", dve_p)
            bar = cx.snap()

        def sync_se0(e):
            cx.waits(e, bar)
            cx.inc(e.dma_start(s3(sehalf[:, :], S), colblock_ap(sepT_d, 0, S)), "ld", 16)
        cx.run("sync", sync_se0)
        lse0 = cx.c["ld"]
        for sc in range(SCH):
            xslot = (sc % 2) * D

            def sync_seh(e, sc=sc, xslot=xslot):
                if sc == 0:
                    e.wait_ge(cx.s["ld"], lse0)
                else:
                    e.wait_ge(cx.s["pe"], cx.c["pe"])
                cx.inc(e.dma_start(s3(seh1[:, :], S), colblock_ap(sepT_d, S, S)), "ld", 16)
                if sc >= 2:
                    e.wait_ge(cx.s["gd"], cx.c["gd"])
                cx.inc(e.dma_start(cx.xch[:, xslot:xslot + D],
                                   xhome[sc * 128:(sc + 1) * 128, :]), "ld", 16)
            cx.run("sync", sync_seh)
            lseh = cx.c["ld"]

            def pe_c0(e, sc=sc):
                if sc == 0:
                    e.wait_ge(cx.s["ld"], lse0)
                e.wait_ge(cx.s["act"], cx.c["act"])
                e.wait_ge(cx.s["dve"], cx.c["dve"])
                for dc in range(DCH):
                    for q4 in range(4):
                        mm = e.matmul(cx.PA[:, q4 * 512:(q4 + 1) * 512],
                                      xnpT[:, dc * S + sc * 128: dc * S + sc * 128 + 128],
                                      sehalf[:, dc * S + q4 * 512: dc * S + (q4 + 1) * 512],
                                      start=(dc == 0), stop=(dc == DCH - 1))
                cx.inc(mm, "pe")
            cx.run("tensor", pe_c0)
            pc0 = cx.c["pe"]

            def dve_c0(e, pc0=pc0):
                e.wait_ge(cx.s["pe"], pc0)
                e.reduce_max(cx.t_neg[:, :], cx.PA[:, :], axis=AX.X, negate=True)
                cx.inc(e.tensor_copy(lhalf[:, :], cx.PA[:, :]), "dve")
            cx.run("vector", dve_c0)
            dc0 = cx.c["dve"]

            def pe_c1(e, sc=sc, dc0=dc0, lseh=lseh):
                e.wait_ge(cx.s["dve"], dc0)
                e.wait_ge(cx.s["ld"], lseh)
                for dc in range(DCH):
                    for q4 in range(4):
                        mm = e.matmul(cx.PA[:, q4 * 512:(q4 + 1) * 512],
                                      xnpT[:, dc * S + sc * 128: dc * S + sc * 128 + 128],
                                      seh1[:, dc * S + q4 * 512: dc * S + (q4 + 1) * 512],
                                      start=(dc == 0), stop=(dc == DCH - 1))
                cx.inc(mm, "pe")
            cx.run("tensor", pe_c1)
            pc1 = cx.c["pe"]

            def dve_c1(e, pc1=pc1):
                e.wait_ge(cx.s["pe"], pc1)
                cx.inc(e.reduce_max(cx.t_asum[:, :], cx.PA[:, :], axis=AX.X, negate=True), "dve")
            cx.run("vector", dve_c1)
            cx.vstep(lambda e: e.tensor_tensor(cx.t_neg[:, :], cx.t_neg[:, :],
                                               cx.t_asum[:, :], OP.min))
            dm2 = cx.c["dve"]

            def act_e0(e, dm2=dm2):
                e.wait_ge(cx.s["dve"], dm2)
                e.activation(nbuf[:, 0:S], lhalf[:, :], AF.Exp,
                             bias=cx.t_neg[:, :], accum_out=cx.t_ra[:, :])
                cx.inc(e.activation(nbuf[:, S:2 * S], cx.PA[:, :], AF.Exp,
                                    bias=cx.t_neg[:, :], accum_out=cx.t_asum[:, :]), "act")
            cx.run("scalar", act_e0)
            ae0 = cx.c["act"]

            def dve_a2(e, ae0=ae0):
                e.wait_ge(cx.s["act"], ae0)
                e.wait_ge(cx.s["dve"], cx.c["dve"])
                cx.inc(e.tensor_tensor(cx.t_ra[:, :], cx.t_ra[:, :], cx.t_asum[:, :],
                                       OP.add), "dve")
            cx.run("vector", dve_a2)
            cx.vstep(lambda e: e.reciprocal(cx.t_ra[:, :], cx.t_ra[:, :]))
            da2 = cx.c["dve"]

            for ej in range(32):
                pt = cx.PTa if ej % 2 == 0 else cx.PTb

                def pe_nt(e, ej=ej, pt=pt, da2=da2):
                    if ej == 0:
                        e.wait_ge(cx.s["dve"], da2)
                    if ej >= 2:
                        e.wait_ge(cx.s["dve"], cx.c["dve"] - 1)
                    cx.inc(e.transpose(pt[:, 0:128], nbuf[:, ej * 128:(ej + 1) * 128],
                                       cx.ident16[:, :]), "pe")
                cx.run("tensor", pe_nt)
                pnt = cx.c["pe"]

                def dve_nt(e, ej=ej, pt=pt, pnt=pnt):
                    e.wait_ge(cx.s["pe"], pnt)
                    cx.inc(e.tensor_copy(ntb[:, ej * 128:(ej + 1) * 128], pt[:, 0:128]), "dve")
                cx.run("vector", dve_nt)
            dnt = cx.c["dve"]

            def pe_out(e, dnt=dnt):
                e.wait_ge(cx.s["dve"], dnt)
                for ej in range(32):
                    for hh in range(2):
                        mm = e.matmul(cx.PB[:, hh * 512:(hh + 1) * 512],
                                      ntb[:, ej * 128:(ej + 1) * 128],
                                      P_(ej)[:, hh * 512:(hh + 1) * 512],
                                      start=(ej == 0), stop=(ej == 31))
                cx.inc(mm, "pe")
            cx.run("tensor", pe_out)
            pout = cx.c["pe"]

            def dve_out(e, sc=sc, xslot=xslot, pout=pout, lseh=lseh):
                e.wait_ge(cx.s["pe"], pout)
                e.wait_ge(cx.s["ld"], lseh)
                e.wait_ge(cx.s["dve"], cx.c["dve"])
                cx.inc(e.scalar_tensor_tensor(cx.xch[:, xslot:xslot + D], cx.PB[:, :],
                                              cx.t_ra[:, :], cx.xch[:, xslot:xslot + D],
                                              OP.mult, OP.add), "dve")
            cx.run("vector", dve_out)
            dxo = cx.c["dve"]

            def pool_xo2(e, sc=sc, xslot=xslot, dxo=dxo):
                e.wait_ge(cx.s["dve"], dxo)
                cx.inc(e.dma_start(xpark_d[sc * 128:(sc + 1) * 128, :],
                                   cx.xch[:, xslot:xslot + D]), "gd", 16)
            cx.run("gpsimd", pool_xo2)
        bar = cx.snap()

        def sync_xnt(e):
            cx.waits(e, bar)
            cx.inc(e.dma_start(s3(xnT[:, :], S), cr_ap(xnT_park, S)), "ld", 16)
        cx.run("sync", sync_xnt)
        lxnt = cx.c["ld"]

        for wsrc, bcol, dst, scaled in ((Wq, bq_t, qT, True), (Wk, bk_t, kT, False)):
            def pool_w(e, wsrc=wsrc):
                e.wait_ge(cx.s["pe"], cx.c["pe"])
                e.wait_ge(cx.s["dve"], cx.c["dve"])
                cx.inc(e.dma_start(s3(cx.wbuf[:, :], D),
                                   wsrc.rearrange("(dc p) o -> p dc o", p=128)), "gd", 16)
            cx.run("gpsimd", pool_w)
            gw = cx.c["gd"]
            for h in range(DCH):
                def pe_q(e, h=h, gw=gw, lxnt=lxnt):
                    if h == 0:
                        e.wait_ge(cx.s["gd"], gw)
                        e.wait_ge(cx.s["ld"], lxnt)
                        e.wait_ge(cx.s["dve"], cx.c["dve"])
                    else:
                        e.wait_ge(cx.s["dve"], cx.c["dve"])
                    for q4 in range(4):
                        for dc in range(DCH):
                            mm = e.matmul(cx.PA[:, q4 * 512:(q4 + 1) * 512],
                                          cx.wbuf[:, dc * D + h * 128: dc * D + h * 128 + 128],
                                          xnT[:, dc * S + q4 * 512: dc * S + (q4 + 1) * 512],
                                          start=(dc == 0), stop=(dc == DCH - 1))
                    cx.inc(mm, "pe")
                cx.run("tensor", pe_q)
                pq = cx.c["pe"]

                def dve_q(e, h=h, pq=pq, dst=dst, bcol=bcol, scaled=scaled):
                    e.wait_ge(cx.s["pe"], pq)
                    if scaled:
                        ii = e.tensor_scalar(dst[:, h * S:(h + 1) * S], cx.PA[:, :],
                                             bcol[:, h:h + 1], SC0, OP.add, OP.mult)
                    else:
                        ii = e.tensor_scalar(dst[:, h * S:(h + 1) * S], cx.PA[:, :],
                                             bcol[:, h:h + 1], None, OP.add, OP.bypass)
                    cx.inc(ii, "dve")
                cx.run("vector", dve_q)
        bar = cx.snap()
        tap(cx, bar, "qT", qT[:, :], tap_outs, b)
        tap(cx, bar, "kT", kT[:, :], tap_outs, b)

        def pool_wv(e):
            cx.waits(e, bar, ["pe"])
            cx.inc(e.dma_start(s3(cx.wbuf[:, :], D), Wv.rearrange("(dc p) o -> p dc o", p=128)), "gd", 16)
        cx.run("gpsimd", pool_wv)
        gwv = cx.c["gd"]
        it = 0
        ptbank = {0: 0, 1: 0}
        avguard = [0, 0]
        sguard = [0, 0, 0, 0]
        for np_ in range(SCH):
            def pe_v(e, np_=np_, gwv=gwv):
                if np_ == 0:
                    e.wait_ge(cx.s["gd"], gwv)
                    cx.waits(e, bar, ["dve", "act"])
                else:
                    e.wait_ge(cx.s["dve"], cx.c["dve"])
                for hh in range(2):
                    sl = cx.PB[:, hh * 512:(hh + 1) * 512]
                    e.matmul(sl, cx.ones16[:, :], bvrow[:, hh * 512:(hh + 1) * 512],
                             start=True, stop=False)
                    for dc in range(DCH):
                        mm = e.matmul(sl, xnT[:, dc * S + np_ * 128: dc * S + np_ * 128 + 128],
                                      cx.wbuf[:, dc * D + hh * 512: dc * D + (hh + 1) * 512],
                                      start=False, stop=(dc == DCH - 1))
                cx.inc(mm, "pe")
            cx.run("tensor", pe_v)
            pv_ = cx.c["pe"]

            def dve_v(e, pv_=pv_):
                e.wait_ge(cx.s["pe"], pv_)
                cx.inc(e.tensor_copy(vchunk[:, :], cx.PB[:, :]), "dve")
            cx.run("vector", dve_v)

            for h in range(DCH):
                sps = cx.PA[:, (it % 4) * 512:(it % 4) * 512 + 128]

                def pe_s(e, h=h, np_=np_, sps=sps, it=it, sg=sguard[0]):
                    if sg:
                        e.wait_ge(cx.s["act"], sg)
                    e.matmul(sps, cx.ident16[:, :], mask0[:, :], start=True, stop=False)
                    cx.inc(e.matmul(sps, qT[:, h * S + np_ * 128: h * S + np_ * 128 + 128],
                                    kT[:, h * S + np_ * 128: h * S + np_ * 128 + 128],
                                    start=False, stop=True), "pe")
                cx.run("tensor", pe_s)
                ps_ = cx.c["pe"]

                def dve_s(e, ps_=ps_, sps=sps, pv=cx.c["act"]):
                    e.wait_ge(cx.s["pe"], ps_)
                    if pv:
                        e.wait_ge(cx.s["act"], pv)
                    cx.inc(e.reduce_max(cx.t_neg[:, :], sps, axis=AX.X, negate=True), "dve")
                cx.run("vector", dve_s)
                ds_ = cx.c["dve"]

                def act_s(e, ds_=ds_, sps=sps):
                    e.wait_ge(cx.s["dve"], ds_)
                    cx.inc(e.activation(pexp[:, :], sps, AF.Exp,
                                        bias=cx.t_neg[:, :], accum_out=cx.t_asum[:, :]), "act")
                cx.run("scalar", act_s)
                as_ = cx.c["act"]
                sguard.pop(0)
                sguard.append(as_)

                def dve_s2(e, as_=as_):
                    e.wait_ge(cx.s["act"], as_)
                    e.wait_ge(cx.s["dve"], cx.c["dve"])
                    cx.inc(e.reciprocal(cx.t_ra[:, :], cx.t_asum[:, :]), "dve")
                cx.run("vector", dve_s2)
                cx.vstep(lambda e, pv=cx.c["pe"]: (e.wait_ge(cx.s["pe"], pv),
                         e.tensor_scalar(pnrm[:, :], pexp[:, :], cx.t_ra[:, :], None,
                                         OP.mult, OP.bypass))[1])
                ds2 = cx.c["dve"]
                pt = cx.PTa if it % 2 == 0 else cx.PTb

                def pe_t(e, ds2=ds2, pt=pt, ptg=ptbank[it % 2]):
                    e.wait_ge(cx.s["dve"], ds2)
                    if ptg:
                        e.wait_ge(cx.s["dve"], ptg)
                    cx.inc(e.transpose(pt[:, 0:128], pnrm[:, :], cx.ident16[:, :]), "pe")
                cx.run("tensor", pe_t)
                pt_ = cx.c["pe"]

                def dve_t(e, pt_=pt_, pt=pt):
                    e.wait_ge(cx.s["pe"], pt_)
                    cx.inc(e.tensor_copy(ptile[:, :], pt[:, 0:128]), "dve")
                cx.run("vector", dve_t)
                ptbank[it % 2] = cx.c["dve"]
                at_ = cx.c["dve"]
                pvs = cx.PB[:, (it % 2) * 512:(it % 2) * 512 + 128]

                def pe_pv(e, h=h, at_=at_, pvs=pvs, avg=avguard[0]):
                    e.wait_ge(cx.s["dve"], at_)
                    if avg:
                        e.wait_ge(cx.s["dve"], avg)
                    cx.inc(e.matmul(pvs, vchunk[:, h * 128:(h + 1) * 128],
                                    ptile[:, :], start=True, stop=True), "pe")
                cx.run("tensor", pe_pv)
                ppv = cx.c["pe"]

                def dve_pv(e, h=h, np_=np_, ppv=ppv, pvs=pvs):
                    e.wait_ge(cx.s["pe"], ppv)
                    cx.inc(e.tensor_copy(attnT[:, h * S + np_ * 128: h * S + np_ * 128 + 128],
                                         pvs), "dve")
                cx.run("vector", dve_pv)
                avguard[0] = avguard[1]
                avguard[1] = cx.c["dve"]
                it += 1
        bar = cx.snap()
        tap(cx, bar, "attnT", attnT[:, :], tap_outs, b)

        def pool_wo(e):
            cx.waits(e, bar, ["pe"])
            cx.inc(e.dma_start(s3(cx.wbuf[:, :], D), Wo.rearrange("(dc p) o -> p dc o", p=128)), "gd", 16)
        cx.run("gpsimd", pool_wo)
        gwo = cx.c["gd"]
        for c in range(SCH):
            slot = (c % 2) * D

            def sync_xo(e, c=c, slot=slot):
                if c == 0:
                    cx.waits(e, bar, ["dve", "act"])
                if c >= 2:
                    e.wait_ge(cx.s["gd"], cx.c["gd"])
                cx.inc(e.dma_start(cx.xch[:, slot:slot + D], xpark_d[c * 128:(c + 1) * 128, :]),
                       "ld", 16)
            cx.run("sync", sync_xo)
            lxo = cx.c["ld"]

            def pe_o(e, c=c, gwo=gwo):
                if c == 0:
                    e.wait_ge(cx.s["gd"], gwo)
                    cx.waits(e, bar, ["dve", "act"])
                else:
                    e.wait_ge(cx.s["dve"], cx.c["dve"])
                for hh in range(2):
                    sl = cx.PB[:, hh * 512:(hh + 1) * 512]
                    e.matmul(sl, cx.ones16[:, :], borow[:, hh * 512:(hh + 1) * 512],
                             start=True, stop=False)
                    for dc in range(DCH):
                        mm = e.matmul(sl, attnT[:, dc * S + c * 128: dc * S + c * 128 + 128],
                                      cx.wbuf[:, dc * D + hh * 512: dc * D + (hh + 1) * 512],
                                      start=False, stop=(dc == DCH - 1))
                cx.inc(mm, "pe")
            cx.run("tensor", pe_o)
            po = cx.c["pe"]

            def dve_o(e, c=c, slot=slot, po=po, lxo=lxo):
                e.wait_ge(cx.s["pe"], po)
                e.wait_ge(cx.s["ld"], lxo)
                cx.inc(e.tensor_tensor(cx.xch[:, slot:slot + D], cx.xch[:, slot:slot + D],
                                       cx.PB[:, :], OP.add), "dve")
            cx.run("vector", dve_o)
            do_ = cx.c["dve"]

            def pool_xo(e, c=c, slot=slot, do_=do_, b=b):
                e.wait_ge(cx.s["dve"], do_)
                cx.inc(e.dma_start(x1[b * S + c * 128: b * S + (c + 1) * 128, :],
                                   cx.xch[:, slot:slot + D]), "gd", 16)
            cx.run("gpsimd", pool_xo)
        bar = cx.snap()

    def pool_end(e):
        e.wait_ge(cx.s["gd"], cx.c["gd"])
        e.wait_ge(cx.s["ld"], cx.c["ld"])
    cx.run("gpsimd", pool_end)


TAP_SHAPES_L0 = {
    "xnT": [128, DCH * S], "xnp": [128, SCH * D], "attnT": [128, DCH * S],
    "qT": [128, DCH * S], "kT": [128, DCH * S], "slotsT": [128, DCH * S],
    "m1": [128, 32], "A1": [128, 32], "P_": [128, 32768],
    "G": [128, 32], "ra2": [1, S], "sepT": [D, ET],
}


def build_l0(taps=()):
    nc = bass.Bass("TRN2", target_bir_lowering=False)
    src = nc.dram_tensor("src", [BPC * S, D], F32, kind="ExternalInput")
    names = dict(Wq=[D, D], bq=[D], Wk=[D, D], bk=[D], Wv=[D, D], bv=[D],
                 Wo=[D, D], bo=[D], gamma_x=[D], gamma_slot=[D],
                 semb=[ET, D], w1=[D, FF], w2=[FF, D])
    tt = {k: nc.dram_tensor(k, v, F32, kind="ExternalInput") for k, v in names.items()}
    ident16_d = nc.dram_tensor("ident16", [128, 128], BF16, kind="ExternalInput")
    identf_d = nc.dram_tensor("identf", [128, 128], F32, kind="ExternalInput")
    mask0_d = nc.dram_tensor("mask0", [128, 128], BF16, kind="ExternalInput")
    x1 = nc.dram_tensor("x1", [BPC * S, D], F32, kind="ExternalOutput")
    xpark_d = nc.dram_tensor("xpark", [S, D], F32)
    xnT_park = nc.dram_tensor("xnT_park", [D, S], BF16)
    slots_park = [nc.dram_tensor(f"slots_park{i}", [D, S], BF16) for i in range(2)]
    sepT_d = nc.dram_tensor("sepT", [D, ET], BF16)
    tap_outs = {t: nc.dram_tensor("tap_" + t, TAP_SHAPES_L0[t], F32, kind="ExternalOutput")
                for t in taps}
    with ExitStack() as st:
        block = st.enter_context(nc.Block())
        cx = Cx(nc, block, st)
        alloc_common(cx)
        emit_load_consts(cx, ident16_d, identf_d)
        emit_l0_body(cx, (src, tt["Wq"], tt["bq"], tt["Wk"], tt["bk"], tt["Wv"],
                          tt["bv"], tt["Wo"], tt["bo"], tt["gamma_x"],
                          tt["gamma_slot"], tt["semb"], tt["w1"], tt["w2"],
                          x1, xpark_d, xnT_park, slots_park, sepT_d,
                          mask0_d), tap_outs)
    return nc



def emit_l1_body(cx, tens, tap_outs, thr8_tile=None):
    nc = cx.nc
    (xin, Wq, Wk, Wv, Wf, bf, W1, b1, W2, b2, ln_g, ln_b, Wr, br, thr8_d,
     y_out, xpark_d, btmp_d, w1p_d, w2p_d, mask1_d) = tens

    mask1 = cx.sb("mask1_s", [128, 3 * 384], BF16)
    bfrow = cx.sb("bfrow", [1, D], BF16)
    b1t = cx.sb("b1t", [128, 32], F32)
    b1pt = cx.sb("b1pt", [128, 32], F32)
    b2row = cx.sb("b2row", [1, D], BF16)
    b2prow = cx.sb("b2prow", [1, D], BF16)
    babsr = cx.xch[0:1, 0:D]
    bc1r = cx.xch[0:1, D:2 * D]
    lng_t = cx.sb("lng_t", [128, 8], F32)
    lnb_t = cx.sb("lnb_t", [128, 8], F32)
    cx.wrp = cx.sb("wrp", [128, 64], BF16)
    wrf = cx.sb("wrf", [128, 64], F32)
    wrg = cx.sb("wrg", [128, 64], F32)
    cx.gpb = cx.sb("gpb", [128, 8], F32)
    cx.cb8 = cx.sb("cb8", [128, 8], F32)
    cx.r_mean = cx.sb("r_mean", [128, 16], F32)
    cx.r_rstd = cx.sb("r_rstd", [128, 16], F32)
    cx.p0 = cx.sb("p0", [128, 16], F32)
    cx.q0 = cx.sb("q0", [128, 16], F32)
    cx.t_rm = cx.sb("t_rm", [128, 1], F32)
    cx.t_r8 = cx.sb("t_r8", [128, 8], F32)
    cx.t_re = cx.sb("t_re", [128, 8], BF16)
    thr8s = thr8_tile if thr8_tile is not None else cx.sb("thr8s", [1, 8], F32)
    thrb = cx.sb("thrb", [128, 8], F32)
    gprow = cx.sb("gprow", [1, 8], F32)
    c8row = cx.sb("c8row", [1, 8], F32)
    brrow = cx.sb("brrow", [1, 8], F32)
    onescol = cx.sb("onescol", [128, 1], F32)
    babs = cx.sb("babs", [128, 32], F32)
    bc1 = cx.sb("bc1", [128, 32], F32)
    bc2 = cx.sb("bc2", [128, 32], F32)
    pexp1 = cx.dchunk[:, 0:384]
    pnrm1 = cx.dchunk[:, 384:768]
    ptile1 = cx.dchunk[:, 768:1152]
    kdup = cx.BIG1[:, 18432:20736]
    v2 = cx.BIG1[:, 20736:21888]
    ahead = cx.BIG1[:, 21888:23936]
    gu = cx.dchunk[:, 0:512]
    gt1 = cx.dchunk[:, 512:1024]
    gt2 = cx.dchunk[:, 1024:1536]
    wkvT = cx.sb("wkvT", [128, 2 * 512], BF16)
    wrowf = cx.xch[:, 0:D]
    wabs = cx.xch[:, D:2 * D]
    wc1 = cx.sb("wc1", [128, D], F32)
    bc2r = wc1[0:1, :]
    wp16 = cx.dchunk[:, 0:D]

    xnT = cx.BIG2[:, 0:16384]
    qT2 = cx.M3
    attnT2 = cx.BIG2[:, 16384:32768]
    actT = cx.BIG2[:, 16384:32768]
    w2res = cx.BIG1
    wT = cx.BIG1[:, 0:16384]

    def sync_pre(e):
        if cx.c["cc"] > 0:
            e.wait_ge(cx.s["cc"], cx.c["cc"])
        cx.inc(e.dma_start(mask1[:, :], mask1_d[:, :]), "ld", 16)
        emit_vec_cols(cx, e, b1, b1t, 32)
        emit_vec_cols(cx, e, ln_g, lng_t, 8)
        emit_vec_cols(cx, e, ln_b, lnb_t, 8)
        for c in range(8):
            cx.inc(e.dma_start(wrf[:, c * 8:(c + 1) * 8], Wr[c * 128:(c + 1) * 128, :]), "ld", 16)
        if thr8_d is not None:
            cx.inc(e.dma_start(thr8s[:, :], thr8_d[:, :]), "ld", 16)
        cx.inc(e.dma_start(brrow[:, :], br.rearrange("(one d) -> one d", one=1)), "ld", 16)
    cx.run("sync", sync_pre)

    def pool_pre(e):
        if cx.c["cc"] > 0:
            e.wait_ge(cx.s["cc"], cx.c["cc"])
        cx.inc(e.dma_start(bfrow[:, :], bf.rearrange("(one d) -> one d", one=1)), "gd", 16)
        cx.inc(e.dma_start(b2row[:, :], b2.rearrange("(one d) -> one d", one=1)), "gd", 16)
    cx.run("gpsimd", pool_pre)
    lp, gp = cx.c["ld"], cx.c["gd"]

    def pe_thr(e):
        e.wait_ge(cx.s["ld"], lp)
        for j in range(8):
            mm = e.matmul(cx.PB[:, j:j + 1], cx.onesf[:, :], thr8s[:, j:j + 1],
                          start=True, stop=True)
        cx.inc(mm, "pe")
    cx.run("tensor", pe_thr)
    pthr = cx.c["pe"]

    def act_abs(e):
        e.wait_ge(cx.s["ld"], lp)
        e.wait_ge(cx.s["gd"], gp)
        e.activation(babs[:, :], b1t[:, :], AF.Abs)
        cx.inc(e.activation(babsr, b2row[:, :], AF.Abs), "act")
    cx.run("scalar", act_abs)
    aabs = cx.c["act"]

    def dve_pre(e):
        e.wait_ge(cx.s["pe"], pthr)
        e.wait_ge(cx.s["act"], aabs)
        if cx.c["dve"] > 0:
            e.wait_ge(cx.s["dve"], cx.c["dve"])
        e.memset(onescol[:, :], 1.0)
        e.tensor_copy(thrb[:, :], cx.PB[:, 0:8])
        for c in range(8):
            e.tensor_scalar(wrg[:, c * 8:(c + 1) * 8], wrf[:, c * 8:(c + 1) * 8],
                            lng_t[:, c:c + 1], None, OP.mult, OP.bypass)
            ii = e.tensor_scalar(wrf[:, c * 8:(c + 1) * 8], wrf[:, c * 8:(c + 1) * 8],
                                 lnb_t[:, c:c + 1], None, OP.mult, OP.bypass)
        cx.inc(ii, "dve")
    cx.run("vector", dve_pre)
    cx.vstep(lambda e: e.tensor_copy(cx.wrp[:, :], wrg[:, :]))

    def bt1(e):
        e.tensor_scalar(bc1[:, :], babs[:, :], thrb[:, 5:6], -0.8, OP.is_ge, OP.mult)
        e.tensor_scalar(bc2[:, :], babs[:, :], thrb[:, 4:5], 0.8, OP.is_le, OP.mult)
        e.tensor_scalar(bc1r, babsr, thr8s[0:1, 7:8], -0.8, OP.is_ge, OP.mult)
        return e.tensor_scalar(bc2r[:, :], babsr, thr8s[0:1, 6:7], 0.8, OP.is_le, OP.mult)
    cx.vstep(bt1)

    def bt2(e):
        e.tensor_tensor(bc1[:, :], bc1[:, :], bc2[:, :], OP.add)
        return e.tensor_tensor(bc1r, bc1r, bc2r[:, :], OP.add)
    cx.vstep(bt2)

    def bt3(e):
        e.tensor_scalar(bc1[:, :], bc1[:, :], 1.0, None, OP.add, OP.bypass)
        return e.tensor_scalar(bc1r, bc1r, 1.0, None, OP.add, OP.bypass)
    cx.vstep(bt3)

    def bt4(e):
        e.tensor_tensor(b1pt[:, :], b1t[:, :], bc1[:, :], OP.mult)
        return e.tensor_tensor(b2prow[:, :], b2row[:, :], bc1r, OP.mult)
    cx.vstep(bt4)
    dpre = cx.c["dve"]

    def pe_gs(e):
        e.wait_ge(cx.s["dve"], dpre)
        for c in range(8):
            e.matmul(cx.PB[0:1, 16:24], onescol[:, :], wrg[:, c * 8:(c + 1) * 8],
                     start=(c == 0), stop=(c == 7))
        for c in range(8):
            mm = e.matmul(cx.PB[0:1, 24:32], onescol[:, :], wrf[:, c * 8:(c + 1) * 8],
                          start=(c == 0), stop=(c == 7))
        cx.inc(mm, "pe")
    cx.run("tensor", pe_gs)
    pgs = cx.c["pe"]

    def dve_gs(e):
        e.wait_ge(cx.s["pe"], pgs)
        e.wait_ge(cx.s["dve"], cx.c["dve"])
        e.tensor_copy(gprow[:, :], cx.PB[0:1, 16:24])
        cx.inc(e.tensor_copy(c8row[:, :], cx.PB[0:1, 24:32]), "dve")
    cx.run("vector", dve_gs)
    cx.vstep(lambda e: e.tensor_tensor(c8row[:, :], c8row[:, :], brrow[:, :], OP.add))
    dgs = cx.c["dve"]

    def pe_gs2(e):
        e.wait_ge(cx.s["dve"], dgs)
        e.matmul(cx.PB[:, 16:24], cx.onesf[:, :], gprow[:, :], start=True, stop=True)
        cx.inc(e.matmul(cx.PB[:, 24:32], cx.onesf[:, :], c8row[:, :], start=True, stop=True), "pe")
    cx.run("tensor", pe_gs2)
    pgs2 = cx.c["pe"]

    def dve_gs2(e):
        e.wait_ge(cx.s["pe"], pgs2)
        e.tensor_copy(cx.gpb[:, :], cx.PB[:, 16:24])
        cx.inc(e.tensor_copy(cx.cb8[:, :], cx.PB[:, 24:32]), "dve")
    cx.run("vector", dve_gs2)
    bar = cx.snap()

    for (wsrc, wdst, nrows, width, jb, jt) in ((W1, w1p_d, DCH, FF, 0, 1),
                                               (W2, w2p_d, FCH, D, 2, 3)):
        for rc in range(nrows):
            for cs in range(width // D):
                def sync_wp(e, wsrc=wsrc, rc=rc, cs=cs):
                    if cx.c["gd"] > 0:
                        e.wait_ge(cx.s["gd"], cx.c["gd"])
                    cx.inc(e.dma_start(wrowf,
                                       wsrc[rc * 128:(rc + 1) * 128, cs * D:(cs + 1) * D]),
                           "ld", 16)
                cx.run("sync", sync_wp)
                lwp = cx.c["ld"]

                def act_wp(e, lwp=lwp):
                    e.wait_ge(cx.s["ld"], lwp)
                    if cx.c["dve"] > 0:
                        e.wait_ge(cx.s["dve"], cx.c["dve"])
                    cx.inc(e.activation(wabs, wrowf, AF.Abs), "act")
                cx.run("scalar", act_wp)
                awp = cx.c["act"]

                def dve_wp(e, awp=awp, jb=jb, jt=jt):
                    e.wait_ge(cx.s["act"], awp)
                    e.wait_ge(cx.s["dve"], cx.c["dve"])
                    e.tensor_scalar(wc1[:, :], wabs, thrb[:, jt:jt + 1], -0.8,
                                    OP.is_ge, OP.mult)
                    cx.inc(e.tensor_scalar(wabs, wabs, thrb[:, jb:jb + 1], 0.8,
                                           OP.is_le, OP.mult), "dve")
                cx.run("vector", dve_wp)
                cx.vstep(lambda e: e.tensor_tensor(wc1[:, :], wc1[:, :], wabs, OP.add))
                cx.vstep(lambda e: e.tensor_scalar(wc1[:, :], wc1[:, :], 1.0, None,
                                                   OP.add, OP.bypass))
                cx.vstep(lambda e: e.tensor_tensor(wp16, wrowf, wc1[:, :], OP.mult))
                dwp = cx.c["dve"]

                def pool_wp(e, wdst=wdst, rc=rc, cs=cs, dwp=dwp):
                    e.wait_ge(cx.s["dve"], dwp)
                    cx.inc(e.dma_start(wdst[rc * 128:(rc + 1) * 128, cs * D:(cs + 1) * D],
                                       wp16), "gd", 16)
                cx.run("gpsimd", pool_wp)
    bar = cx.snap()

    for b in range(BPC):
        xhome = xin[b * S:(b + 1) * S, :]
        yhome = y_out[b * S:(b + 1) * S, :]
        bar = emit_ln(cx, bar, xhome, xnT, None, None, want_tau=False, router=True)
        tap(cx, bar, "xnT2", xnT[:, :], tap_outs, b)
        tap(cx, bar, "p0", cx.p0[:, :], tap_outs, b)

        import os as _os
        for ex in ([] if _os.environ.get("L1_SKIP_MOE") else range(2)):
            w1src = W1 if ex == 0 else w1p_d
            w2src = W2 if ex == 0 else w2p_d
            bsrc = b1t if ex == 0 else b1pt
            b2r = b2row if ex == 0 else b2prow
            pcol = cx.p0 if ex == 0 else cx.q0
            xsrc_home = xhome if ex == 0 else xpark_d
            xdst_home = xpark_d

            def pool_w2r(e, w2src=w2src, ex=ex):
                e.wait_ge(cx.s["pe"], cx.c["pe"])
                if ex == 0:
                    cx.waits(e, bar)
                else:
                    e.wait_ge(cx.s["gd"], cx.c["gd"])
                cx.inc(e.dma_start(s3(w2res[:, :], D),
                                   w2src.rearrange("(fc p) d -> p fc d", p=128)), "gd", 16)
            cx.run("gpsimd", pool_w2r)
            gw2r = cx.c["gd"]

            for sq in range(4):
                for fc in range(FCH):
                    wslot = (fc % 2) * 1024

                    def pool_w1e(e, fc=fc, wslot=wslot, w1src=w1src):
                        if fc >= 2:
                            e.wait_ge(cx.s["pe"], cx.c["pe"] - 1)
                        cx.inc(e.dma_start(s3(cx.wbuf[:, wslot:wslot + 1024], 128),
                                           colblock_ap(w1src, fc * 128, 128)), "gd", 16)
                    cx.run("gpsimd", pool_w1e)
                    gw1e = cx.c["gd"]
                    pps = cx.PA[:, (fc % 4) * 512:(fc % 4) * 512 + 512]

                    def pe_pre(e, fc=fc, sq=sq, gw1e=gw1e, pps=pps, wslot=wslot, gw2r=gw2r):
                        e.wait_ge(cx.s["gd"], gw1e)
                        if fc == 0:
                            e.wait_ge(cx.s["gd"], gw2r)
                            e.wait_ge(cx.s["dve"], cx.c["dve"])
                        elif fc >= 4:
                            e.wait_ge(cx.s["dve"], cx.c["dve"] - 4 * 3 + 2)
                        for dc in range(DCH):
                            mm = e.matmul(pps,
                                          cx.wbuf[:, wslot + dc * 128: wslot + dc * 128 + 128],
                                          xnT[:, dc * S + sq * 512: dc * S + (sq + 1) * 512],
                                          start=(dc == 0), stop=(dc == DCH - 1))
                        cx.inc(mm, "pe")
                    cx.run("tensor", pe_pre)
                    ppre = cx.c["pe"]

                    def dve_u(e, fc=fc, ppre=ppre, pps=pps, bsrc=bsrc):
                        e.wait_ge(cx.s["pe"], ppre)
                        cx.inc(e.tensor_scalar(gu, pps, bsrc[:, fc:fc + 1], None,
                                               OP.add, OP.bypass), "dve")
                    cx.run("vector", dve_u)
                    du_ = cx.c["dve"]

                    cx.vstep(lambda e: e.tensor_tensor(gt1, gu, gu, OP.mult))
                    cx.vstep(lambda e: e.tensor_scalar(gt2, gt1, GC2, GC1, OP.mult, OP.add))
                    cx.vstep(lambda e: e.tensor_tensor(gt2, gt2, gu, OP.mult))
                    dz = cx.c["dve"]

                    def act_sg(e, dz=dz):
                        e.wait_ge(cx.s["dve"], dz)
                        cx.inc(e.activation(gt1, gt2, AF.Sigmoid), "act")
                    cx.run("scalar", act_sg)
                    asg = cx.c["act"]

                    def dve_a(e, fc=fc, asg=asg):
                        e.wait_ge(cx.s["act"], asg)
                        cx.inc(e.tensor_tensor(actT[:, fc * 512:(fc + 1) * 512],
                                               gt1, gu, OP.mult), "dve")
                    cx.run("vector", dve_a)
                dact = cx.c["dve"]

                for ci in range(4):
                    sc = sq * 4 + ci
                    slot = (ci % 2) * D

                    def sync_xm(e, sc=sc, slot=slot, xsrc_home=xsrc_home):
                        if cx.c["gd"] > 0:
                            e.wait_ge(cx.s["gd"], cx.c["gd"])
                        cx.inc(e.dma_start(cx.xch[:, slot:slot + D],
                                           xsrc_home[sc * 128:(sc + 1) * 128, :]), "ld", 16)
                    cx.run("sync", sync_xm)
                    lxm = cx.c["ld"]

                    def pe_y(e, ci=ci, dact=dact, b2r=b2r):
                        if ci == 0:
                            e.wait_ge(cx.s["dve"], dact)
                        else:
                            e.wait_ge(cx.s["dve"], cx.c["dve"])
                        for hh in range(2):
                            sl = cx.PB[:, hh * 512:(hh + 1) * 512]
                            e.matmul(sl, cx.ones16[:, :], b2r[:, hh * 512:(hh + 1) * 512],
                                     start=True, stop=False)
                            for fc in range(FCH):
                                mm = e.matmul(sl,
                                              actT[:, fc * 512 + ci * 128: fc * 512 + ci * 128 + 128],
                                              w2res[:, fc * D + hh * 512: fc * D + (hh + 1) * 512],
                                              start=False, stop=(fc == FCH - 1))
                        cx.inc(mm, "pe")
                    cx.run("tensor", pe_y)
                    py = cx.c["pe"]

                    def dve_y(e, sc=sc, slot=slot, py=py, lxm=lxm, pcol=pcol):
                        e.wait_ge(cx.s["pe"], py)
                        e.wait_ge(cx.s["ld"], lxm)
                        cx.inc(e.scalar_tensor_tensor(cx.xch[:, slot:slot + D], cx.PB[:, :],
                                                      pcol[:, sc:sc + 1],
                                                      cx.xch[:, slot:slot + D],
                                                      OP.mult, OP.add), "dve")
                    cx.run("vector", dve_y)
                    dy = cx.c["dve"]

                    def pool_xm(e, sc=sc, slot=slot, dy=dy, xdst_home=xdst_home):
                        e.wait_ge(cx.s["dve"], dy)
                        cx.inc(e.dma_start(xdst_home[sc * 128:(sc + 1) * 128, :],
                                           cx.xch[:, slot:slot + D]), "gd", 16)
                    cx.run("gpsimd", pool_xm)
            bar = cx.snap()
        tap(cx, bar, "x_after_moe", None, tap_outs, b) if False else None

        import os as _os
        _skip_qprep = bool(_os.environ.get("L1_SKIP_QPREP"))
        if _os.environ.get("L1_SKIP_ATTN"):
            def pool_skip(e, b=b, yhome=yhome):
                cx.waits(e, bar)
                cx.inc(e.dma_start(yhome[:, :], xpark_d[:, :]), "gd", 16)
            cx.run("gpsimd", pool_skip)
            bar = cx.snap()
            continue
        def pool_wq(e):
            if _skip_qprep:
                cx.waits(e, bar, ["pe"])
                return
            cx.waits(e, bar, ["pe"])
            cx.inc(e.dma_start(s3(cx.wbuf[:, :], D), Wq.rearrange("(oc p) i -> p oc i", p=128)),
                   "gd", 16)
        cx.run("gpsimd", pool_wq)
        gwq = cx.c["gd"]
        for oc in ([] if _skip_qprep else range(DCH)):
            for ic in range(DCH):
                pt = cx.PTa if (oc * DCH + ic) % 2 == 0 else cx.PTb

                def pe_wt(e, oc=oc, ic=ic, pt=pt, gwq=gwq):
                    if oc == 0 and ic == 0:
                        e.wait_ge(cx.s["gd"], gwq)
                        cx.waits(e, bar, ["act", "dve"])
                    if cx.c["dve"] > 1:
                        e.wait_ge(cx.s["dve"], cx.c["dve"] - 1)
                    cx.inc(e.transpose(pt[:, 0:128],
                                       cx.wbuf[:, oc * D + ic * 128: oc * D + ic * 128 + 128],
                                       cx.ident16[:, :]), "pe")
                cx.run("tensor", pe_wt)
                pwt = cx.c["pe"]

                def dve_wt(e, oc=oc, ic=ic, pt=pt, pwt=pwt):
                    e.wait_ge(cx.s["pe"], pwt)
                    cx.inc(e.tensor_copy(wT[:, ic * D + oc * 128: ic * D + oc * 128 + 128],
                                         pt[:, 0:128]), "dve")
                cx.run("vector", dve_wt)
        bar2 = cx.snap()

        for h in ([] if _skip_qprep else range(16)):
            def pe_qh(e, h=h, bar2=bar2):
                if h == 0:
                    cx.waits(e, bar2, ["act", "dve", "gd"])
                else:
                    e.wait_ge(cx.s["dve"], cx.c["dve"])
                    e.wait_ge(cx.s["gd"], cx.c["gd"])
                for q4 in range(4):
                    for dc in range(DCH):
                        mm = e.matmul(cx.PA[0:64, q4 * 512:(q4 + 1) * 512],
                                      wT[:, dc * D + h * 64: dc * D + h * 64 + 64],
                                      xnT[:, dc * S + q4 * 512: dc * S + (q4 + 1) * 512],
                                      start=(dc == 0), stop=(dc == DCH - 1))
                cx.inc(mm, "pe")
            cx.run("tensor", pe_qh)
            pqh = cx.c["pe"]

            if h % 2 == 0:
                def dve_qh(e, h=h, pqh=pqh):
                    e.wait_ge(cx.s["pe"], pqh)
                    cx.inc(e.tensor_scalar(qT2[:, (h // 2) * S:(h // 2 + 1) * S][0:64, :],
                                           cx.PA[0:64, :], SC1, None, OP.mult, OP.bypass), "dve")
                cx.run("vector", dve_qh)
            else:
                def dve_qh(e, h=h, pqh=pqh):
                    e.wait_ge(cx.s["pe"], pqh)
                    cx.inc(e.tensor_scalar(ahead[0:64, :], cx.PA[0:64, :], SC1, None,
                                           OP.mult, OP.bypass), "dve")
                cx.run("vector", dve_qh)
                dqh = cx.c["dve"]

                def pool_qh(e, h=h, dqh=dqh):
                    e.wait_ge(cx.s["dve"], dqh)
                    cx.inc(e.dma_start(qT2[:, (h // 2) * S:(h // 2 + 1) * S][64:128, :],
                                       ahead[0:64, :]), "gd", 16)
                cx.run("gpsimd", pool_qh)
        bar = cx.snap()

        _skip_kv = bool(_os.environ.get("L1_SKIP_KV"))
        def pool_wkv(e):
            if _skip_kv:
                cx.waits(e, bar, ["pe"])
                return
            cx.waits(e, bar, ["pe"])
            cx.inc(e.dma_start(cx.wbuf[0:64, 0:1024], Wk[0:64, :]), "gd", 16)
            cx.inc(e.dma_start(cx.wbuf[0:64, 1024:2048], Wv[0:64, :]), "gd", 16)
        cx.run("gpsimd", pool_wkv)
        gwkv = cx.c["gd"]
        for j, base in ([] if _skip_kv else ((0, 0), (1, 1024))):
            for ic in range(DCH):
                pt = cx.PTa if ic % 2 == 0 else cx.PTb

                def pe_wk(e, j=j, base=base, ic=ic, pt=pt, gwkv=gwkv):
                    if j == 0 and ic == 0:
                        e.wait_ge(cx.s["gd"], gwkv)
                        e.wait_ge(cx.s["dve"], cx.c["dve"])
                    if cx.c["dve"] > 1:
                        e.wait_ge(cx.s["dve"], cx.c["dve"] - 1)
                    cx.inc(e.transpose(pt[:, 0:64],
                                       cx.wbuf[0:64, base + ic * 128: base + ic * 128 + 128],
                                       cx.ident16[0:64, 0:64]), "pe")
                cx.run("tensor", pe_wk)
                pwk = cx.c["pe"]

                def dve_wk(e, j=j, ic=ic, pt=pt, pwk=pwk):
                    e.wait_ge(cx.s["pe"], pwk)
                    cx.inc(e.tensor_copy(wkvT[:, j * 512 + ic * 64: j * 512 + (ic + 1) * 64],
                                         pt[:, 0:64]), "dve")
                cx.run("vector", dve_wk)
        awk = cx.c["dve"]

        def dve_kz(e, awk=awk):
            if _skip_kv:
                return
            e.wait_ge(cx.s["dve"], awk)
            e.memset(kdup[:, 0:256], 0.0)
            cx.inc(e.memset(v2[:, 0:128], 0.0), "dve")
        cx.run("vector", dve_kz)

        for q4 in ([] if _skip_kv else range(4)):
            def pe_kp(e, q4=q4):
                if q4 > 0:
                    e.wait_ge(cx.s["dve"], cx.c["dve"])
                for dc in range(DCH):
                    mm = e.matmul(cx.PA[0:64, 0:512],
                                  wkvT[:, dc * 64:(dc + 1) * 64],
                                  xnT[:, dc * S + q4 * 512: dc * S + (q4 + 1) * 512],
                                  start=(dc == 0), stop=(dc == DCH - 1))
                cx.inc(mm, "pe")
            cx.run("tensor", pe_kp)
            pkp = cx.c["pe"]

            def dve_kp(e, q4=q4, pkp=pkp):
                e.wait_ge(cx.s["pe"], pkp)
                cx.inc(e.tensor_copy(kdup[0:64, 256 + q4 * 512: 256 + (q4 + 1) * 512],
                                     cx.PA[0:64, 0:512]), "dve")
            cx.run("vector", dve_kp)
        dkp = cx.c["dve"]

        def pool_kd(e, dkp=dkp):
            if _skip_kv:
                return
            e.wait_ge(cx.s["dve"], dkp)
            cx.inc(e.dma_start(kdup[64:128, :], kdup[0:64, :]), "gd", 16)
        cx.run("gpsimd", pool_kd)

        for r in ([] if _skip_kv else range(SCH)):
            def pe_vp(e, r=r):
                e.wait_ge(cx.s["dve"], cx.c["dve"])
                for dc in range(DCH):
                    mm = e.matmul(cx.PA[:, 1024:1088],
                                  xnT[:, dc * S + r * 128: dc * S + r * 128 + 128],
                                  wkvT[:, 512 + dc * 64: 512 + (dc + 1) * 64],
                                  start=(dc == 0), stop=(dc == DCH - 1))
                cx.inc(mm, "pe")
            cx.run("tensor", pe_vp)
            pvp = cx.c["pe"]

            def dve_vp(e, r=r, pvp=pvp):
                e.wait_ge(cx.s["pe"], pvp)
                cx.inc(e.tensor_copy(v2[:, (r + 2) * 64:(r + 3) * 64], cx.PA[:, 1024:1088]), "dve")
            cx.run("vector", dve_vp)
        bar = cx.snap()
        tap(cx, bar, "kdup", kdup[:, :], tap_outs, b)
        tap(cx, bar, "qT2", qT2[:, :], tap_outs, b)

        import os as _os2
        _skip_blocks = bool(_os2.environ.get("L1_SKIP_BLOCKS"))
        it = 0
        ptbank = {0: 0, 1: 0}
        avguard = [0, 0]
        sguard = [0, 0, 0, 0]
        for h in ([] if _skip_blocks else range(16)):
            hb = (h % 2) * 64
            for qb in range(SCH):
                sps = cx.PA[:, (it % 4) * 512:(it % 4) * 512 + 384]

                msel = min(qb, 2) * 384

                def pe_s(e, h=h, hb=hb, qb=qb, sps=sps, it=it, msel=msel, sg=sguard[0]):
                    if it == 0:
                        cx.waits(e, bar, ["dve", "gd"])
                    elif sg:
                        e.wait_ge(cx.s["act"], sg)
                    e.matmul(sps, cx.ident16[:, :], mask1[:, msel:msel + 384], start=True, stop=False)
                    cx.inc(e.matmul(sps,
                                    qT2[:, (h // 2) * S + qb * 128:(h // 2) * S + qb * 128 + 128][hb:hb + 64, :],
                                    kdup[hb:hb + 64, qb * 128: qb * 128 + 384],
                                    start=False, stop=True), "pe")
                cx.run("tensor", pe_s)
                ps_ = cx.c["pe"]

                def dve_s(e, ps_=ps_, sps=sps, pv=cx.c["act"]):
                    e.wait_ge(cx.s["pe"], ps_)
                    if pv:
                        e.wait_ge(cx.s["act"], pv)
                    cx.inc(e.reduce_max(cx.t_neg[:, :], sps, axis=AX.X, negate=True), "dve")
                cx.run("vector", dve_s)
                ds_ = cx.c["dve"]

                def act_s(e, ds_=ds_, sps=sps):
                    e.wait_ge(cx.s["dve"], ds_)
                    cx.inc(e.activation(pexp1[:, :], sps, AF.Exp,
                                        bias=cx.t_neg[:, :], accum_out=cx.t_asum[:, :]), "act")
                cx.run("scalar", act_s)
                as_ = cx.c["act"]
                sguard.pop(0)
                sguard.append(as_)

                def dve_s2(e, as_=as_):
                    e.wait_ge(cx.s["act"], as_)
                    e.wait_ge(cx.s["dve"], cx.c["dve"])
                    cx.inc(e.reciprocal(cx.t_ra[:, :], cx.t_asum[:, :]), "dve")
                cx.run("vector", dve_s2)
                cx.vstep(lambda e, pv=cx.c["pe"]: (e.wait_ge(cx.s["pe"], pv),
                         e.tensor_scalar(pnrm1[:, :], pexp1[:, :], cx.t_ra[:, :], None,
                                         OP.mult, OP.bypass))[1])
                ds2 = cx.c["dve"]

                for t in range(3):
                    bankid = t % 2
                    pt = cx.PTa if bankid == 0 else cx.PTb

                    def pe_t(e, t=t, ds2=ds2, pt=pt, ptg=ptbank[bankid]):
                        if t == 0:
                            e.wait_ge(cx.s["dve"], ds2)
                        e.wait_ge(cx.s["dve"], cx.c["dve"])
                        cx.inc(e.transpose(pt[:, 0:128], pnrm1[:, t * 128:(t + 1) * 128],
                                           cx.ident16[:, :]), "pe")
                    cx.run("tensor", pe_t)
                    pt_ = cx.c["pe"]

                    def dve_t(e, t=t, pt_=pt_, pt=pt):
                        e.wait_ge(cx.s["pe"], pt_)
                        cx.inc(e.tensor_copy(ptile1[:, t * 128:(t + 1) * 128], pt[:, 0:128]),
                               "dve")
                    cx.run("vector", dve_t)
                    ptbank[bankid] = cx.c["dve"]
                at_ = cx.c["dve"]
                pvs = cx.PB[0:64, (it % 2) * 512:(it % 2) * 512 + 128]

                def pe_pv(e, qb=qb, at_=at_, pvs=pvs, avg=avguard[0]):
                    e.wait_ge(cx.s["dve"], at_)
                    e.wait_ge(cx.s["dve"], cx.c["dve"])
                    for t in range(3):
                        mm = e.matmul(pvs, v2[:, (qb + t) * 64:(qb + t + 1) * 64],
                                      ptile1[:, t * 128:(t + 1) * 128],
                                      start=(t == 0), stop=(t == 2))
                    cx.inc(mm, "pe")
                cx.run("tensor", pe_pv)
                ppv = cx.c["pe"]

                if h % 2 == 0:
                    def dve_pv(e, h=h, qb=qb, ppv=ppv, pvs=pvs):
                        e.wait_ge(cx.s["pe"], ppv)
                        cx.inc(e.tensor_copy(
                            attnT2[:, (h // 2) * S + qb * 128:(h // 2) * S + qb * 128 + 128][0:64, :],
                            pvs), "dve")
                    cx.run("vector", dve_pv)
                else:
                    def dve_pv(e, h=h, qb=qb, ppv=ppv, pvs=pvs):
                        e.wait_ge(cx.s["pe"], ppv)
                        cx.inc(e.tensor_copy(ahead[0:64, qb * 128:(qb + 1) * 128], pvs), "dve")
                    cx.run("vector", dve_pv)
                avguard[0] = avguard[1]
                avguard[1] = cx.c["dve"]
                it += 1
            if h % 2 == 1:
                dah = cx.c["dve"]

                def pool_ah(e, h=h, dah=dah):
                    e.wait_ge(cx.s["dve"], dah)
                    cx.inc(e.dma_start(attnT2[:, (h // 2) * S:(h // 2 + 1) * S][64:128, :],
                                       ahead[0:64, :]), "gd", 16)
                cx.run("gpsimd", pool_ah)
        bar = cx.snap()
        tap(cx, bar, "attnT2", attnT2[:, :], tap_outs, b)

        def pool_wf(e):
            cx.waits(e, bar, ["pe"])
            cx.inc(e.dma_start(s3(cx.wbuf[:, :], D), Wf.rearrange("(oc p) i -> p oc i", p=128)),
                   "gd", 16)
        cx.run("gpsimd", pool_wf)
        gwf = cx.c["gd"]
        for oc in range(DCH):
            for ic in range(DCH):
                pt = cx.PTa if (oc * DCH + ic) % 2 == 0 else cx.PTb

                def pe_wf(e, oc=oc, ic=ic, pt=pt, gwf=gwf):
                    if oc == 0 and ic == 0:
                        e.wait_ge(cx.s["gd"], gwf)
                        e.wait_ge(cx.s["dve"], cx.c["dve"])
                    if cx.c["dve"] > 1:
                        e.wait_ge(cx.s["dve"], cx.c["dve"] - 1)
                    cx.inc(e.transpose(pt[:, 0:128],
                                       cx.wbuf[:, oc * D + ic * 128: oc * D + ic * 128 + 128],
                                       cx.ident16[:, :]), "pe")
                cx.run("tensor", pe_wf)
                pwf = cx.c["pe"]

                def dve_wf(e, oc=oc, ic=ic, pt=pt, pwf=pwf):
                    e.wait_ge(cx.s["pe"], pwf)
                    cx.inc(e.tensor_copy(wT[:, ic * D + oc * 128: ic * D + oc * 128 + 128],
                                         pt[:, 0:128]), "dve")
                cx.run("vector", dve_wf)
        awf = cx.c["dve"]

        for c in range(SCH):
            slot = (c % 2) * D

            def sync_xf(e, c=c, slot=slot):
                if c == 0:
                    e.wait_ge(cx.s["act"], cx.c["act"])
                if c >= 2:
                    e.wait_ge(cx.s["gd"], cx.c["gd"])
                cx.inc(e.dma_start(cx.xch[:, slot:slot + D],
                                   xpark_d[c * 128:(c + 1) * 128, :]), "ld", 16)
            cx.run("sync", sync_xf)
            lxf = cx.c["ld"]

            def pe_f(e, c=c, awf=awf):
                if c == 0:
                    e.wait_ge(cx.s["dve"], awf)
                else:
                    e.wait_ge(cx.s["dve"], cx.c["dve"])
                for hh in range(2):
                    sl = cx.PB[:, hh * 512:(hh + 1) * 512]
                    e.matmul(sl, cx.ones16[:, :], bfrow[:, hh * 512:(hh + 1) * 512],
                             start=True, stop=False)
                    for dc in range(DCH):
                        mm = e.matmul(sl, attnT2[:, dc * S + c * 128: dc * S + c * 128 + 128],
                                      wT[:, dc * D + hh * 512: dc * D + (hh + 1) * 512],
                                      start=False, stop=(dc == DCH - 1))
                cx.inc(mm, "pe")
            cx.run("tensor", pe_f)
            pf = cx.c["pe"]

            def dve_f(e, c=c, slot=slot, pf=pf, lxf=lxf):
                e.wait_ge(cx.s["pe"], pf)
                e.wait_ge(cx.s["ld"], lxf)
                cx.inc(e.tensor_tensor(cx.xch[:, slot:slot + D], cx.xch[:, slot:slot + D],
                                       cx.PB[:, :], OP.add), "dve")
            cx.run("vector", dve_f)
            df = cx.c["dve"]

            def pool_yf(e, c=c, slot=slot, df=df, yhome=yhome):
                e.wait_ge(cx.s["dve"], df)
                cx.inc(e.dma_start(yhome[c * 128:(c + 1) * 128, :], cx.xch[:, slot:slot + D]),
                       "gd", 16)
            cx.run("gpsimd", pool_yf)
        bar = cx.snap()

    def pool_end(e):
        e.wait_ge(cx.s["gd"], cx.c["gd"])
        e.wait_ge(cx.s["ld"], cx.c["ld"])
    cx.run("gpsimd", pool_end)


TAP_SHAPES_L1 = {
    "xnT2": [128, DCH * S], "p0": [128, 16], "kdup": [128, S + 256],
    "qT2": [128, DCH * S], "attnT2": [128, DCH * S],
}


def build_l1(taps=()):
    nc = bass.Bass("TRN2", target_bir_lowering=False)
    xin = nc.dram_tensor("xin", [BPC * S, D], F32, kind="ExternalInput")
    names = dict(Wq=[D, D], Wk=[64, D], Wv=[64, D], Wf=[D, D], bf=[D],
                 W1=[D, FF], b1=[FF], W2=[FF, D], b2=[D],
                 ln_g=[D], ln_b=[D], Wr=[D, 8], br=[8])
    tt = {k: nc.dram_tensor(k, v, F32, kind="ExternalInput") for k, v in names.items()}
    thr8_d = nc.dram_tensor("thr8", [1, 8], F32, kind="ExternalInput")
    ident16_d = nc.dram_tensor("ident16", [128, 128], BF16, kind="ExternalInput")
    identf_d = nc.dram_tensor("identf", [128, 128], F32, kind="ExternalInput")
    mask1_d = nc.dram_tensor("mask1", [128, 3 * 384], BF16, kind="ExternalInput")
    y_out = nc.dram_tensor("y", [BPC * S, D], F32, kind="ExternalOutput")
    xpark_d = nc.dram_tensor("xpark", [S, D], F32)
    btmp_d = nc.dram_tensor("btmp", [128, 8], F32)
    w1p_d = nc.dram_tensor("w1p", [D, FF], BF16)
    w2p_d = nc.dram_tensor("w2p", [FF, D], BF16)
    tap_outs = {t: nc.dram_tensor("tap_" + t, TAP_SHAPES_L1[t], F32, kind="ExternalOutput")
                for t in taps}
    with ExitStack() as st:
        block = st.enter_context(nc.Block())
        cx = Cx(nc, block, st)
        alloc_common(cx)
        emit_load_consts(cx, ident16_d, identf_d)
        emit_l1_body(cx, (xin, tt["Wq"], tt["Wk"], tt["Wv"], tt["Wf"], tt["bf"],
                          tt["W1"], tt["b1"], tt["W2"], tt["b2"], tt["ln_g"],
                          tt["ln_b"], tt["Wr"], tt["br"], thr8_d, y_out,
                          xpark_d, btmp_d, w1p_d, w2p_d, mask1_d), tap_outs)
    return nc



def _consts():
    i16 = np.eye(128, dtype=ml_dtypes.bfloat16)
    idf = np.eye(128, dtype=np.float32)
    m0 = np.full((128, 128), NEGBIG, np.float32)
    m0[:64, :64] = 0.0
    m0[64:, 64:] = 0.0
    m0 = m0.astype(ml_dtypes.bfloat16)
    q = np.arange(128)[:, None]
    k = np.arange(384)[None, :]
    diff = q - (k - 256)
    band = (diff >= 0) & (diff <= 256)
    ms = []
    for qb in (0, 1, 2):
        valid = band & (k >= max(0, 256 - qb * 128))
        ms.append(np.where(valid, 0.0, NEGBIG).astype(np.float32))
    m1 = np.concatenate(ms, axis=1).astype(ml_dtypes.bfloat16)
    return i16, idf, m0, m1


def _thr_inputs(inputs):
    specs = [("L1_ffn_W1", "bot"), ("L1_ffn_W1", "top"),
             ("L1_ffn_W2", "bot"), ("L1_ffn_W2", "top"),
             ("L1_ffn_b1", "bot"), ("L1_ffn_b1", "top"),
             ("L1_ffn_b2", "bot"), ("L1_ffn_b2", "top")]
    out = []
    NTOT = 128 * 32768
    for name, side in specs:
        w = np.asarray(inputs[name], np.float32).ravel()
        n = w.size
        k = max(1, int(n * 0.1 / 2))
        kk = k if side == "bot" else n - k + 1
        pad = NTOT - n
        arr = np.full(NTOT, -1.0e-30, np.float32)
        arr[:n] = w
        out.append((arr.reshape(128, 32768), np.array([[float(kk + pad)]], np.float32)))
    return out


def kernel(**inputs):
    i16, idf, m0, m1 = _consts()

    if "thr" not in _CACHE:
        _CACHE["thr"] = build_thr_program()
    thr_maps = [dict(search=a, kk=k, identf=idf) for a, k in _thr_inputs(inputs)]
    res_t = run_bass_kernel_spmd(_CACHE["thr"], thr_maps, core_ids=list(range(NCORE)))
    thr8 = np.ascontiguousarray(np.asarray(res_t.results[0]["thr8"], np.float32))

    if "l0" not in _CACHE:
        _CACHE["l0"] = build_l0()
    src = np.ascontiguousarray(np.asarray(inputs["L0_attn_src"], np.float32).reshape(32, S, D))
    common0 = dict(
        Wq=np.ascontiguousarray(np.asarray(inputs["L0_attn_Wq"], np.float32)),
        bq=np.ascontiguousarray(np.asarray(inputs["L0_attn_bq"], np.float32)),
        Wk=np.ascontiguousarray(np.asarray(inputs["L0_attn_Wk"], np.float32)),
        bk=np.ascontiguousarray(np.asarray(inputs["L0_attn_bk"], np.float32)),
        Wv=np.ascontiguousarray(np.asarray(inputs["L0_attn_Wv"], np.float32)),
        bv=np.ascontiguousarray(np.asarray(inputs["L0_attn_bv"], np.float32)),
        Wo=np.ascontiguousarray(np.asarray(inputs["L0_attn_Wo"], np.float32)),
        bo=np.ascontiguousarray(np.asarray(inputs["L0_attn_bo"], np.float32)),
        gamma_x=np.ascontiguousarray(np.asarray(inputs["L0_ffn_gamma_x"], np.float32)),
        gamma_slot=np.ascontiguousarray(np.asarray(inputs["L0_ffn_gamma_slot"], np.float32)),
        semb=np.ascontiguousarray(np.asarray(inputs["L0_ffn_slot_embeds"], np.float32).reshape(ET, D)),
        w1=np.ascontiguousarray(np.asarray(inputs["L0_ffn_w1"], np.float32)),
        w2=np.ascontiguousarray(np.asarray(inputs["L0_ffn_w2"], np.float32)),
        ident16=i16, identf=idf, mask0=m0,
    )
    maps0 = [dict(common0, src=np.ascontiguousarray(
        src[c * BPC:(c + 1) * BPC].reshape(BPC * S, D))) for c in range(NCORE)]
    res0 = run_bass_kernel_spmd(_CACHE["l0"], maps0, core_ids=list(range(NCORE)))

    if "l1" not in _CACHE:
        _CACHE["l1"] = build_l1()
    common1 = dict(
        Wq=np.ascontiguousarray(np.asarray(inputs["L1_attn_Wq"], np.float32)),
        Wk=np.ascontiguousarray(np.asarray(inputs["L1_attn_Wk"], np.float32)),
        Wv=np.ascontiguousarray(np.asarray(inputs["L1_attn_Wv"], np.float32)),
        Wf=np.ascontiguousarray(np.asarray(inputs["L1_attn_Wf"], np.float32)),
        bf=np.ascontiguousarray(np.asarray(inputs["L1_attn_bf"], np.float32)),
        W1=np.ascontiguousarray(np.asarray(inputs["L1_ffn_W1"], np.float32)),
        b1=np.ascontiguousarray(np.asarray(inputs["L1_ffn_b1"], np.float32)),
        W2=np.ascontiguousarray(np.asarray(inputs["L1_ffn_W2"], np.float32)),
        b2=np.ascontiguousarray(np.asarray(inputs["L1_ffn_b2"], np.float32)),
        ln_g=np.ascontiguousarray(np.asarray(inputs["L1_ffn_ln_gamma"], np.float32)),
        ln_b=np.ascontiguousarray(np.asarray(inputs["L1_ffn_ln_beta"], np.float32)),
        Wr=np.ascontiguousarray(np.asarray(inputs["L1_ffn_Wr"], np.float32)),
        br=np.ascontiguousarray(np.asarray(inputs["L1_ffn_br"], np.float32)),
        thr8=thr8, ident16=i16, identf=idf, mask1=m1,
    )
    maps1 = [dict(common1, xin=np.ascontiguousarray(
        np.asarray(res0.results[c]["x1"], np.float32))) for c in range(NCORE)]
    res1 = run_bass_kernel_spmd(_CACHE["l1"], maps1, core_ids=list(range(NCORE)))
    out = np.concatenate([np.asarray(r["y"], np.float32).reshape(BPC, S, D)
                          for r in res1.results], axis=0)
    return out



def build_fused(taps=()):
    nc = bass.Bass("TRN2", target_bir_lowering=False)
    src_d = nc.dram_tensor("src", [BPC * S, D], F32, kind="ExternalInput")
    n0 = dict(Wq=[D, D], bq=[D], Wk=[D, D], bk=[D], Wv=[D, D], bv=[D],
              Wo=[D, D], bo=[D], gamma_x=[D], gamma_slot=[D],
              semb=[ET, D], w1=[D, FF], w2=[FF, D])
    t0 = {k: nc.dram_tensor(k, v, F32, kind="ExternalInput") for k, v in n0.items()}
    n1 = dict(Wq1=[D, D], Wk1=[64, D], Wv1=[64, D], Wf1=[D, D], bf1=[D],
              W1=[D, FF], b1=[FF], W2=[FF, D], b2=[D],
              ln_g=[D], ln_b=[D], Wr=[D, 8], br=[8])
    t1 = {k: nc.dram_tensor(k, v, F32, kind="ExternalInput") for k, v in n1.items()}
    search = nc.dram_tensor("search", [128, 32768], F32, kind="ExternalInput")
    kk = nc.dram_tensor("kk", [1, 1], F32, kind="ExternalInput")
    ident16_d = nc.dram_tensor("ident16", [128, 128], BF16, kind="ExternalInput")
    identf_d = nc.dram_tensor("identf", [128, 128], F32, kind="ExternalInput")
    mask0_d = nc.dram_tensor("mask0", [128, 128], BF16, kind="ExternalInput")
    mask1_d = nc.dram_tensor("mask1", [128, 3 * 384], BF16, kind="ExternalInput")
    y_out = nc.dram_tensor("y", [BPC * S, D], F32, kind="ExternalOutput")
    x1_d = nc.dram_tensor("x1scr", [BPC * S, D], F32)
    xpark_d = nc.dram_tensor("xpark", [S, D], F32)
    xnT_park = nc.dram_tensor("xnT_park", [D, S], BF16)
    slots_park = [nc.dram_tensor(f"slots_park{i}", [D, S], BF16) for i in range(2)]
    sepT_d = nc.dram_tensor("sepT", [D, ET], BF16)
    btmp_d = nc.dram_tensor("btmp", [128, 8], F32)
    w1p_d = nc.dram_tensor("w1p", [D, FF], BF16)
    w2p_d = nc.dram_tensor("w2p", [FF, D], BF16)
    cc_in = nc.dram_tensor("cc_in", [1, 128], F32)
    cc_out = nc.dram_tensor("cc_out", [8, 128], F32)

    with ExitStack() as st:
        block = st.enter_context(nc.Block())
        cx = Cx(nc, block, st)
        cx.t_sum = cx.sb("t_sum", [128, 1], F32)
        cx.t_ssq = cx.sb("t_ssq", [128, 1], F32)
        cx.t_mean = cx.sb("t_mean", [128, 1], F32)
        cx.t_m2 = cx.sb("t_m2", [128, 1], F32)
        cx.t_var = cx.sb("t_var", [128, 1], F32)
        cx.t_sd = cx.sb("t_sd", [128, 1], F32)
        cx.t_sdv = cx.sb("t_sdv", [128, 1], F32)
        cx.t_rstd = cx.sb("t_rstd", [128, 1], F32)
        cx.t_nrm = cx.sb("t_nrm", [128, 1], F32)
        cx.t_tau = cx.sb("t_tau", [128, 1], F32)
        cx.t_rt = cx.sb("t_rt", [128, 1], F32)
        cx.t_neg = cx.sb("t_neg", [128, 1], F32)
        cx.t_asum = cx.sb("t_asum", [128, 1], F32)
        cx.t_ra = cx.sb("t_ra", [128, 1], F32)
        cx.ident16 = cx.sb("ident16_s", [128, 128], BF16)
        cx.identf = cx.sb("identf_s", [128, 128], F32)
        cx.onesf = cx.sb("onesf_s", [1, 128], F32)
        cx.epsb = cx.sb("epsb_s", [128, 1], F32)
        cx.ones16 = cx.sb("ones16_s", [1, 128], BF16)
        cx.xch = cx.sb("xch", [128, 2 * D], F32)
        cx.dchunk = cx.sb("dchunk", [128, S], BF16)
        cx.t_dump = cx.dchunk[:, 0:D]
        cx.t_xn = cx.dchunk[:, D:2 * D]
        cx.PA = cx.ps("PA", [128, 2048], F32)
        cx.PB = cx.ps("PB", [128, 1024], F32)
        cx.PTa = cx.ps("PTa", [128, 1024], BF16)
        cx.PTb = cx.ps("PTb", [128, 1024], BF16)
        thr8_s = cx.sb("thr8_s", [1, 8], F32)
        emit_load_consts(cx, ident16_d, identf_d)
        bar = cx.snap()
        with ExitStack() as st2:
            saved = cx.st
            cx.st = st2
            bar = emit_threshold_search(cx, bar, search, kk, cc_in, cc_out, thr8_s)
            cx.st = saved
        def pool_bar(e):
            cx.waits(e, bar)
            e.wait_ge(cx.s["gd"], cx.c["gd"])
            e.sem_inc(cx.s["cc"], 1)
        cx.run("gpsimd", pool_bar)
        cx.c["cc"] += 1
        cx.BIG1 = cx.sb("BIG1", [128, 32768], BF16)
        cx.BIG2 = cx.sb("BIG2", [128, 32768], BF16)
        cx.M3 = cx.sb("M3", [128, 16384], BF16)
        cx.wbuf = cx.sb("wbuf", [128, 8192], BF16)
        bar = cx.snap()
        with ExitStack() as stl0:
            saved = cx.st
            cx.st = stl0
            emit_l0_body(cx, (src_d, t0["Wq"], t0["bq"], t0["Wk"], t0["bk"], t0["Wv"],
                              t0["bv"], t0["Wo"], t0["bo"], t0["gamma_x"],
                              t0["gamma_slot"], t0["semb"], t0["w1"], t0["w2"],
                              x1_d, xpark_d, xnT_park, slots_park, sepT_d,
                              mask0_d), {t: None for t in ()})
            cx.st = saved

        def pool_bar2(e):
            cx.waits(e, cx.snap())
            e.wait_ge(cx.s["gd"], cx.c["gd"])
            e.wait_ge(cx.s["ld"], cx.c["ld"])
            e.sem_inc(cx.s["cc"], 1)
        cx.run("gpsimd", pool_bar2)
        cx.c["cc"] += 1
        emit_l1_body(cx, (x1_d, t1["Wq1"], t1["Wk1"], t1["Wv1"], t1["Wf1"],
                          t1["bf1"], t1["W1"], t1["b1"], t1["W2"], t1["b2"],
                          t1["ln_g"], t1["ln_b"], t1["Wr"], t1["br"], None,
                          y_out, xpark_d, btmp_d, w1p_d, w2p_d, mask1_d),
                     {}, thr8_tile=thr8_s)
    return nc


def kernel_fused(**inputs):
    i16, idf, m0, m1 = _consts()
    if "fused" not in _CACHE:
        _CACHE["fused"] = build_fused()
    src = np.ascontiguousarray(np.asarray(inputs["L0_attn_src"], np.float32).reshape(32, S, D))
    segpos = np.asarray(inputs["L1_attn_segment_pos"])
    common = dict(
        Wq=np.ascontiguousarray(np.asarray(inputs["L0_attn_Wq"], np.float32)),
        bq=np.ascontiguousarray(np.asarray(inputs["L0_attn_bq"], np.float32)),
        Wk=np.ascontiguousarray(np.asarray(inputs["L0_attn_Wk"], np.float32)),
        bk=np.ascontiguousarray(np.asarray(inputs["L0_attn_bk"], np.float32)),
        Wv=np.ascontiguousarray(np.asarray(inputs["L0_attn_Wv"], np.float32)),
        bv=np.ascontiguousarray(np.asarray(inputs["L0_attn_bv"], np.float32)),
        Wo=np.ascontiguousarray(np.asarray(inputs["L0_attn_Wo"], np.float32)),
        bo=np.ascontiguousarray(np.asarray(inputs["L0_attn_bo"], np.float32)),
        gamma_x=np.ascontiguousarray(np.asarray(inputs["L0_ffn_gamma_x"], np.float32)),
        gamma_slot=np.ascontiguousarray(np.asarray(inputs["L0_ffn_gamma_slot"], np.float32)),
        semb=np.ascontiguousarray(np.asarray(inputs["L0_ffn_slot_embeds"], np.float32).reshape(ET, D)),
        w1=np.ascontiguousarray(np.asarray(inputs["L0_ffn_w1"], np.float32)),
        w2=np.ascontiguousarray(np.asarray(inputs["L0_ffn_w2"], np.float32)),
        Wq1=np.ascontiguousarray(np.asarray(inputs["L1_attn_Wq"], np.float32)),
        Wk1=np.ascontiguousarray(np.asarray(inputs["L1_attn_Wk"], np.float32)),
        Wv1=np.ascontiguousarray(np.asarray(inputs["L1_attn_Wv"], np.float32)),
        Wf1=np.ascontiguousarray(np.asarray(inputs["L1_attn_Wf"], np.float32)),
        bf1=np.ascontiguousarray(np.asarray(inputs["L1_attn_bf"], np.float32)),
        W1=np.ascontiguousarray(np.asarray(inputs["L1_ffn_W1"], np.float32)),
        b1=np.ascontiguousarray(np.asarray(inputs["L1_ffn_b1"], np.float32)),
        W2=np.ascontiguousarray(np.asarray(inputs["L1_ffn_W2"], np.float32)),
        b2=np.ascontiguousarray(np.asarray(inputs["L1_ffn_b2"], np.float32)),
        ln_g=np.ascontiguousarray(np.asarray(inputs["L1_ffn_ln_gamma"], np.float32)),
        ln_b=np.ascontiguousarray(np.asarray(inputs["L1_ffn_ln_beta"], np.float32)),
        Wr=np.ascontiguousarray(np.asarray(inputs["L1_ffn_Wr"], np.float32)),
        br=np.ascontiguousarray(np.asarray(inputs["L1_ffn_br"], np.float32)),
        ident16=i16, identf=idf, mask0=m0, mask1=m1,
    )
    thr_in = _thr_inputs(inputs)
    maps = []
    for c in range(NCORE):
        arr, kkv = thr_in[c]
        maps.append(dict(common, src=np.ascontiguousarray(
            src[c * BPC:(c + 1) * BPC].reshape(BPC * S, D)),
            search=arr, kk=kkv))
    res = run_bass_kernel_spmd(_CACHE["fused"], maps, core_ids=list(range(NCORE)))
    return np.concatenate([np.asarray(r["y"], np.float32).reshape(BPC, S, D)
                           for r in res.results], axis=0)


_UNFUSED_KERNEL = kernel
kernel = kernel_fused


# version 08 (final)
# speedup vs baseline: 1.0075x; 1.0075x over previous
import numpy as np
import ml_dtypes
import concourse.bass as bass
import concourse.mybir as mybir
from concourse.alu_op_type import AluOpType as OP
from concourse.bass_utils import run_bass_kernel_spmd
from contextlib import ExitStack

F32 = mybir.dt.float32
BF16 = mybir.dt.bfloat16
AF = mybir.ActivationFunctionType
AX = mybir.AxisListType

NCORE = 8
BPC = 4
S = 2048
D = 1024
SCH = S // 128
DCH = D // 128
FF = 4096
FCH = FF // 128
ET = 4096
ETCH = ET // 32 // 4
NEGBIG = -1.0e30
SC0 = float(128.0 ** -0.5)
SC1 = float(64.0 ** -0.5)
GC1 = 1.5957691216057308
GC2 = 0.07135481282426948

_CACHE = {}


class Cx:
    def __init__(self, nc, block, st):
        self.nc, self.block, self.st = nc, block, st
        self.c = {}
        self.s = {}
        for name in ("pe", "dve", "act", "ld", "gd", "cc"):
            self.s[name] = st.enter_context(nc.semaphore("sem_" + name))
            self.c[name] = 0

    def run(self, eng, fn):
        getattr(self.block, eng)(fn)

    def snap(self):
        return dict(self.c)

    def waits(self, e, snap, names=None):
        for n in (names if names is not None else list(snap)):
            if snap.get(n, 0) > 0:
                e.wait_ge(self.s[n], snap[n])

    def inc(self, instr, name, n=1):
        instr.then_inc(self.s[name], n)
        self.c[name] += n
        return self.c[name]

    def vstep(self, fn):
        def blk(e, fn=fn):
            if self.c["dve"] > 0:
                e.wait_ge(self.s["dve"], self.c["dve"])
            self.inc(fn(e), "dve")
        self.run("vector", blk)

    def astep(self, fn):
        def blk(e, fn=fn):
            if self.c["act"] > 0:
                e.wait_ge(self.s["act"], self.c["act"])
            self.inc(fn(e), "act")
        self.run("scalar", blk)

    def sb(self, name, shape, dt):
        return self.st.enter_context(self.nc.sbuf_tensor(name, shape, dt))

    def ps(self, name, shape, dt):
        return self.st.enter_context(self.nc.psum_tensor(name, shape, dt))


def cr_ap(dram, width):
    return dram.rearrange("(c p) w -> p c w", p=128)


def colblock_ap(dram, c0, w):
    return dram[:, c0:c0 + w].rearrange("(c p) w -> p c w", p=128)


def s3(tile_ap, w):
    return tile_ap.rearrange("p (c w) -> p c w", w=w)



def emit_threshold_search(cx, bar, search_dram, kk_dram, cc_in, cc_out,
                          thr8_sbuf, n_iter=40):
    W = 32768
    av = cx.sb("thr_a", [128, W], F32)
    dump = cx.sb("thr_dump", [128, W], BF16)
    lo = cx.sb("thr_lo", [128, 1], F32)
    hi = cx.sb("thr_hi", [128, 1], F32)
    mid = cx.sb("thr_m", [128, 1], F32)
    kkb = cx.sb("thr_kkb", [128, 1], F32)
    pcnt = cx.sb("thr_pcnt", [128, 1], F32)
    crow = cx.sb("thr_crow", [1, 128], F32)
    ctot = cx.sb("thr_ctot", [1, 1], F32)
    cntb = cx.sb("thr_cntb", [128, 1], F32)
    geb = cx.sb("thr_geb", [128, 1], F32)
    gnb = cx.sb("thr_gnb", [128, 1], F32)
    tt1 = cx.sb("thr_tt1", [128, 1], F32)
    tt2 = cx.sb("thr_tt2", [128, 1], F32)
    skk = cx.sb("thr_skk", [1, 1], F32)
    PF = cx.PB

    def sync_f(e):
        cx.waits(e, bar)
        cx.inc(e.dma_start(av[:, :], search_dram[:, :]), "ld", 16)
        cx.inc(e.dma_start(skk[:, :], kk_dram[:, :]), "ld", 16)
    cx.run("sync", sync_f)
    ld0 = cx.c["ld"]

    def act_p0(e):
        e.wait_ge(cx.s["ld"], ld0)
        cx.inc(e.activation(av[:, :], av[:, :], AF.Abs), "act")
    cx.run("scalar", act_p0)
    a0 = cx.c["act"]

    def dve_p1(e):
        e.wait_ge(cx.s["act"], a0)
        e.memset(lo[:, :], -1.0)
        cx.inc(e.reduce_max(pcnt[:, :], av[:, :], axis=AX.X), "dve")
    cx.run("vector", dve_p1)
    d0 = cx.c["dve"]

    def pe_p1(e):
        e.wait_ge(cx.s["dve"], d0)
        cx.inc(e.transpose(PF[0:1, 0:128], pcnt[:, :], cx.identf[:, :]), "pe")
    cx.run("tensor", pe_p1)
    p0_ = cx.c["pe"]

    def dve_p2(e):
        e.wait_ge(cx.s["pe"], p0_)
        cx.inc(e.tensor_copy(crow[:, :], PF[0:1, 0:128]), "dve")
    cx.run("vector", dve_p2)
    cx.vstep(lambda e: e.reduce_max(ctot[:, :], crow[:, :], axis=AX.X))
    d1 = cx.c["dve"]

    def pe_p2(e):
        e.wait_ge(cx.s["dve"], d1)
        e.matmul(PF[:, 128:129], cx.onesf[:, :], ctot[:, :], start=True, stop=True)
        cx.inc(e.matmul(PF[:, 129:130], cx.onesf[:, :], skk[:, :], start=True, stop=True), "pe")
    cx.run("tensor", pe_p2)
    p1_ = cx.c["pe"]

    def dve_p3(e):
        e.wait_ge(cx.s["pe"], p1_)
        e.tensor_copy(hi[:, :], PF[:, 128:129])
        cx.inc(e.tensor_copy(kkb[:, :], PF[:, 129:130]), "dve")
    cx.run("vector", dve_p3)

    for _ in range(n_iter):
        cx.vstep(lambda e: e.tensor_tensor(mid[:, :], lo[:, :], hi[:, :], OP.add))
        cx.vstep(lambda e: e.tensor_scalar(mid[:, :], mid[:, :], 0.5, None, OP.mult, OP.bypass))
        cx.vstep(lambda e: e.tensor_scalar(dump[:, :], av[:, :], mid[:, :], None,
                                           OP.is_le, OP.add, accum_out=pcnt[:, :]))
        da = cx.c["dve"]

        def pe_a(e, da=da):
            e.wait_ge(cx.s["dve"], da)
            cx.inc(e.transpose(PF[0:1, 0:128], pcnt[:, :], cx.identf[:, :]), "pe")
        cx.run("tensor", pe_a)
        pa_ = cx.c["pe"]

        def dve_b(e, pa_=pa_):
            e.wait_ge(cx.s["pe"], pa_)
            cx.inc(e.tensor_copy(crow[:, :], PF[0:1, 0:128]), "dve")
        cx.run("vector", dve_b)
        cx.vstep(lambda e: e.reduce_sum(ctot[:, :], crow[:, :], axis=AX.X))
        db = cx.c["dve"]

        def pe_b(e, db=db):
            e.wait_ge(cx.s["dve"], db)
            cx.inc(e.matmul(PF[:, 128:129], cx.onesf[:, :], ctot[:, :],
                            start=True, stop=True), "pe")
        cx.run("tensor", pe_b)
        pb_ = cx.c["pe"]

        def dve_c(e, pb_=pb_):
            e.wait_ge(cx.s["pe"], pb_)
            cx.inc(e.tensor_copy(cntb[:, :], PF[:, 128:129]), "dve")
        cx.run("vector", dve_c)
        cx.vstep(lambda e: e.tensor_scalar(geb[:, :], cntb[:, :], kkb[:, :], None,
                                           OP.is_ge, OP.bypass))
        cx.vstep(lambda e: e.tensor_scalar(gnb[:, :], geb[:, :], -1.0, 1.0, OP.mult, OP.add))

        def bl1(e):
            e.tensor_tensor(tt1[:, :], mid[:, :], geb[:, :], OP.mult)
            return e.tensor_tensor(tt2[:, :], hi[:, :], gnb[:, :], OP.mult)
        cx.vstep(bl1)
        cx.vstep(lambda e: e.tensor_tensor(hi[:, :], tt1[:, :], tt2[:, :], OP.add))

        def bl2(e):
            e.tensor_tensor(tt1[:, :], mid[:, :], gnb[:, :], OP.mult)
            return e.tensor_tensor(tt2[:, :], lo[:, :], geb[:, :], OP.mult)
        cx.vstep(bl2)
        cx.vstep(lambda e: e.tensor_tensor(lo[:, :], tt1[:, :], tt2[:, :], OP.add))

    dfin = cx.c["dve"]

    def pool_fin(e):
        e.wait_ge(cx.s["dve"], dfin)
        cx.inc(e.dma_start(cc_in[0:1, 0:1], hi[0:1, 0:1]), "gd", 16)
        e.wait_ge(cx.s["gd"], cx.c["gd"])
        cx.inc(e.collective_compute(
            "AllGather", mybir.AluOpType.bypass,
            replica_groups=[list(range(NCORE))],
            ins=[cc_in.ap().opt()], outs=[cc_out.ap().opt()]), "cc")
        e.wait_ge(cx.s["cc"], cx.c["cc"])
        cx.inc(e.dma_start(thr8_sbuf[:, :], cc_out[:, 0:1].rearrange("r one -> one r"),
                           allow_slow_non_contiguous=True), "gd", 16)
    cx.run("gpsimd", pool_fin)
    return cx.snap()


def build_thr_program():
    nc = bass.Bass("TRN2", target_bir_lowering=False)
    search = nc.dram_tensor("search", [128, 32768], F32, kind="ExternalInput")
    kk = nc.dram_tensor("kk", [1, 1], F32, kind="ExternalInput")
    identf_d = nc.dram_tensor("identf", [128, 128], F32, kind="ExternalInput")
    thr8 = nc.dram_tensor("thr8", [1, 8], F32, kind="ExternalOutput")
    cc_in = nc.dram_tensor("cc_in", [1, 128], F32)
    cc_out = nc.dram_tensor("cc_out", [8, 128], F32)
    with ExitStack() as st:
        block = st.enter_context(nc.Block())
        cx = Cx(nc, block, st)
        cx.identf = cx.sb("identf_s", [128, 128], F32)
        cx.onesf = cx.sb("onesf_s", [1, 128], F32)
        cx.PB = cx.ps("PB", [128, 1024], F32)
        thr8_s = cx.sb("thr8_s", [1, 8], F32)

        def sync0(e):
            cx.inc(e.dma_start(cx.identf[:, :], identf_d[:, :]), "ld", 16)
        cx.run("sync", sync0)
        l0_ = cx.c["ld"]

        def dve0(e):
            e.wait_ge(cx.s["ld"], l0_)
            cx.inc(e.memset(cx.onesf[:, :], 1.0), "dve")
        cx.run("vector", dve0)
        bar = emit_threshold_search(cx, cx.snap(), search, kk, cc_in, cc_out, thr8_s)

        def pool_out(e):
            e.wait_ge(cx.s["gd"], cx.c["gd"])
            cx.inc(e.dma_start(thr8[:, :], thr8_s[:, :]), "gd", 16)
            e.wait_ge(cx.s["gd"], cx.c["gd"])
        cx.run("gpsimd", pool_out)
    return nc



def alloc_common(cx):
    cx.t_sum = cx.sb("t_sum", [128, 1], F32)
    cx.t_ssq = cx.sb("t_ssq", [128, 1], F32)
    cx.t_mean = cx.sb("t_mean", [128, 1], F32)
    cx.t_m2 = cx.sb("t_m2", [128, 1], F32)
    cx.t_var = cx.sb("t_var", [128, 1], F32)
    cx.t_sd = cx.sb("t_sd", [128, 1], F32)
    cx.t_sdv = cx.sb("t_sdv", [128, 1], F32)
    cx.t_rstd = cx.sb("t_rstd", [128, 1], F32)
    cx.t_nrm = cx.sb("t_nrm", [128, 1], F32)
    cx.t_tau = cx.sb("t_tau", [128, 1], F32)
    cx.t_rt = cx.sb("t_rt", [128, 1], F32)
    cx.t_neg = cx.sb("t_neg", [128, 1], F32)
    cx.t_asum = cx.sb("t_asum", [128, 1], F32)
    cx.t_ra = cx.sb("t_ra", [128, 1], F32)
    cx.ident16 = cx.sb("ident16_s", [128, 128], BF16)
    cx.identf = cx.sb("identf_s", [128, 128], F32)
    cx.onesf = cx.sb("onesf_s", [1, 128], F32)
    cx.epsb = cx.sb("epsb_s", [128, 1], F32)
    cx.ones16 = cx.sb("ones16_s", [1, 128], BF16)
    cx.xch = cx.sb("xch", [128, 2 * D], F32)
    cx.dchunk = cx.sb("dchunk", [128, S], BF16)
    cx.t_dump = cx.dchunk[:, 0:D]
    cx.t_xn = cx.dchunk[:, D:2 * D]
    cx.PA = cx.ps("PA", [128, 2048], F32)
    cx.PB = cx.ps("PB", [128, 1024], F32)
    cx.PTa = cx.ps("PTa", [128, 1024], BF16)
    cx.PTb = cx.ps("PTb", [128, 1024], BF16)
    cx.BIG1 = cx.sb("BIG1", [128, 32768], BF16)
    cx.BIG2 = cx.sb("BIG2", [128, 32768], BF16)
    cx.M3 = cx.sb("M3", [128, 16384], BF16)
    cx.wbuf = cx.sb("wbuf", [128, 8192], BF16)


def emit_load_consts(cx, ident16_d, identf_d):
    def sync_c(e):
        cx.inc(e.dma_start(cx.ident16[:, :], ident16_d[:, :]), "ld", 16)
        cx.inc(e.dma_start(cx.identf[:, :], identf_d[:, :]), "ld", 16)
    cx.run("sync", sync_c)
    lc = cx.c["ld"]

    def dve_c(e):
        e.wait_ge(cx.s["ld"], lc)
        e.memset(cx.onesf[:, :], 1.0)
        e.memset(cx.epsb[:, :], 1e-5)
        cx.inc(e.memset(cx.ones16[:, :], 1.0), "dve")
    cx.run("vector", dve_c)
    return cx.snap()


def emit_vec_cols(cx, e, dram_vec, tile, ncols):
    for c in range(ncols):
        cx.inc(e.dma_start(tile[:, c:c + 1],
                           dram_vec[c * 128:(c + 1) * 128].rearrange("(p one) -> p one", one=1)), "ld", 16)


def emit_ln(cx, bar, xhome, xnT, xnp, xnpT, want_tau, router=False):
    for c in range(SCH):
        slot = (c % 2) * D
        xc = cx.xch[:, slot:slot + D]

        def sync_x(e, c=c, slot=slot):
            if c == 0:
                cx.waits(e, bar)
            else:
                e.wait_ge(cx.s["act"], cx.c["act"])
            cx.inc(e.dma_start(cx.xch[:, slot:slot + D],
                               xhome[c * 128:(c + 1) * 128, :]), "ld", 16)
        cx.run("sync", sync_x)
        lx = cx.c["ld"]

        def dve_1(e, lx=lx, xc=xc):
            e.wait_ge(cx.s["ld"], lx)
            cx.inc(e.reduce_sum(cx.t_sum[:, :], xc, axis=AX.X), "dve")
        cx.run("vector", dve_1)
        d1 = cx.c["dve"]

        def act_1(e, d1=d1, xc=xc):
            e.wait_ge(cx.s["dve"], d1)
            cx.inc(e.activation(cx.t_dump, xc, AF.Square, accum_out=cx.t_ssq[:, :]), "act")
        cx.run("scalar", act_1)
        a1 = cx.c["act"]

        def dve_2(e, a1=a1):
            e.wait_ge(cx.s["act"], a1)
            if cx.c["dve"] > 0:
                e.wait_ge(cx.s["dve"], cx.c["dve"])
            cx.inc(e.tensor_scalar(cx.t_mean[:, :], cx.t_sum[:, :], 1.0 / D, None,
                                   OP.mult, OP.bypass), "dve")
        cx.run("vector", dve_2)
        cx.vstep(lambda e: e.tensor_tensor(cx.t_m2[:, :], cx.t_mean[:, :], cx.t_mean[:, :], OP.mult))
        cx.vstep(lambda e: e.scalar_tensor_tensor(cx.t_var[:, :], cx.t_ssq[:, :], 1.0 / D,
                                                  cx.t_m2[:, :], OP.mult, OP.subtract))
        d2 = cx.c["dve"]

        def act_2(e, d2=d2):
            e.wait_ge(cx.s["dve"], d2)
            if cx.c["act"] > 0:
                e.wait_ge(cx.s["act"], cx.c["act"])
            ii = e.activation(cx.t_sd[:, :], cx.t_var[:, :], AF.Ln, bias=cx.epsb[:, :])
            if want_tau:
                ii = e.activation(cx.t_sdv[:, :], cx.t_var[:, :], AF.Ln)
            cx.inc(ii, "act")
        cx.run("scalar", act_2)
        cx.astep(lambda e: e.activation(cx.t_rstd[:, :], cx.t_sd[:, :], AF.Exp, scale=-0.5))
        if want_tau:
            cx.astep(lambda e: e.activation(cx.t_sdv[:, :], cx.t_sdv[:, :], AF.Exp, scale=0.5))
        a2 = cx.c["act"]

        def dve_3a(e, a2=a2):
            e.wait_ge(cx.s["act"], a2)
            e.wait_ge(cx.s["dve"], cx.c["dve"])
            cx.inc(e.nop(), "dve")
        cx.run("vector", dve_3a)

        def st_xn(e, c=c, xc=xc):
            ii = e.tensor_scalar(cx.t_xn[:, :], xc, cx.t_mean[:, :], cx.t_rstd[:, :],
                                 OP.subtract, OP.mult)
            if router:
                ii = e.tensor_copy(cx.r_mean[:, c:c + 1], cx.t_mean[:, :])
            return ii
        cx.vstep(st_xn)
        if router:
            cx.vstep(lambda e: e.tensor_tensor(cx.t_nrm[:, :], cx.t_rstd[:, :],
                                               cx.t_rstd[:, :], OP.mult))
            cx.vstep(lambda e: e.tensor_tensor(cx.t_nrm[:, :], cx.t_nrm[:, :],
                                               cx.t_var[:, :], OP.mult))
            dvr = cx.c["dve"]

            def act_vr(e, dvr=dvr):
                e.wait_ge(cx.s["dve"], dvr)
                if cx.c["act"] > 0:
                    e.wait_ge(cx.s["act"], cx.c["act"])
                cx.inc(e.activation(cx.t_nrm[:, :], cx.t_nrm[:, :], AF.Ln,
                                    bias=cx.epsb[:, :]), "act")
            cx.run("scalar", act_vr)
            cx.astep(lambda e, c=c: e.activation(cx.r_rstd[:, c:c + 1], cx.t_nrm[:, :],
                                                 AF.Exp, scale=-0.5))
        if want_tau:
            cx.vstep(lambda e: e.tensor_tensor(cx.t_nrm[:, :], cx.t_rstd[:, :],
                                               cx.t_sdv[:, :], OP.mult))
            cx.vstep(lambda e: e.tensor_scalar(cx.t_nrm[:, :], cx.t_nrm[:, :], 32.0, 1e-12,
                                               OP.mult, OP.max))
            cx.vstep(lambda e: e.reciprocal(cx.t_tau[:, :], cx.t_nrm[:, :]))
            cx.vstep(lambda e: e.tensor_tensor(cx.t_rt[:, :], cx.t_rstd[:, :],
                                               cx.t_tau[:, :], OP.mult))
            cx.vstep(lambda e: e.tensor_scalar(cx.t_rt[:, :], cx.t_rt[:, :], 32.0, None,
                                               OP.mult, OP.bypass))
            cx.vstep(lambda e, c=c, xc=xc: e.tensor_scalar(
                xnp[:, c * D:(c + 1) * D], xc, cx.t_mean[:, :], cx.t_rt[:, :],
                OP.subtract, OP.mult))
        d3 = cx.c["dve"]

        srcs = [(True, xnT)]
        if want_tau:
            srcs.append((False, xnpT))
        for use_txn, dstT in srcs:
            for dc in range(DCH):
                pt = cx.PTa if dc % 2 == 0 else cx.PTb
                src_ap = (cx.t_xn[:, dc * 128:(dc + 1) * 128] if use_txn else
                          xnp[:, c * D + dc * 128: c * D + dc * 128 + 128])

                def pe_t(e, pt=pt, src_ap=src_ap, d3=d3, dc=dc):
                    if dc == 0:
                        e.wait_ge(cx.s["dve"], d3)
                    if cx.c["dve"] > 1:
                        e.wait_ge(cx.s["dve"], cx.c["dve"] - 1)
                    cx.inc(e.transpose(pt[:, 0:128], src_ap, cx.ident16[:, :]), "pe")
                cx.run("tensor", pe_t)
                pn = cx.c["pe"]

                def dve_t(e, c=c, dc=dc, pt=pt, dstT=dstT, pn=pn):
                    e.wait_ge(cx.s["pe"], pn)
                    cx.inc(e.tensor_copy(dstT[:, dc * S + c * 128: dc * S + c * 128 + 128],
                                         pt[:, 0:128]), "dve")
                cx.run("vector", dve_t)

        if router:
            atr = cx.c["act"]

            def pe_r(e, c=c, atr=atr):
                e.wait_ge(cx.s["act"], atr)
                e.wait_ge(cx.s["dve"], cx.c["dve"])
                for dc in range(DCH):
                    mm = e.matmul(cx.PB[:, 0:8],
                                  xnT[:, dc * S + c * 128: dc * S + c * 128 + 128],
                                  cx.wrp[:, dc * 8:(dc + 1) * 8],
                                  start=(dc == 0), stop=(dc == DCH - 1))
                cx.inc(mm, "pe")
            cx.run("tensor", pe_r)
            pr = cx.c["pe"]

            def dve_r(e, c=c, pr=pr):
                e.wait_ge(cx.s["pe"], pr)
                e.wait_ge(cx.s["dve"], cx.c["dve"])
                cx.inc(e.tensor_scalar(cx.t_r8[:, :], cx.PB[:, 0:8], cx.r_rstd[:, c:c + 1], None,
                                       OP.mult, OP.bypass), "dve")
            cx.run("vector", dve_r)
            cx.vstep(lambda e: e.tensor_tensor(cx.t_r8[:, :], cx.t_r8[:, :], cx.cb8[:, :], OP.add))
            cx.vstep(lambda e: e.reduce_max(cx.t_neg[:, :], cx.t_r8[:, :], axis=AX.X, negate=True))
            dr = cx.c["dve"]

            def act_r(e, dr=dr):
                e.wait_ge(cx.s["dve"], dr)
                cx.inc(e.activation(cx.t_re[:, :], cx.t_r8[:, :], AF.Exp,
                                    bias=cx.t_neg[:, :], accum_out=cx.t_asum[:, :]), "act")
            cx.run("scalar", act_r)
            ar = cx.c["act"]

            def dve_r2(e, ar=ar):
                e.wait_ge(cx.s["act"], ar)
                e.wait_ge(cx.s["dve"], cx.c["dve"])
                cx.inc(e.reciprocal(cx.t_ra[:, :], cx.t_asum[:, :]), "dve")
            cx.run("vector", dve_r2)
            cx.vstep(lambda e, c=c: e.tensor_scalar(cx.p0[:, c:c + 1], cx.t_re[:, 0:1],
                                                    cx.t_ra[:, :], None, OP.mult, OP.bypass))
            cx.vstep(lambda e, c=c: e.tensor_scalar(cx.q0[:, c:c + 1], cx.p0[:, c:c + 1],
                                                    -1.0, 1.0, OP.mult, OP.add))

    return cx.snap()


def tap(cx, bar, name, tile, tap_outs, b):
    if name not in tap_outs or b != 0:
        return

    def pool_t(e):
        cx.waits(e, bar)
        cx.inc(e.dma_start(tap_outs[name][:, :], tile), "gd", 16)
        e.wait_ge(cx.s["gd"], cx.c["gd"])
    cx.run("gpsimd", pool_t)



def emit_l0_body(cx, tens, tap_outs):
    nc = cx.nc
    (src, Wq, bq, Wk, bk, Wv, bv, Wo, bo, gamma_x, gamma_slot, semb, w1, w2,
     x1, xpark_d, xnT_park, slots_park, sepT_d, mask0_d) = tens

    mask0 = cx.sb("mask0_s", [128, 128], BF16)
    bq_t = cx.sb("bq_t", [128, 8], F32)
    bk_t = cx.sb("bk_t", [128, 8], F32)
    gx_t = cx.sb("gx_t", [128, 8], F32)
    gs_t = cx.sb("gs_t", [128, 8], F32)
    bvrow = cx.sb("bvrow", [1, D], BF16)
    borow = cx.sb("borow", [1, D], BF16)
    m1_all = cx.sb("m1_all", [128, 32], F32)
    a1_all = cx.sb("a1_all", [128, 32], F32)
    lhalf = cx.sb("lhalf", [128, 2048], F32)
    sew = cx.wbuf[:, 0:1024]
    pexp = cx.dchunk[:, 1024:1152]
    pnrm = cx.dchunk[:, 1152:1280]
    ptile = cx.dchunk[:, 1280:1408]
    vchunk = cx.dchunk[:, 0:1024]
    sen_t = cx.dchunk[:, 1024:2048]

    xnT = cx.BIG2[:, 0:16384]
    xnpT = cx.BIG2[:, 16384:32768]
    xnp = cx.M3
    dthalf = cx.BIG1
    slotsT = cx.BIG2[:, 0:16384]
    hTq = cx.M3
    P_lo = cx.BIG1[:, 0:16384]
    P_hi = cx.BIG1[:, 16384:32768]
    nhalf = cx.BIG2
    obuf = cx.M3
    qT = cx.M3
    kT = cx.BIG2[:, 16384:32768]
    attnT = cx.BIG1[:, 0:16384]
    w1col = cx.wbuf[:, 0:2048]
    w2row = cx.wbuf[:, 2048:4096]

    sehalf = cx.M3
    seh1 = cx.BIG2[:, 0:16384]
    nbuf = cx.wbuf[:, 0:4096]
    ntb = cx.wbuf[:, 4096:8192]

    def P_(ej):
        return (P_lo[:, ej * D:(ej + 1) * D] if ej < 16 else
                P_hi[:, (ej - 16) * D:(ej - 15) * D])

    def sync_pre(e):
        if cx.c["cc"] > 0:
            e.wait_ge(cx.s["cc"], cx.c["cc"])
        cx.inc(e.dma_start(mask0[:, :], mask0_d[:, :]), "ld", 16)
        emit_vec_cols(cx, e, bq, bq_t, 8)
        emit_vec_cols(cx, e, bk, bk_t, 8)
        emit_vec_cols(cx, e, gamma_x, gx_t, 8)
        emit_vec_cols(cx, e, gamma_slot, gs_t, 8)
    cx.run("sync", sync_pre)

    def pool_pre(e):
        cx.inc(e.dma_start(bvrow[:, :], bv.rearrange("(one d) -> one d", one=1)), "gd", 16)
        cx.inc(e.dma_start(borow[:, :], bo.rearrange("(one d) -> one d", one=1)), "gd", 16)
    cx.run("gpsimd", pool_pre)
    lp, gp = cx.c["ld"], cx.c["gd"]

    def dve_pre(e):
        e.wait_ge(cx.s["ld"], lp)
        e.wait_ge(cx.s["gd"], gp)
        cx.inc(e.tensor_tensor(gs_t[:, :], gs_t[:, :], gx_t[:, :], OP.mult), "dve")
    cx.run("vector", dve_pre)
    bar = cx.snap()

    serows = cx.BIG1
    for ec in range(32):
        def sync_se(e, ec=ec):
            if ec == 0:
                cx.waits(e, bar, ["dve"])
            else:
                e.wait_ge(cx.s["act"], cx.c["act"])
                e.wait_ge(cx.s["dve"], cx.c["dve"])
            cx.inc(e.dma_start(cx.xch[:, 0:D], semb[ec * 128:(ec + 1) * 128, :]), "ld", 16)
        cx.run("sync", sync_se)
        lse = cx.c["ld"]

        def act_se(e, lse=lse):
            e.wait_ge(cx.s["ld"], lse)
            if cx.c["dve"] > 0:
                e.wait_ge(cx.s["dve"], cx.c["dve"])
            cx.inc(e.activation(cx.t_dump, cx.xch[:, 0:D], AF.Square,
                                accum_out=cx.t_ssq[:, :]), "act")
        cx.run("scalar", act_se)
        cx.astep(lambda e: e.activation(cx.t_sd[:, :], cx.t_ssq[:, :], AF.Ln))
        cx.astep(lambda e: e.activation(cx.t_sd[:, :], cx.t_sd[:, :], AF.Exp, scale=0.5))
        ase = cx.c["act"]

        def dve_se(e, ase=ase):
            e.wait_ge(cx.s["act"], ase)
            e.wait_ge(cx.s["dve"], cx.c["dve"])
            cx.inc(e.tensor_scalar(cx.t_sd[:, :], cx.t_sd[:, :], 1e-12, None,
                                   OP.max, OP.bypass), "dve")
        cx.run("vector", dve_se)
        cx.vstep(lambda e: e.reciprocal(cx.t_ra[:, :], cx.t_sd[:, :]))
        cx.vstep(lambda e: e.tensor_scalar(cx.t_ra[:, :], cx.t_ra[:, :], 32.0, None,
                                           OP.mult, OP.bypass))
        cx.vstep(lambda e: e.tensor_scalar(sen_t[:, :], cx.xch[:, 0:D], cx.t_ra[:, :], None,
                                           OP.mult, OP.bypass))
        dse = cx.c["dve"]

        for dc in range(DCH):
            pt = cx.PTa if dc % 2 == 0 else cx.PTb

            def pe_se(e, dc=dc, pt=pt, dse=dse):
                if dc == 0:
                    e.wait_ge(cx.s["dve"], dse)
                if dc >= 2:
                    e.wait_ge(cx.s["dve"], cx.c["dve"] - 1)
                cx.inc(e.transpose(pt[:, 0:128], sen_t[:, dc * 128:(dc + 1) * 128],
                                   cx.ident16[:, :]), "pe")
            cx.run("tensor", pe_se)
            pse = cx.c["pe"]

            def dve_se2(e, ec=ec, dc=dc, pt=pt, pse=pse):
                e.wait_ge(cx.s["pe"], pse)
                cx.inc(e.tensor_scalar(serows[:, dc * ET + ec * 128: dc * ET + ec * 128 + 128],
                                       pt[:, 0:128], gs_t[:, dc:dc + 1], None,
                                       OP.mult, OP.bypass), "dve")
            cx.run("vector", dve_se2)
    dse_all = cx.c["dve"]

    def pool_se(e):
        e.wait_ge(cx.s["dve"], dse_all)
        for dc in range(DCH):
            cx.inc(e.dma_start(sepT_d[dc * 128:(dc + 1) * 128, :],
                               serows[:, dc * ET:(dc + 1) * ET]), "gd", 16)
        e.wait_ge(cx.s["gd"], cx.c["gd"])
    cx.run("gpsimd", pool_se)
    if "sepT" in tap_outs:
        def pool_tsep(e):
            e.wait_ge(cx.s["gd"], cx.c["gd"])
            cx.inc(e.dma_start(tap_outs["sepT"][:, :], sepT_d[:, :]), "gd", 16)
            e.wait_ge(cx.s["gd"], cx.c["gd"])
        cx.run("gpsimd", pool_tsep)
    bar = cx.snap()

    for b in range(BPC):
        xhome = src[b * S:(b + 1) * S, :]
        bar = emit_ln(cx, bar, xhome, xnT, xnp, xnpT, want_tau=True)
        tap(cx, bar, "xnT", xnT[:, :], tap_outs, b)
        tap(cx, bar, "xnp", xnp[:, :], tap_outs, b)

        def pool_pk(e):
            cx.waits(e, bar)
            cx.inc(e.dma_start(cr_ap(xnT_park, S), s3(xnT[:, :], S)), "gd", 16)
        cx.run("gpsimd", pool_pk)
        bar = cx.snap()

        for half in range(2):
            for eci in range(16):
                ec = half * 16 + eci

                def sync_sew(e, ec=ec, eci=eci, half=half):
                    if eci == 0 and half == 0:
                        cx.waits(e, bar)
                    else:
                        e.wait_ge(cx.s["pe"], cx.c["pe"])
                    cx.inc(e.dma_start(s3(sew[:, :], 128), colblock_ap(sepT_d, ec * 128, 128)), "ld", 16)
                cx.run("sync", sync_sew)
                lsew = cx.c["ld"]

                def pe_d(e, lsew=lsew):
                    e.wait_ge(cx.s["ld"], lsew)
                    e.wait_ge(cx.s["act"], cx.c["act"])
                    for q4 in range(4):
                        for dc in range(DCH):
                            mm = e.matmul(cx.PA[:, q4 * 512:(q4 + 1) * 512],
                                          sew[:, dc * 128:(dc + 1) * 128],
                                          xnpT[:, dc * S + q4 * 512: dc * S + (q4 + 1) * 512],
                                          start=(dc == 0), stop=(dc == DCH - 1))
                    cx.inc(mm, "pe")
                cx.run("tensor", pe_d)
                pd_ = cx.c["pe"]

                def dve_d(e, pd_=pd_):
                    e.wait_ge(cx.s["pe"], pd_)
                    cx.inc(e.reduce_max(cx.t_neg[:, :], cx.PA[:, :], axis=AX.X, negate=True), "dve")
                cx.run("vector", dve_d)
                dd_ = cx.c["dve"]

                def act_d(e, ec=ec, dd_=dd_):
                    e.wait_ge(cx.s["dve"], dd_)
                    e.wait_ge(cx.s["gd"], cx.c["gd"])
                    cx.inc(e.activation(cx.dchunk[:, :], cx.PA[:, :], AF.Exp,
                                        bias=cx.t_neg[:, :], accum_out=a1_all[:, ec:ec + 1]), "act")
                cx.run("scalar", act_d)
                ad_ = cx.c["act"]

                def dve_d2(e, ec=ec, ad_=ad_):
                    e.wait_ge(cx.s["act"], ad_)
                    e.wait_ge(cx.s["dve"], cx.c["dve"])
                    e.tensor_scalar(m1_all[:, ec:ec + 1], cx.t_neg[:, :], -1.0, None,
                                    OP.mult, OP.bypass)
                    cx.inc(e.reciprocal(cx.t_ra[:, :], a1_all[:, ec:ec + 1]), "dve")
                cx.run("vector", dve_d2)
                cx.vstep(lambda e: e.tensor_scalar(cx.dchunk[:, :], cx.dchunk[:, :],
                                                   cx.t_ra[:, :], None, OP.mult, OP.bypass))
                dd2 = cx.c["dve"]

                for k in range(SCH):
                    pt = cx.PTa if k % 2 == 0 else cx.PTb

                    def pe_dt(e, k=k, pt=pt, dd2=dd2):
                        if k == 0:
                            e.wait_ge(cx.s["dve"], dd2)
                        if k >= 2:
                            e.wait_ge(cx.s["dve"], cx.c["dve"] - 1)
                        cx.inc(e.transpose(pt[:, 0:128], cx.dchunk[:, k * 128:(k + 1) * 128],
                                           cx.ident16[:, :]), "pe")
                    cx.run("tensor", pe_dt)
                    pdt = cx.c["pe"]

                    def dve_dt(e, k=k, eci=eci, pt=pt, pdt=pdt):
                        e.wait_ge(cx.s["pe"], pdt)
                        cx.inc(e.tensor_copy(
                            dthalf[:, k * S + eci * 128: k * S + eci * 128 + 128],
                            pt[:, 0:128]), "dve")
                    cx.run("vector", dve_dt)
            bar = cx.snap()

            for dc in range(DCH):
                def pe_u(e, dc=dc):
                    if dc == 0:
                        cx.waits(e, bar, ["dve", "act", "gd"])
                    else:
                        e.wait_ge(cx.s["dve"], cx.c["dve"])
                    for k in range(SCH):
                        for et4 in range(4):
                            mm = e.matmul(cx.PA[:, et4 * 512:(et4 + 1) * 512],
                                          xnp[:, k * D + dc * 128: k * D + dc * 128 + 128],
                                          dthalf[:, k * S + et4 * 512: k * S + (et4 + 1) * 512],
                                          start=(k == 0), stop=(k == SCH - 1))
                    cx.inc(mm, "pe")
                cx.run("tensor", pe_u)
                pu = cx.c["pe"]

                def dve_u(e, dc=dc, pu=pu):
                    e.wait_ge(cx.s["pe"], pu)
                    cx.inc(e.tensor_scalar(slotsT[:, dc * S: dc * S + S], cx.PA[:, :],
                                           gx_t[:, dc:dc + 1], None, OP.mult, OP.bypass), "dve")
                cx.run("vector", dve_u)
            du = cx.c["dve"]

            def pool_su(e, half=half, du=du):
                e.wait_ge(cx.s["dve"], du)
                cx.inc(e.dma_start(cr_ap(slots_park[half], S), s3(slotsT[:, :], S)), "gd", 16)
            cx.run("gpsimd", pool_su)
            bar = cx.snap()
            if half == 0:
                tap(cx, bar, "slotsT", slotsT[:, :], tap_outs, b)
        tap(cx, bar, "m1", m1_all[:, :], tap_outs, b)
        tap(cx, bar, "A1", a1_all[:, :], tap_outs, b)

        for half in range(2):
            def sync_sl(e, half=half):
                cx.waits(e, bar)
                e.wait_ge(cx.s["pe"], cx.c["pe"])
                cx.inc(e.dma_start(s3(slotsT[:, :], S), cr_ap(slots_park[half], S)), "ld", 16)
            cx.run("sync", sync_sl)
            lsl = cx.c["ld"]

            for qtr in range(4):
                for fc in range(FCH):
                    wslot = (fc % 2) * 1024

                    def pool_w1(e, fc=fc, qtr=qtr, half=half, wslot=wslot, lsl=lsl):
                        if fc >= 2:
                            e.wait_ge(cx.s["pe"], cx.c["pe"] - 1)
                        cx.inc(e.dma_start(s3(w1col[:, wslot:wslot + 1024], 128),
                                           colblock_ap(w1, fc * 128, 128)), "gd", 16)
                    cx.run("gpsimd", pool_w1)
                    gw1 = cx.c["gd"]
                    hps = cx.PB[:, (fc % 2) * 512:(fc % 2) * 512 + 512]

                    def pe_h(e, fc=fc, qtr=qtr, wslot=wslot, gw1=gw1, hps=hps, lsl=lsl):
                        e.wait_ge(cx.s["gd"], gw1)
                        if fc == 0:
                            e.wait_ge(cx.s["ld"], lsl)
                        if fc >= 2 and cx.c["act"] >= 1:
                            e.wait_ge(cx.s["act"], cx.c["act"] - 1)
                        for dc in range(DCH):
                            mm = e.matmul(hps,
                                          w1col[:, wslot + dc * 128: wslot + dc * 128 + 128],
                                          slotsT[:, dc * S + qtr * 512: dc * S + (qtr + 1) * 512],
                                          start=(dc == 0), stop=(dc == DCH - 1))
                        cx.inc(mm, "pe")
                    cx.run("tensor", pe_h)
                    ph = cx.c["pe"]

                    def act_h(e, fc=fc, ph=ph, hps=hps):
                        e.wait_ge(cx.s["pe"], ph)
                        cx.inc(e.activation(hTq[:, fc * 512:(fc + 1) * 512], hps, AF.Gelu), "act")
                    cx.run("scalar", act_h)
                ah = cx.c["act"]

                for pair in range(2):
                    for fc in range(FCH):
                        wslot = (fc % 2) * 1024

                        def pool_w2(e, fc=fc, wslot=wslot):
                            if fc >= 2:
                                e.wait_ge(cx.s["pe"], cx.c["pe"] - 1)
                            cx.inc(e.dma_start(w2row[:, wslot:wslot + 1024],
                                               w2[fc * 128:(fc + 1) * 128, :]), "gd", 16)
                        cx.run("gpsimd", pool_w2)
                        gw2 = cx.c["gd"]

                        def pe_p(e, fc=fc, pair=pair, gw2=gw2, wslot=wslot, ah=ah):
                            e.wait_ge(cx.s["gd"], gw2)
                            if fc == 0:
                                e.wait_ge(cx.s["act"], ah)
                                e.wait_ge(cx.s["dve"], cx.c["dve"])
                            for e1 in (pair * 2, pair * 2 + 1):
                                off = (e1 % 2) * 1024
                                for hh in range(2):
                                    mm = e.matmul(
                                        cx.PA[:, off + hh * 512: off + (hh + 1) * 512],
                                        hTq[:, fc * 512 + e1 * 128: fc * 512 + e1 * 128 + 128],
                                        w2row[:, wslot + hh * 512: wslot + (hh + 1) * 512],
                                        start=(fc == 0), stop=(fc == FCH - 1))
                            cx.inc(mm, "pe")
                        cx.run("tensor", pe_p)
                    pp = cx.c["pe"]

                    def dve_p(e, qtr=qtr, half=half, pair=pair, pp=pp):
                        e.wait_ge(cx.s["pe"], pp)
                        for e1 in (pair * 2, pair * 2 + 1):
                            off = (e1 % 2) * 1024
                            ej = half * 16 + qtr * 4 + e1
                            ii = e.tensor_copy(P_(ej), cx.PA[:, off:off + 1024])
                        cx.inc(ii, "dve")
                    cx.run("vector", dve_p)
            bar = cx.snap()

        def sync_se0(e):
            cx.waits(e, bar)
            cx.inc(e.dma_start(s3(sehalf[:, :], S), colblock_ap(sepT_d, 0, S)), "ld", 16)
            cx.inc(e.dma_start(s3(seh1[:, :], S), colblock_ap(sepT_d, S, S)), "ld", 16)
        cx.run("sync", sync_se0)
        lse0 = cx.c["ld"]
        for sc in range(SCH):
            xslot = (sc % 2) * D

            def sync_seh(e, sc=sc, xslot=xslot):
                if sc == 0:
                    e.wait_ge(cx.s["ld"], lse0)
                if sc >= 2:
                    e.wait_ge(cx.s["gd"], cx.c["gd"])
                cx.inc(e.dma_start(cx.xch[:, xslot:xslot + D],
                                   xhome[sc * 128:(sc + 1) * 128, :]), "ld", 16)
            cx.run("sync", sync_seh)
            lseh = cx.c["ld"]

            def pe_c0(e, sc=sc):
                if sc == 0:
                    e.wait_ge(cx.s["ld"], lse0)
                e.wait_ge(cx.s["act"], cx.c["act"])
                e.wait_ge(cx.s["dve"], cx.c["dve"])
                for dc in range(DCH):
                    for q4 in range(4):
                        mm = e.matmul(cx.PA[:, q4 * 512:(q4 + 1) * 512],
                                      xnpT[:, dc * S + sc * 128: dc * S + sc * 128 + 128],
                                      sehalf[:, dc * S + q4 * 512: dc * S + (q4 + 1) * 512],
                                      start=(dc == 0), stop=(dc == DCH - 1))
                cx.inc(mm, "pe")
            cx.run("tensor", pe_c0)
            pc0 = cx.c["pe"]

            def dve_c0(e, pc0=pc0):
                e.wait_ge(cx.s["pe"], pc0)
                e.reduce_max(cx.t_neg[:, :], cx.PA[:, :], axis=AX.X, negate=True)
                cx.inc(e.tensor_copy(lhalf[:, :], cx.PA[:, :]), "dve")
            cx.run("vector", dve_c0)
            dc0 = cx.c["dve"]

            def pe_c1(e, sc=sc, dc0=dc0, lseh=lseh):
                e.wait_ge(cx.s["dve"], dc0)
                e.wait_ge(cx.s["ld"], lseh)
                for dc in range(DCH):
                    for q4 in range(4):
                        mm = e.matmul(cx.PA[:, q4 * 512:(q4 + 1) * 512],
                                      xnpT[:, dc * S + sc * 128: dc * S + sc * 128 + 128],
                                      seh1[:, dc * S + q4 * 512: dc * S + (q4 + 1) * 512],
                                      start=(dc == 0), stop=(dc == DCH - 1))
                cx.inc(mm, "pe")
            cx.run("tensor", pe_c1)
            pc1 = cx.c["pe"]

            def dve_c1(e, pc1=pc1):
                e.wait_ge(cx.s["pe"], pc1)
                cx.inc(e.reduce_max(cx.t_asum[:, :], cx.PA[:, :], axis=AX.X, negate=True), "dve")
            cx.run("vector", dve_c1)
            cx.vstep(lambda e: e.tensor_tensor(cx.t_neg[:, :], cx.t_neg[:, :],
                                               cx.t_asum[:, :], OP.min))
            dm2 = cx.c["dve"]

            def act_e0(e, dm2=dm2):
                e.wait_ge(cx.s["dve"], dm2)
                e.activation(nbuf[:, 0:S], lhalf[:, :], AF.Exp,
                             bias=cx.t_neg[:, :], accum_out=cx.t_ra[:, :])
                cx.inc(e.activation(nbuf[:, S:2 * S], cx.PA[:, :], AF.Exp,
                                    bias=cx.t_neg[:, :], accum_out=cx.t_asum[:, :]), "act")
            cx.run("scalar", act_e0)
            ae0 = cx.c["act"]

            def dve_a2(e, ae0=ae0):
                e.wait_ge(cx.s["act"], ae0)
                e.wait_ge(cx.s["dve"], cx.c["dve"])
                cx.inc(e.tensor_tensor(cx.t_ra[:, :], cx.t_ra[:, :], cx.t_asum[:, :],
                                       OP.add), "dve")
            cx.run("vector", dve_a2)
            cx.vstep(lambda e: e.reciprocal(cx.t_ra[:, :], cx.t_ra[:, :]))
            da2 = cx.c["dve"]

            for ej in range(32):
                pt = cx.PTa if ej % 2 == 0 else cx.PTb

                def pe_nt(e, ej=ej, pt=pt, da2=da2):
                    if ej == 0:
                        e.wait_ge(cx.s["dve"], da2)
                    if ej >= 2:
                        e.wait_ge(cx.s["dve"], cx.c["dve"] - 1)
                    cx.inc(e.transpose(pt[:, 0:128], nbuf[:, ej * 128:(ej + 1) * 128],
                                       cx.ident16[:, :]), "pe")
                cx.run("tensor", pe_nt)
                pnt = cx.c["pe"]

                def dve_nt(e, ej=ej, pt=pt, pnt=pnt):
                    e.wait_ge(cx.s["pe"], pnt)
                    cx.inc(e.tensor_copy(ntb[:, ej * 128:(ej + 1) * 128], pt[:, 0:128]), "dve")
                cx.run("vector", dve_nt)
            dnt = cx.c["dve"]

            def pe_out(e, dnt=dnt):
                e.wait_ge(cx.s["dve"], dnt)
                for ej in range(32):
                    for hh in range(2):
                        mm = e.matmul(cx.PB[:, hh * 512:(hh + 1) * 512],
                                      ntb[:, ej * 128:(ej + 1) * 128],
                                      P_(ej)[:, hh * 512:(hh + 1) * 512],
                                      start=(ej == 0), stop=(ej == 31))
                cx.inc(mm, "pe")
            cx.run("tensor", pe_out)
            pout = cx.c["pe"]

            def dve_out(e, sc=sc, xslot=xslot, pout=pout, lseh=lseh):
                e.wait_ge(cx.s["pe"], pout)
                e.wait_ge(cx.s["ld"], lseh)
                e.wait_ge(cx.s["dve"], cx.c["dve"])
                cx.inc(e.scalar_tensor_tensor(cx.xch[:, xslot:xslot + D], cx.PB[:, :],
                                              cx.t_ra[:, :], cx.xch[:, xslot:xslot + D],
                                              OP.mult, OP.add), "dve")
            cx.run("vector", dve_out)
            dxo = cx.c["dve"]

            def pool_xo2(e, sc=sc, xslot=xslot, dxo=dxo):
                e.wait_ge(cx.s["dve"], dxo)
                cx.inc(e.dma_start(xpark_d[sc * 128:(sc + 1) * 128, :],
                                   cx.xch[:, xslot:xslot + D]), "gd", 16)
            cx.run("gpsimd", pool_xo2)
        bar = cx.snap()

        def sync_xnt(e):
            cx.waits(e, bar)
            cx.inc(e.dma_start(s3(xnT[:, :], S), cr_ap(xnT_park, S)), "ld", 16)
        cx.run("sync", sync_xnt)
        lxnt = cx.c["ld"]

        for wsrc, bcol, dst, scaled in ((Wq, bq_t, qT, True), (Wk, bk_t, kT, False)):
            def pool_w(e, wsrc=wsrc):
                e.wait_ge(cx.s["pe"], cx.c["pe"])
                e.wait_ge(cx.s["dve"], cx.c["dve"])
                cx.inc(e.dma_start(s3(cx.wbuf[:, :], D),
                                   wsrc.rearrange("(dc p) o -> p dc o", p=128)), "gd", 16)
            cx.run("gpsimd", pool_w)
            gw = cx.c["gd"]
            for h in range(DCH):
                def pe_q(e, h=h, gw=gw, lxnt=lxnt):
                    if h == 0:
                        e.wait_ge(cx.s["gd"], gw)
                        e.wait_ge(cx.s["ld"], lxnt)
                        e.wait_ge(cx.s["dve"], cx.c["dve"])
                    else:
                        e.wait_ge(cx.s["dve"], cx.c["dve"])
                    for q4 in range(4):
                        for dc in range(DCH):
                            mm = e.matmul(cx.PA[:, q4 * 512:(q4 + 1) * 512],
                                          cx.wbuf[:, dc * D + h * 128: dc * D + h * 128 + 128],
                                          xnT[:, dc * S + q4 * 512: dc * S + (q4 + 1) * 512],
                                          start=(dc == 0), stop=(dc == DCH - 1))
                    cx.inc(mm, "pe")
                cx.run("tensor", pe_q)
                pq = cx.c["pe"]

                def dve_q(e, h=h, pq=pq, dst=dst, bcol=bcol, scaled=scaled):
                    e.wait_ge(cx.s["pe"], pq)
                    if scaled:
                        ii = e.tensor_scalar(dst[:, h * S:(h + 1) * S], cx.PA[:, :],
                                             bcol[:, h:h + 1], SC0, OP.add, OP.mult)
                    else:
                        ii = e.tensor_scalar(dst[:, h * S:(h + 1) * S], cx.PA[:, :],
                                             bcol[:, h:h + 1], None, OP.add, OP.bypass)
                    cx.inc(ii, "dve")
                cx.run("vector", dve_q)
        bar = cx.snap()
        tap(cx, bar, "qT", qT[:, :], tap_outs, b)
        tap(cx, bar, "kT", kT[:, :], tap_outs, b)

        def pool_wv(e):
            cx.waits(e, bar, ["pe"])
            cx.inc(e.dma_start(s3(cx.wbuf[:, :], D), Wv.rearrange("(dc p) o -> p dc o", p=128)), "gd", 16)
        cx.run("gpsimd", pool_wv)
        gwv = cx.c["gd"]
        it = 0
        ptbank = {0: 0, 1: 0}
        avguard = [0, 0]
        sguard = [0, 0, 0, 0]
        for np_ in range(SCH):
            def pe_v(e, np_=np_, gwv=gwv):
                if np_ == 0:
                    e.wait_ge(cx.s["gd"], gwv)
                    cx.waits(e, bar, ["dve", "act"])
                else:
                    e.wait_ge(cx.s["dve"], cx.c["dve"])
                for hh in range(2):
                    sl = cx.PB[:, hh * 512:(hh + 1) * 512]
                    e.matmul(sl, cx.ones16[:, :], bvrow[:, hh * 512:(hh + 1) * 512],
                             start=True, stop=False)
                    for dc in range(DCH):
                        mm = e.matmul(sl, xnT[:, dc * S + np_ * 128: dc * S + np_ * 128 + 128],
                                      cx.wbuf[:, dc * D + hh * 512: dc * D + (hh + 1) * 512],
                                      start=False, stop=(dc == DCH - 1))
                cx.inc(mm, "pe")
            cx.run("tensor", pe_v)
            pv_ = cx.c["pe"]

            def dve_v(e, pv_=pv_):
                e.wait_ge(cx.s["pe"], pv_)
                cx.inc(e.tensor_copy(vchunk[:, :], cx.PB[:, :]), "dve")
            cx.run("vector", dve_v)

            for h in range(DCH):
                sps = cx.PA[:, (it % 4) * 512:(it % 4) * 512 + 128]

                def pe_s(e, h=h, np_=np_, sps=sps, it=it, sg=sguard[0]):
                    if sg:
                        e.wait_ge(cx.s["act"], sg)
                    e.matmul(sps, cx.ident16[:, :], mask0[:, :], start=True, stop=False)
                    cx.inc(e.matmul(sps, qT[:, h * S + np_ * 128: h * S + np_ * 128 + 128],
                                    kT[:, h * S + np_ * 128: h * S + np_ * 128 + 128],
                                    start=False, stop=True), "pe")
                cx.run("tensor", pe_s)
                ps_ = cx.c["pe"]

                def dve_s(e, ps_=ps_, sps=sps, pv=cx.c["act"]):
                    e.wait_ge(cx.s["pe"], ps_)
                    if pv:
                        e.wait_ge(cx.s["act"], pv)
                    cx.inc(e.reduce_max(cx.t_neg[:, :], sps, axis=AX.X, negate=True), "dve")
                cx.run("vector", dve_s)
                ds_ = cx.c["dve"]

                def act_s(e, ds_=ds_, sps=sps):
                    e.wait_ge(cx.s["dve"], ds_)
                    cx.inc(e.activation(pexp[:, :], sps, AF.Exp,
                                        bias=cx.t_neg[:, :], accum_out=cx.t_asum[:, :]), "act")
                cx.run("scalar", act_s)
                as_ = cx.c["act"]
                sguard.pop(0)
                sguard.append(as_)

                def dve_s2(e, as_=as_):
                    e.wait_ge(cx.s["act"], as_)
                    e.wait_ge(cx.s["dve"], cx.c["dve"])
                    cx.inc(e.reciprocal(cx.t_ra[:, :], cx.t_asum[:, :]), "dve")
                cx.run("vector", dve_s2)
                cx.vstep(lambda e, pv=cx.c["pe"]: (e.wait_ge(cx.s["pe"], pv),
                         e.tensor_scalar(pnrm[:, :], pexp[:, :], cx.t_ra[:, :], None,
                                         OP.mult, OP.bypass))[1])
                ds2 = cx.c["dve"]
                pt = cx.PTa if it % 2 == 0 else cx.PTb

                def pe_t(e, ds2=ds2, pt=pt, ptg=ptbank[it % 2]):
                    e.wait_ge(cx.s["dve"], ds2)
                    if ptg:
                        e.wait_ge(cx.s["dve"], ptg)
                    cx.inc(e.transpose(pt[:, 0:128], pnrm[:, :], cx.ident16[:, :]), "pe")
                cx.run("tensor", pe_t)
                pt_ = cx.c["pe"]

                def dve_t(e, pt_=pt_, pt=pt):
                    e.wait_ge(cx.s["pe"], pt_)
                    cx.inc(e.tensor_copy(ptile[:, :], pt[:, 0:128]), "dve")
                cx.run("vector", dve_t)
                ptbank[it % 2] = cx.c["dve"]
                at_ = cx.c["dve"]
                pvs = cx.PB[:, (it % 2) * 512:(it % 2) * 512 + 128]

                def pe_pv(e, h=h, at_=at_, pvs=pvs, avg=avguard[0]):
                    e.wait_ge(cx.s["dve"], at_)
                    if avg:
                        e.wait_ge(cx.s["dve"], avg)
                    cx.inc(e.matmul(pvs, vchunk[:, h * 128:(h + 1) * 128],
                                    ptile[:, :], start=True, stop=True), "pe")
                cx.run("tensor", pe_pv)
                ppv = cx.c["pe"]

                def dve_pv(e, h=h, np_=np_, ppv=ppv, pvs=pvs):
                    e.wait_ge(cx.s["pe"], ppv)
                    cx.inc(e.tensor_copy(attnT[:, h * S + np_ * 128: h * S + np_ * 128 + 128],
                                         pvs), "dve")
                cx.run("vector", dve_pv)
                avguard[0] = avguard[1]
                avguard[1] = cx.c["dve"]
                it += 1
        bar = cx.snap()
        tap(cx, bar, "attnT", attnT[:, :], tap_outs, b)

        def pool_wo(e):
            cx.waits(e, bar, ["pe"])
            cx.inc(e.dma_start(s3(cx.wbuf[:, :], D), Wo.rearrange("(dc p) o -> p dc o", p=128)), "gd", 16)
        cx.run("gpsimd", pool_wo)
        gwo = cx.c["gd"]
        for c in range(SCH):
            slot = (c % 2) * D

            def sync_xo(e, c=c, slot=slot):
                if c == 0:
                    cx.waits(e, bar, ["dve", "act"])
                if c >= 2:
                    e.wait_ge(cx.s["gd"], cx.c["gd"])
                cx.inc(e.dma_start(cx.xch[:, slot:slot + D], xpark_d[c * 128:(c + 1) * 128, :]),
                       "ld", 16)
            cx.run("sync", sync_xo)
            lxo = cx.c["ld"]

            def pe_o(e, c=c, gwo=gwo):
                if c == 0:
                    e.wait_ge(cx.s["gd"], gwo)
                    cx.waits(e, bar, ["dve", "act"])
                else:
                    e.wait_ge(cx.s["dve"], cx.c["dve"])
                for hh in range(2):
                    sl = cx.PB[:, hh * 512:(hh + 1) * 512]
                    e.matmul(sl, cx.ones16[:, :], borow[:, hh * 512:(hh + 1) * 512],
                             start=True, stop=False)
                    for dc in range(DCH):
                        mm = e.matmul(sl, attnT[:, dc * S + c * 128: dc * S + c * 128 + 128],
                                      cx.wbuf[:, dc * D + hh * 512: dc * D + (hh + 1) * 512],
                                      start=False, stop=(dc == DCH - 1))
                cx.inc(mm, "pe")
            cx.run("tensor", pe_o)
            po = cx.c["pe"]

            def dve_o(e, c=c, slot=slot, po=po, lxo=lxo):
                e.wait_ge(cx.s["pe"], po)
                e.wait_ge(cx.s["ld"], lxo)
                cx.inc(e.tensor_tensor(cx.xch[:, slot:slot + D], cx.xch[:, slot:slot + D],
                                       cx.PB[:, :], OP.add), "dve")
            cx.run("vector", dve_o)
            do_ = cx.c["dve"]

            def pool_xo(e, c=c, slot=slot, do_=do_, b=b):
                e.wait_ge(cx.s["dve"], do_)
                cx.inc(e.dma_start(x1[b * S + c * 128: b * S + (c + 1) * 128, :],
                                   cx.xch[:, slot:slot + D]), "gd", 16)
            cx.run("gpsimd", pool_xo)
        bar = cx.snap()

    def pool_end(e):
        e.wait_ge(cx.s["gd"], cx.c["gd"])
        e.wait_ge(cx.s["ld"], cx.c["ld"])
    cx.run("gpsimd", pool_end)


TAP_SHAPES_L0 = {
    "xnT": [128, DCH * S], "xnp": [128, SCH * D], "attnT": [128, DCH * S],
    "qT": [128, DCH * S], "kT": [128, DCH * S], "slotsT": [128, DCH * S],
    "m1": [128, 32], "A1": [128, 32], "P_": [128, 32768],
    "G": [128, 32], "ra2": [1, S], "sepT": [D, ET],
}


def build_l0(taps=()):
    nc = bass.Bass("TRN2", target_bir_lowering=False)
    src = nc.dram_tensor("src", [BPC * S, D], F32, kind="ExternalInput")
    names = dict(Wq=[D, D], bq=[D], Wk=[D, D], bk=[D], Wv=[D, D], bv=[D],
                 Wo=[D, D], bo=[D], gamma_x=[D], gamma_slot=[D],
                 semb=[ET, D], w1=[D, FF], w2=[FF, D])
    tt = {k: nc.dram_tensor(k, v, F32, kind="ExternalInput") for k, v in names.items()}
    ident16_d = nc.dram_tensor("ident16", [128, 128], BF16, kind="ExternalInput")
    identf_d = nc.dram_tensor("identf", [128, 128], F32, kind="ExternalInput")
    mask0_d = nc.dram_tensor("mask0", [128, 128], BF16, kind="ExternalInput")
    x1 = nc.dram_tensor("x1", [BPC * S, D], F32, kind="ExternalOutput")
    xpark_d = nc.dram_tensor("xpark", [S, D], F32)
    xnT_park = nc.dram_tensor("xnT_park", [D, S], BF16)
    slots_park = [nc.dram_tensor(f"slots_park{i}", [D, S], BF16) for i in range(2)]
    sepT_d = nc.dram_tensor("sepT", [D, ET], BF16)
    tap_outs = {t: nc.dram_tensor("tap_" + t, TAP_SHAPES_L0[t], F32, kind="ExternalOutput")
                for t in taps}
    with ExitStack() as st:
        block = st.enter_context(nc.Block())
        cx = Cx(nc, block, st)
        alloc_common(cx)
        emit_load_consts(cx, ident16_d, identf_d)
        emit_l0_body(cx, (src, tt["Wq"], tt["bq"], tt["Wk"], tt["bk"], tt["Wv"],
                          tt["bv"], tt["Wo"], tt["bo"], tt["gamma_x"],
                          tt["gamma_slot"], tt["semb"], tt["w1"], tt["w2"],
                          x1, xpark_d, xnT_park, slots_park, sepT_d,
                          mask0_d), tap_outs)
    return nc



def emit_l1_body(cx, tens, tap_outs, thr8_tile=None):
    nc = cx.nc
    (xin, Wq, Wk, Wv, Wf, bf, W1, b1, W2, b2, ln_g, ln_b, Wr, br, thr8_d,
     y_out, xpark_d, btmp_d, w1p_d, w2p_d, mask1_d) = tens

    mask1 = cx.sb("mask1_s", [128, 3 * 384], BF16)
    bfrow = cx.sb("bfrow", [1, D], BF16)
    b1t = cx.sb("b1t", [128, 32], F32)
    b1pt = cx.sb("b1pt", [128, 32], F32)
    b2row = cx.sb("b2row", [1, D], BF16)
    b2prow = cx.sb("b2prow", [1, D], BF16)
    babsr = cx.xch[0:1, 0:D]
    bc1r = cx.xch[0:1, D:2 * D]
    lng_t = cx.sb("lng_t", [128, 8], F32)
    lnb_t = cx.sb("lnb_t", [128, 8], F32)
    cx.wrp = cx.sb("wrp", [128, 64], BF16)
    wrf = cx.sb("wrf", [128, 64], F32)
    wrg = cx.sb("wrg", [128, 64], F32)
    cx.gpb = cx.sb("gpb", [128, 8], F32)
    cx.cb8 = cx.sb("cb8", [128, 8], F32)
    cx.r_mean = cx.sb("r_mean", [128, 16], F32)
    cx.r_rstd = cx.sb("r_rstd", [128, 16], F32)
    cx.p0 = cx.sb("p0", [128, 16], F32)
    cx.q0 = cx.sb("q0", [128, 16], F32)
    cx.t_rm = cx.sb("t_rm", [128, 1], F32)
    cx.t_r8 = cx.sb("t_r8", [128, 8], F32)
    cx.t_re = cx.sb("t_re", [128, 8], BF16)
    thr8s = thr8_tile if thr8_tile is not None else cx.sb("thr8s", [1, 8], F32)
    thrb = cx.sb("thrb", [128, 8], F32)
    gprow = cx.sb("gprow", [1, 8], F32)
    c8row = cx.sb("c8row", [1, 8], F32)
    brrow = cx.sb("brrow", [1, 8], F32)
    onescol = cx.sb("onescol", [128, 1], F32)
    babs = cx.sb("babs", [128, 32], F32)
    bc1 = cx.sb("bc1", [128, 32], F32)
    bc2 = cx.sb("bc2", [128, 32], F32)
    pexp1 = cx.dchunk[:, 0:384]
    pnrm1 = cx.dchunk[:, 384:768]
    ptile1 = cx.dchunk[:, 768:1152]
    kdup = cx.BIG1[:, 18432:20736]
    v2 = cx.BIG1[:, 20736:21888]
    ahead = cx.BIG1[:, 21888:23936]
    gu = cx.dchunk[:, 0:512]
    gt1 = cx.dchunk[:, 512:1024]
    gt2 = cx.dchunk[:, 1024:1536]
    wkvT = cx.sb("wkvT", [128, 2 * 512], BF16)
    wrowf = cx.xch[:, 0:D]
    wabs = cx.xch[:, D:2 * D]
    wc1 = cx.sb("wc1", [128, D], F32)
    bc2r = wc1[0:1, :]
    wp16 = cx.dchunk[:, 0:D]

    xnT = cx.BIG2[:, 0:16384]
    qT2 = cx.M3
    attnT2 = cx.BIG2[:, 16384:32768]
    actT = cx.BIG2[:, 16384:32768]
    w2res = cx.BIG1
    wT = cx.BIG1[:, 0:16384]

    def sync_pre(e):
        if cx.c["cc"] > 0:
            e.wait_ge(cx.s["cc"], cx.c["cc"])
        cx.inc(e.dma_start(mask1[:, :], mask1_d[:, :]), "ld", 16)
        emit_vec_cols(cx, e, b1, b1t, 32)
        emit_vec_cols(cx, e, ln_g, lng_t, 8)
        emit_vec_cols(cx, e, ln_b, lnb_t, 8)
        for c in range(8):
            cx.inc(e.dma_start(wrf[:, c * 8:(c + 1) * 8], Wr[c * 128:(c + 1) * 128, :]), "ld", 16)
        if thr8_d is not None:
            cx.inc(e.dma_start(thr8s[:, :], thr8_d[:, :]), "ld", 16)
        cx.inc(e.dma_start(brrow[:, :], br.rearrange("(one d) -> one d", one=1)), "ld", 16)
    cx.run("sync", sync_pre)

    def pool_pre(e):
        if cx.c["cc"] > 0:
            e.wait_ge(cx.s["cc"], cx.c["cc"])
        cx.inc(e.dma_start(bfrow[:, :], bf.rearrange("(one d) -> one d", one=1)), "gd", 16)
        cx.inc(e.dma_start(b2row[:, :], b2.rearrange("(one d) -> one d", one=1)), "gd", 16)
    cx.run("gpsimd", pool_pre)
    lp, gp = cx.c["ld"], cx.c["gd"]

    def pe_thr(e):
        e.wait_ge(cx.s["ld"], lp)
        for j in range(8):
            mm = e.matmul(cx.PB[:, j:j + 1], cx.onesf[:, :], thr8s[:, j:j + 1],
                          start=True, stop=True)
        cx.inc(mm, "pe")
    cx.run("tensor", pe_thr)
    pthr = cx.c["pe"]

    def act_abs(e):
        e.wait_ge(cx.s["ld"], lp)
        e.wait_ge(cx.s["gd"], gp)
        e.activation(babs[:, :], b1t[:, :], AF.Abs)
        cx.inc(e.activation(babsr, b2row[:, :], AF.Abs), "act")
    cx.run("scalar", act_abs)
    aabs = cx.c["act"]

    def dve_pre(e):
        e.wait_ge(cx.s["pe"], pthr)
        e.wait_ge(cx.s["act"], aabs)
        if cx.c["dve"] > 0:
            e.wait_ge(cx.s["dve"], cx.c["dve"])
        e.memset(onescol[:, :], 1.0)
        e.tensor_copy(thrb[:, :], cx.PB[:, 0:8])
        for c in range(8):
            e.tensor_scalar(wrg[:, c * 8:(c + 1) * 8], wrf[:, c * 8:(c + 1) * 8],
                            lng_t[:, c:c + 1], None, OP.mult, OP.bypass)
            ii = e.tensor_scalar(wrf[:, c * 8:(c + 1) * 8], wrf[:, c * 8:(c + 1) * 8],
                                 lnb_t[:, c:c + 1], None, OP.mult, OP.bypass)
        cx.inc(ii, "dve")
    cx.run("vector", dve_pre)
    cx.vstep(lambda e: e.tensor_copy(cx.wrp[:, :], wrg[:, :]))

    def bt1(e):
        e.tensor_scalar(bc1[:, :], babs[:, :], thrb[:, 5:6], -0.8, OP.is_ge, OP.mult)
        e.tensor_scalar(bc2[:, :], babs[:, :], thrb[:, 4:5], 0.8, OP.is_le, OP.mult)
        e.tensor_scalar(bc1r, babsr, thr8s[0:1, 7:8], -0.8, OP.is_ge, OP.mult)
        return e.tensor_scalar(bc2r[:, :], babsr, thr8s[0:1, 6:7], 0.8, OP.is_le, OP.mult)
    cx.vstep(bt1)

    def bt2(e):
        e.tensor_tensor(bc1[:, :], bc1[:, :], bc2[:, :], OP.add)
        return e.tensor_tensor(bc1r, bc1r, bc2r[:, :], OP.add)
    cx.vstep(bt2)

    def bt3(e):
        e.tensor_scalar(bc1[:, :], bc1[:, :], 1.0, None, OP.add, OP.bypass)
        return e.tensor_scalar(bc1r, bc1r, 1.0, None, OP.add, OP.bypass)
    cx.vstep(bt3)

    def bt4(e):
        e.tensor_tensor(b1pt[:, :], b1t[:, :], bc1[:, :], OP.mult)
        return e.tensor_tensor(b2prow[:, :], b2row[:, :], bc1r, OP.mult)
    cx.vstep(bt4)
    dpre = cx.c["dve"]

    def pe_gs(e):
        e.wait_ge(cx.s["dve"], dpre)
        for c in range(8):
            e.matmul(cx.PB[0:1, 16:24], onescol[:, :], wrg[:, c * 8:(c + 1) * 8],
                     start=(c == 0), stop=(c == 7))
        for c in range(8):
            mm = e.matmul(cx.PB[0:1, 24:32], onescol[:, :], wrf[:, c * 8:(c + 1) * 8],
                          start=(c == 0), stop=(c == 7))
        cx.inc(mm, "pe")
    cx.run("tensor", pe_gs)
    pgs = cx.c["pe"]

    def dve_gs(e):
        e.wait_ge(cx.s["pe"], pgs)
        e.wait_ge(cx.s["dve"], cx.c["dve"])
        e.tensor_copy(gprow[:, :], cx.PB[0:1, 16:24])
        cx.inc(e.tensor_copy(c8row[:, :], cx.PB[0:1, 24:32]), "dve")
    cx.run("vector", dve_gs)
    cx.vstep(lambda e: e.tensor_tensor(c8row[:, :], c8row[:, :], brrow[:, :], OP.add))
    dgs = cx.c["dve"]

    def pe_gs2(e):
        e.wait_ge(cx.s["dve"], dgs)
        e.matmul(cx.PB[:, 16:24], cx.onesf[:, :], gprow[:, :], start=True, stop=True)
        cx.inc(e.matmul(cx.PB[:, 24:32], cx.onesf[:, :], c8row[:, :], start=True, stop=True), "pe")
    cx.run("tensor", pe_gs2)
    pgs2 = cx.c["pe"]

    def dve_gs2(e):
        e.wait_ge(cx.s["pe"], pgs2)
        e.tensor_copy(cx.gpb[:, :], cx.PB[:, 16:24])
        cx.inc(e.tensor_copy(cx.cb8[:, :], cx.PB[:, 24:32]), "dve")
    cx.run("vector", dve_gs2)
    bar = cx.snap()

    for (wsrc, wdst, nrows, width, jb, jt) in ((W1, w1p_d, DCH, FF, 0, 1),
                                               (W2, w2p_d, FCH, D, 2, 3)):
        for rc in range(nrows):
            for cs in range(width // D):
                def sync_wp(e, wsrc=wsrc, rc=rc, cs=cs):
                    if cx.c["gd"] > 0:
                        e.wait_ge(cx.s["gd"], cx.c["gd"])
                    cx.inc(e.dma_start(wrowf,
                                       wsrc[rc * 128:(rc + 1) * 128, cs * D:(cs + 1) * D]),
                           "ld", 16)
                cx.run("sync", sync_wp)
                lwp = cx.c["ld"]

                def act_wp(e, lwp=lwp):
                    e.wait_ge(cx.s["ld"], lwp)
                    if cx.c["dve"] > 0:
                        e.wait_ge(cx.s["dve"], cx.c["dve"])
                    cx.inc(e.activation(wabs, wrowf, AF.Abs), "act")
                cx.run("scalar", act_wp)
                awp = cx.c["act"]

                def dve_wp(e, awp=awp, jb=jb, jt=jt):
                    e.wait_ge(cx.s["act"], awp)
                    e.wait_ge(cx.s["dve"], cx.c["dve"])
                    e.tensor_scalar(wc1[:, :], wabs, thrb[:, jt:jt + 1], -0.8,
                                    OP.is_ge, OP.mult)
                    cx.inc(e.tensor_scalar(wabs, wabs, thrb[:, jb:jb + 1], 0.8,
                                           OP.is_le, OP.mult), "dve")
                cx.run("vector", dve_wp)
                cx.vstep(lambda e: e.tensor_tensor(wc1[:, :], wc1[:, :], wabs, OP.add))
                cx.vstep(lambda e: e.tensor_scalar(wc1[:, :], wc1[:, :], 1.0, None,
                                                   OP.add, OP.bypass))
                cx.vstep(lambda e: e.tensor_tensor(wp16, wrowf, wc1[:, :], OP.mult))
                dwp = cx.c["dve"]

                def pool_wp(e, wdst=wdst, rc=rc, cs=cs, dwp=dwp):
                    e.wait_ge(cx.s["dve"], dwp)
                    cx.inc(e.dma_start(wdst[rc * 128:(rc + 1) * 128, cs * D:(cs + 1) * D],
                                       wp16), "gd", 16)
                cx.run("gpsimd", pool_wp)
    bar = cx.snap()

    for b in range(BPC):
        xhome = xin[b * S:(b + 1) * S, :]
        yhome = y_out[b * S:(b + 1) * S, :]
        bar = emit_ln(cx, bar, xhome, xnT, None, None, want_tau=False, router=True)
        tap(cx, bar, "xnT2", xnT[:, :], tap_outs, b)
        tap(cx, bar, "p0", cx.p0[:, :], tap_outs, b)

        import os as _os
        for ex in ([] if _os.environ.get("L1_SKIP_MOE") else range(2)):
            w1src = W1 if ex == 0 else w1p_d
            w2src = W2 if ex == 0 else w2p_d
            bsrc = b1t if ex == 0 else b1pt
            b2r = b2row if ex == 0 else b2prow
            pcol = cx.p0 if ex == 0 else cx.q0
            xsrc_home = xhome if ex == 0 else xpark_d
            xdst_home = xpark_d

            def pool_w2r(e, w2src=w2src, ex=ex):
                e.wait_ge(cx.s["pe"], cx.c["pe"])
                if ex == 0:
                    cx.waits(e, bar)
                else:
                    e.wait_ge(cx.s["gd"], cx.c["gd"])
                cx.inc(e.dma_start(s3(w2res[:, :], D),
                                   w2src.rearrange("(fc p) d -> p fc d", p=128)), "gd", 16)
            cx.run("gpsimd", pool_w2r)
            gw2r = cx.c["gd"]

            for sq in range(4):
                for fc in range(FCH):
                    wslot = (fc % 2) * 1024

                    def pool_w1e(e, fc=fc, wslot=wslot, w1src=w1src):
                        if fc >= 2:
                            e.wait_ge(cx.s["pe"], cx.c["pe"] - 1)
                        cx.inc(e.dma_start(s3(cx.wbuf[:, wslot:wslot + 1024], 128),
                                           colblock_ap(w1src, fc * 128, 128)), "gd", 16)
                    cx.run("gpsimd", pool_w1e)
                    gw1e = cx.c["gd"]
                    pps = cx.PA[:, (fc % 4) * 512:(fc % 4) * 512 + 512]

                    def pe_pre(e, fc=fc, sq=sq, gw1e=gw1e, pps=pps, wslot=wslot, gw2r=gw2r):
                        e.wait_ge(cx.s["gd"], gw1e)
                        if fc == 0:
                            e.wait_ge(cx.s["gd"], gw2r)
                            e.wait_ge(cx.s["dve"], cx.c["dve"])
                        elif fc >= 4:
                            e.wait_ge(cx.s["dve"], cx.c["dve"] - 4 * 3 + 2)
                        for dc in range(DCH):
                            mm = e.matmul(pps,
                                          cx.wbuf[:, wslot + dc * 128: wslot + dc * 128 + 128],
                                          xnT[:, dc * S + sq * 512: dc * S + (sq + 1) * 512],
                                          start=(dc == 0), stop=(dc == DCH - 1))
                        cx.inc(mm, "pe")
                    cx.run("tensor", pe_pre)
                    ppre = cx.c["pe"]

                    def dve_u(e, fc=fc, ppre=ppre, pps=pps, bsrc=bsrc):
                        e.wait_ge(cx.s["pe"], ppre)
                        cx.inc(e.tensor_scalar(gu, pps, bsrc[:, fc:fc + 1], None,
                                               OP.add, OP.bypass), "dve")
                    cx.run("vector", dve_u)
                    du_ = cx.c["dve"]

                    cx.vstep(lambda e: e.tensor_tensor(gt1, gu, gu, OP.mult))
                    cx.vstep(lambda e: e.tensor_scalar(gt2, gt1, GC2, GC1, OP.mult, OP.add))
                    cx.vstep(lambda e: e.tensor_tensor(gt2, gt2, gu, OP.mult))
                    dz = cx.c["dve"]

                    def act_sg(e, dz=dz):
                        e.wait_ge(cx.s["dve"], dz)
                        cx.inc(e.activation(gt1, gt2, AF.Sigmoid), "act")
                    cx.run("scalar", act_sg)
                    asg = cx.c["act"]

                    def dve_a(e, fc=fc, asg=asg):
                        e.wait_ge(cx.s["act"], asg)
                        cx.inc(e.tensor_tensor(actT[:, fc * 512:(fc + 1) * 512],
                                               gt1, gu, OP.mult), "dve")
                    cx.run("vector", dve_a)
                dact = cx.c["dve"]

                for ci in range(4):
                    sc = sq * 4 + ci
                    slot = (ci % 2) * D

                    def sync_xm(e, sc=sc, slot=slot, xsrc_home=xsrc_home):
                        if cx.c["gd"] > 0:
                            e.wait_ge(cx.s["gd"], cx.c["gd"])
                        cx.inc(e.dma_start(cx.xch[:, slot:slot + D],
                                           xsrc_home[sc * 128:(sc + 1) * 128, :]), "ld", 16)
                    cx.run("sync", sync_xm)
                    lxm = cx.c["ld"]

                    def pe_y(e, ci=ci, dact=dact, b2r=b2r):
                        if ci == 0:
                            e.wait_ge(cx.s["dve"], dact)
                        else:
                            e.wait_ge(cx.s["dve"], cx.c["dve"])
                        for hh in range(2):
                            sl = cx.PB[:, hh * 512:(hh + 1) * 512]
                            e.matmul(sl, cx.ones16[:, :], b2r[:, hh * 512:(hh + 1) * 512],
                                     start=True, stop=False)
                            for fc in range(FCH):
                                mm = e.matmul(sl,
                                              actT[:, fc * 512 + ci * 128: fc * 512 + ci * 128 + 128],
                                              w2res[:, fc * D + hh * 512: fc * D + (hh + 1) * 512],
                                              start=False, stop=(fc == FCH - 1))
                        cx.inc(mm, "pe")
                    cx.run("tensor", pe_y)
                    py = cx.c["pe"]

                    def dve_y(e, sc=sc, slot=slot, py=py, lxm=lxm, pcol=pcol):
                        e.wait_ge(cx.s["pe"], py)
                        e.wait_ge(cx.s["ld"], lxm)
                        cx.inc(e.scalar_tensor_tensor(cx.xch[:, slot:slot + D], cx.PB[:, :],
                                                      pcol[:, sc:sc + 1],
                                                      cx.xch[:, slot:slot + D],
                                                      OP.mult, OP.add), "dve")
                    cx.run("vector", dve_y)
                    dy = cx.c["dve"]

                    def pool_xm(e, sc=sc, slot=slot, dy=dy, xdst_home=xdst_home):
                        e.wait_ge(cx.s["dve"], dy)
                        cx.inc(e.dma_start(xdst_home[sc * 128:(sc + 1) * 128, :],
                                           cx.xch[:, slot:slot + D]), "gd", 16)
                    cx.run("gpsimd", pool_xm)
            bar = cx.snap()
        tap(cx, bar, "x_after_moe", None, tap_outs, b) if False else None

        import os as _os
        _skip_qprep = bool(_os.environ.get("L1_SKIP_QPREP"))
        if _os.environ.get("L1_SKIP_ATTN"):
            def pool_skip(e, b=b, yhome=yhome):
                cx.waits(e, bar)
                cx.inc(e.dma_start(yhome[:, :], xpark_d[:, :]), "gd", 16)
            cx.run("gpsimd", pool_skip)
            bar = cx.snap()
            continue
        def pool_wq(e):
            if _skip_qprep:
                cx.waits(e, bar, ["pe"])
                return
            cx.waits(e, bar, ["pe"])
            cx.inc(e.dma_start(s3(cx.wbuf[:, :], D), Wq.rearrange("(oc p) i -> p oc i", p=128)),
                   "gd", 16)
        cx.run("gpsimd", pool_wq)
        gwq = cx.c["gd"]
        for oc in ([] if _skip_qprep else range(DCH)):
            for ic in range(DCH):
                pt = cx.PTa if (oc * DCH + ic) % 2 == 0 else cx.PTb

                def pe_wt(e, oc=oc, ic=ic, pt=pt, gwq=gwq):
                    if oc == 0 and ic == 0:
                        e.wait_ge(cx.s["gd"], gwq)
                        cx.waits(e, bar, ["act", "dve"])
                    if cx.c["dve"] > 1:
                        e.wait_ge(cx.s["dve"], cx.c["dve"] - 1)
                    cx.inc(e.transpose(pt[:, 0:128],
                                       cx.wbuf[:, oc * D + ic * 128: oc * D + ic * 128 + 128],
                                       cx.ident16[:, :]), "pe")
                cx.run("tensor", pe_wt)
                pwt = cx.c["pe"]

                def dve_wt(e, oc=oc, ic=ic, pt=pt, pwt=pwt):
                    e.wait_ge(cx.s["pe"], pwt)
                    cx.inc(e.tensor_copy(wT[:, ic * D + oc * 128: ic * D + oc * 128 + 128],
                                         pt[:, 0:128]), "dve")
                cx.run("vector", dve_wt)
        bar2 = cx.snap()

        for h in ([] if _skip_qprep else range(16)):
            def pe_qh(e, h=h, bar2=bar2):
                if h == 0:
                    cx.waits(e, bar2, ["act", "dve", "gd"])
                else:
                    e.wait_ge(cx.s["dve"], cx.c["dve"])
                    e.wait_ge(cx.s["gd"], cx.c["gd"])
                for q4 in range(4):
                    for dc in range(DCH):
                        mm = e.matmul(cx.PA[0:64, q4 * 512:(q4 + 1) * 512],
                                      wT[:, dc * D + h * 64: dc * D + h * 64 + 64],
                                      xnT[:, dc * S + q4 * 512: dc * S + (q4 + 1) * 512],
                                      start=(dc == 0), stop=(dc == DCH - 1))
                cx.inc(mm, "pe")
            cx.run("tensor", pe_qh)
            pqh = cx.c["pe"]

            if h % 2 == 0:
                def dve_qh(e, h=h, pqh=pqh):
                    e.wait_ge(cx.s["pe"], pqh)
                    cx.inc(e.tensor_scalar(qT2[:, (h // 2) * S:(h // 2 + 1) * S][0:64, :],
                                           cx.PA[0:64, :], SC1, None, OP.mult, OP.bypass), "dve")
                cx.run("vector", dve_qh)
            else:
                def dve_qh(e, h=h, pqh=pqh):
                    e.wait_ge(cx.s["pe"], pqh)
                    cx.inc(e.tensor_scalar(ahead[0:64, :], cx.PA[0:64, :], SC1, None,
                                           OP.mult, OP.bypass), "dve")
                cx.run("vector", dve_qh)
                dqh = cx.c["dve"]

                def pool_qh(e, h=h, dqh=dqh):
                    e.wait_ge(cx.s["dve"], dqh)
                    cx.inc(e.dma_start(qT2[:, (h // 2) * S:(h // 2 + 1) * S][64:128, :],
                                       ahead[0:64, :]), "gd", 16)
                cx.run("gpsimd", pool_qh)
        bar = cx.snap()

        _skip_kv = bool(_os.environ.get("L1_SKIP_KV"))
        def pool_wkv(e):
            if _skip_kv:
                cx.waits(e, bar, ["pe"])
                return
            cx.waits(e, bar, ["pe"])
            cx.inc(e.dma_start(cx.wbuf[0:64, 0:1024], Wk[0:64, :]), "gd", 16)
            cx.inc(e.dma_start(cx.wbuf[0:64, 1024:2048], Wv[0:64, :]), "gd", 16)
        cx.run("gpsimd", pool_wkv)
        gwkv = cx.c["gd"]
        for j, base in ([] if _skip_kv else ((0, 0), (1, 1024))):
            for ic in range(DCH):
                pt = cx.PTa if ic % 2 == 0 else cx.PTb

                def pe_wk(e, j=j, base=base, ic=ic, pt=pt, gwkv=gwkv):
                    if j == 0 and ic == 0:
                        e.wait_ge(cx.s["gd"], gwkv)
                        e.wait_ge(cx.s["dve"], cx.c["dve"])
                    if cx.c["dve"] > 1:
                        e.wait_ge(cx.s["dve"], cx.c["dve"] - 1)
                    cx.inc(e.transpose(pt[:, 0:64],
                                       cx.wbuf[0:64, base + ic * 128: base + ic * 128 + 128],
                                       cx.ident16[0:64, 0:64]), "pe")
                cx.run("tensor", pe_wk)
                pwk = cx.c["pe"]

                def dve_wk(e, j=j, ic=ic, pt=pt, pwk=pwk):
                    e.wait_ge(cx.s["pe"], pwk)
                    cx.inc(e.tensor_copy(wkvT[:, j * 512 + ic * 64: j * 512 + (ic + 1) * 64],
                                         pt[:, 0:64]), "dve")
                cx.run("vector", dve_wk)
        awk = cx.c["dve"]

        def dve_kz(e, awk=awk):
            if _skip_kv:
                return
            e.wait_ge(cx.s["dve"], awk)
            e.memset(kdup[:, 0:256], 0.0)
            cx.inc(e.memset(v2[:, 0:128], 0.0), "dve")
        cx.run("vector", dve_kz)

        for q4 in ([] if _skip_kv else range(4)):
            def pe_kp(e, q4=q4):
                if q4 > 0:
                    e.wait_ge(cx.s["dve"], cx.c["dve"])
                for dc in range(DCH):
                    mm = e.matmul(cx.PA[0:64, 0:512],
                                  wkvT[:, dc * 64:(dc + 1) * 64],
                                  xnT[:, dc * S + q4 * 512: dc * S + (q4 + 1) * 512],
                                  start=(dc == 0), stop=(dc == DCH - 1))
                cx.inc(mm, "pe")
            cx.run("tensor", pe_kp)
            pkp = cx.c["pe"]

            def dve_kp(e, q4=q4, pkp=pkp):
                e.wait_ge(cx.s["pe"], pkp)
                cx.inc(e.tensor_copy(kdup[0:64, 256 + q4 * 512: 256 + (q4 + 1) * 512],
                                     cx.PA[0:64, 0:512]), "dve")
            cx.run("vector", dve_kp)
        dkp = cx.c["dve"]

        def pool_kd(e, dkp=dkp):
            if _skip_kv:
                return
            e.wait_ge(cx.s["dve"], dkp)
            cx.inc(e.dma_start(kdup[64:128, :], kdup[0:64, :]), "gd", 16)
        cx.run("gpsimd", pool_kd)

        for r in ([] if _skip_kv else range(SCH)):
            def pe_vp(e, r=r):
                e.wait_ge(cx.s["dve"], cx.c["dve"])
                for dc in range(DCH):
                    mm = e.matmul(cx.PA[:, 1024:1088],
                                  xnT[:, dc * S + r * 128: dc * S + r * 128 + 128],
                                  wkvT[:, 512 + dc * 64: 512 + (dc + 1) * 64],
                                  start=(dc == 0), stop=(dc == DCH - 1))
                cx.inc(mm, "pe")
            cx.run("tensor", pe_vp)
            pvp = cx.c["pe"]

            def dve_vp(e, r=r, pvp=pvp):
                e.wait_ge(cx.s["pe"], pvp)
                cx.inc(e.tensor_copy(v2[:, (r + 2) * 64:(r + 3) * 64], cx.PA[:, 1024:1088]), "dve")
            cx.run("vector", dve_vp)
        bar = cx.snap()
        tap(cx, bar, "kdup", kdup[:, :], tap_outs, b)
        tap(cx, bar, "qT2", qT2[:, :], tap_outs, b)

        import os as _os2
        _skip_blocks = bool(_os2.environ.get("L1_SKIP_BLOCKS"))
        it = 0
        ptbank = {0: 0, 1: 0}
        avguard = [0, 0]
        sguard = [0, 0, 0, 0]
        for h in ([] if _skip_blocks else range(16)):
            hb = (h % 2) * 64
            for qb in range(SCH):
                sps = cx.PA[:, (it % 4) * 512:(it % 4) * 512 + 384]

                msel = min(qb, 2) * 384

                def pe_s(e, h=h, hb=hb, qb=qb, sps=sps, it=it, msel=msel, sg=sguard[0]):
                    if it == 0:
                        cx.waits(e, bar, ["dve", "gd"])
                    elif sg:
                        e.wait_ge(cx.s["act"], sg)
                    e.matmul(sps, cx.ident16[:, :], mask1[:, msel:msel + 384], start=True, stop=False)
                    cx.inc(e.matmul(sps,
                                    qT2[:, (h // 2) * S + qb * 128:(h // 2) * S + qb * 128 + 128][hb:hb + 64, :],
                                    kdup[hb:hb + 64, qb * 128: qb * 128 + 384],
                                    start=False, stop=True), "pe")
                cx.run("tensor", pe_s)
                ps_ = cx.c["pe"]

                def dve_s(e, ps_=ps_, sps=sps, pv=cx.c["act"]):
                    e.wait_ge(cx.s["pe"], ps_)
                    if pv:
                        e.wait_ge(cx.s["act"], pv)
                    cx.inc(e.reduce_max(cx.t_neg[:, :], sps, axis=AX.X, negate=True), "dve")
                cx.run("vector", dve_s)
                ds_ = cx.c["dve"]

                def act_s(e, ds_=ds_, sps=sps):
                    e.wait_ge(cx.s["dve"], ds_)
                    cx.inc(e.activation(pexp1[:, :], sps, AF.Exp,
                                        bias=cx.t_neg[:, :], accum_out=cx.t_asum[:, :]), "act")
                cx.run("scalar", act_s)
                as_ = cx.c["act"]
                sguard.pop(0)
                sguard.append(as_)

                def dve_s2(e, as_=as_):
                    e.wait_ge(cx.s["act"], as_)
                    e.wait_ge(cx.s["dve"], cx.c["dve"])
                    cx.inc(e.reciprocal(cx.t_ra[:, :], cx.t_asum[:, :]), "dve")
                cx.run("vector", dve_s2)
                cx.vstep(lambda e, pv=cx.c["pe"]: (e.wait_ge(cx.s["pe"], pv),
                         e.tensor_scalar(pnrm1[:, :], pexp1[:, :], cx.t_ra[:, :], None,
                                         OP.mult, OP.bypass))[1])
                ds2 = cx.c["dve"]

                for t in range(3):
                    bankid = t % 2
                    pt = cx.PTa if bankid == 0 else cx.PTb

                    def pe_t(e, t=t, ds2=ds2, pt=pt, ptg=ptbank[bankid]):
                        if t == 0:
                            e.wait_ge(cx.s["dve"], ds2)
                        e.wait_ge(cx.s["dve"], cx.c["dve"])
                        cx.inc(e.transpose(pt[:, 0:128], pnrm1[:, t * 128:(t + 1) * 128],
                                           cx.ident16[:, :]), "pe")
                    cx.run("tensor", pe_t)
                    pt_ = cx.c["pe"]

                    def dve_t(e, t=t, pt_=pt_, pt=pt):
                        e.wait_ge(cx.s["pe"], pt_)
                        cx.inc(e.tensor_copy(ptile1[:, t * 128:(t + 1) * 128], pt[:, 0:128]),
                               "dve")
                    cx.run("vector", dve_t)
                    ptbank[bankid] = cx.c["dve"]
                at_ = cx.c["dve"]
                pvs = cx.PB[0:64, (it % 2) * 512:(it % 2) * 512 + 128]

                def pe_pv(e, qb=qb, at_=at_, pvs=pvs, avg=avguard[0]):
                    e.wait_ge(cx.s["dve"], at_)
                    e.wait_ge(cx.s["dve"], cx.c["dve"])
                    for t in range(3):
                        mm = e.matmul(pvs, v2[:, (qb + t) * 64:(qb + t + 1) * 64],
                                      ptile1[:, t * 128:(t + 1) * 128],
                                      start=(t == 0), stop=(t == 2))
                    cx.inc(mm, "pe")
                cx.run("tensor", pe_pv)
                ppv = cx.c["pe"]

                if h % 2 == 0:
                    def dve_pv(e, h=h, qb=qb, ppv=ppv, pvs=pvs):
                        e.wait_ge(cx.s["pe"], ppv)
                        cx.inc(e.tensor_copy(
                            attnT2[:, (h // 2) * S + qb * 128:(h // 2) * S + qb * 128 + 128][0:64, :],
                            pvs), "dve")
                    cx.run("vector", dve_pv)
                else:
                    def dve_pv(e, h=h, qb=qb, ppv=ppv, pvs=pvs):
                        e.wait_ge(cx.s["pe"], ppv)
                        cx.inc(e.tensor_copy(ahead[0:64, qb * 128:(qb + 1) * 128], pvs), "dve")
                    cx.run("vector", dve_pv)
                avguard[0] = avguard[1]
                avguard[1] = cx.c["dve"]
                it += 1
            if h % 2 == 1:
                dah = cx.c["dve"]

                def pool_ah(e, h=h, dah=dah):
                    e.wait_ge(cx.s["dve"], dah)
                    cx.inc(e.dma_start(attnT2[:, (h // 2) * S:(h // 2 + 1) * S][64:128, :],
                                       ahead[0:64, :]), "gd", 16)
                cx.run("gpsimd", pool_ah)
        bar = cx.snap()
        tap(cx, bar, "attnT2", attnT2[:, :], tap_outs, b)

        def pool_wf(e):
            cx.waits(e, bar, ["pe"])
            cx.inc(e.dma_start(s3(cx.wbuf[:, :], D), Wf.rearrange("(oc p) i -> p oc i", p=128)),
                   "gd", 16)
        cx.run("gpsimd", pool_wf)
        gwf = cx.c["gd"]
        for oc in range(DCH):
            for ic in range(DCH):
                pt = cx.PTa if (oc * DCH + ic) % 2 == 0 else cx.PTb

                def pe_wf(e, oc=oc, ic=ic, pt=pt, gwf=gwf):
                    if oc == 0 and ic == 0:
                        e.wait_ge(cx.s["gd"], gwf)
                        e.wait_ge(cx.s["dve"], cx.c["dve"])
                    if cx.c["dve"] > 1:
                        e.wait_ge(cx.s["dve"], cx.c["dve"] - 1)
                    cx.inc(e.transpose(pt[:, 0:128],
                                       cx.wbuf[:, oc * D + ic * 128: oc * D + ic * 128 + 128],
                                       cx.ident16[:, :]), "pe")
                cx.run("tensor", pe_wf)
                pwf = cx.c["pe"]

                def dve_wf(e, oc=oc, ic=ic, pt=pt, pwf=pwf):
                    e.wait_ge(cx.s["pe"], pwf)
                    cx.inc(e.tensor_copy(wT[:, ic * D + oc * 128: ic * D + oc * 128 + 128],
                                         pt[:, 0:128]), "dve")
                cx.run("vector", dve_wf)
        awf = cx.c["dve"]

        for c in range(SCH):
            slot = (c % 2) * D

            def sync_xf(e, c=c, slot=slot):
                if c == 0:
                    e.wait_ge(cx.s["act"], cx.c["act"])
                if c >= 2:
                    e.wait_ge(cx.s["gd"], cx.c["gd"])
                cx.inc(e.dma_start(cx.xch[:, slot:slot + D],
                                   xpark_d[c * 128:(c + 1) * 128, :]), "ld", 16)
            cx.run("sync", sync_xf)
            lxf = cx.c["ld"]

            def pe_f(e, c=c, awf=awf):
                if c == 0:
                    e.wait_ge(cx.s["dve"], awf)
                else:
                    e.wait_ge(cx.s["dve"], cx.c["dve"])
                for hh in range(2):
                    sl = cx.PB[:, hh * 512:(hh + 1) * 512]
                    e.matmul(sl, cx.ones16[:, :], bfrow[:, hh * 512:(hh + 1) * 512],
                             start=True, stop=False)
                    for dc in range(DCH):
                        mm = e.matmul(sl, attnT2[:, dc * S + c * 128: dc * S + c * 128 + 128],
                                      wT[:, dc * D + hh * 512: dc * D + (hh + 1) * 512],
                                      start=False, stop=(dc == DCH - 1))
                cx.inc(mm, "pe")
            cx.run("tensor", pe_f)
            pf = cx.c["pe"]

            def dve_f(e, c=c, slot=slot, pf=pf, lxf=lxf):
                e.wait_ge(cx.s["pe"], pf)
                e.wait_ge(cx.s["ld"], lxf)
                cx.inc(e.tensor_tensor(cx.xch[:, slot:slot + D], cx.xch[:, slot:slot + D],
                                       cx.PB[:, :], OP.add), "dve")
            cx.run("vector", dve_f)
            df = cx.c["dve"]

            def pool_yf(e, c=c, slot=slot, df=df, yhome=yhome):
                e.wait_ge(cx.s["dve"], df)
                cx.inc(e.dma_start(yhome[c * 128:(c + 1) * 128, :], cx.xch[:, slot:slot + D]),
                       "gd", 16)
            cx.run("gpsimd", pool_yf)
        bar = cx.snap()

    def pool_end(e):
        e.wait_ge(cx.s["gd"], cx.c["gd"])
        e.wait_ge(cx.s["ld"], cx.c["ld"])
    cx.run("gpsimd", pool_end)


TAP_SHAPES_L1 = {
    "xnT2": [128, DCH * S], "p0": [128, 16], "kdup": [128, S + 256],
    "qT2": [128, DCH * S], "attnT2": [128, DCH * S],
}


def build_l1(taps=()):
    nc = bass.Bass("TRN2", target_bir_lowering=False)
    xin = nc.dram_tensor("xin", [BPC * S, D], F32, kind="ExternalInput")
    names = dict(Wq=[D, D], Wk=[64, D], Wv=[64, D], Wf=[D, D], bf=[D],
                 W1=[D, FF], b1=[FF], W2=[FF, D], b2=[D],
                 ln_g=[D], ln_b=[D], Wr=[D, 8], br=[8])
    tt = {k: nc.dram_tensor(k, v, F32, kind="ExternalInput") for k, v in names.items()}
    thr8_d = nc.dram_tensor("thr8", [1, 8], F32, kind="ExternalInput")
    ident16_d = nc.dram_tensor("ident16", [128, 128], BF16, kind="ExternalInput")
    identf_d = nc.dram_tensor("identf", [128, 128], F32, kind="ExternalInput")
    mask1_d = nc.dram_tensor("mask1", [128, 3 * 384], BF16, kind="ExternalInput")
    y_out = nc.dram_tensor("y", [BPC * S, D], F32, kind="ExternalOutput")
    xpark_d = nc.dram_tensor("xpark", [S, D], F32)
    btmp_d = nc.dram_tensor("btmp", [128, 8], F32)
    w1p_d = nc.dram_tensor("w1p", [D, FF], BF16)
    w2p_d = nc.dram_tensor("w2p", [FF, D], BF16)
    tap_outs = {t: nc.dram_tensor("tap_" + t, TAP_SHAPES_L1[t], F32, kind="ExternalOutput")
                for t in taps}
    with ExitStack() as st:
        block = st.enter_context(nc.Block())
        cx = Cx(nc, block, st)
        alloc_common(cx)
        emit_load_consts(cx, ident16_d, identf_d)
        emit_l1_body(cx, (xin, tt["Wq"], tt["Wk"], tt["Wv"], tt["Wf"], tt["bf"],
                          tt["W1"], tt["b1"], tt["W2"], tt["b2"], tt["ln_g"],
                          tt["ln_b"], tt["Wr"], tt["br"], thr8_d, y_out,
                          xpark_d, btmp_d, w1p_d, w2p_d, mask1_d), tap_outs)
    return nc



def _consts():
    i16 = np.eye(128, dtype=ml_dtypes.bfloat16)
    idf = np.eye(128, dtype=np.float32)
    m0 = np.full((128, 128), NEGBIG, np.float32)
    m0[:64, :64] = 0.0
    m0[64:, 64:] = 0.0
    m0 = m0.astype(ml_dtypes.bfloat16)
    q = np.arange(128)[:, None]
    k = np.arange(384)[None, :]
    diff = q - (k - 256)
    band = (diff >= 0) & (diff <= 256)
    ms = []
    for qb in (0, 1, 2):
        valid = band & (k >= max(0, 256 - qb * 128))
        ms.append(np.where(valid, 0.0, NEGBIG).astype(np.float32))
    m1 = np.concatenate(ms, axis=1).astype(ml_dtypes.bfloat16)
    return i16, idf, m0, m1


def _thr_inputs(inputs):
    specs = [("L1_ffn_W1", "bot"), ("L1_ffn_W1", "top"),
             ("L1_ffn_W2", "bot"), ("L1_ffn_W2", "top"),
             ("L1_ffn_b1", "bot"), ("L1_ffn_b1", "top"),
             ("L1_ffn_b2", "bot"), ("L1_ffn_b2", "top")]
    out = []
    NTOT = 128 * 32768
    for name, side in specs:
        w = np.asarray(inputs[name], np.float32).ravel()
        n = w.size
        k = max(1, int(n * 0.1 / 2))
        kk = k if side == "bot" else n - k + 1
        pad = NTOT - n
        arr = np.full(NTOT, -1.0e-30, np.float32)
        arr[:n] = w
        out.append((arr.reshape(128, 32768), np.array([[float(kk + pad)]], np.float32)))
    return out


def kernel(**inputs):
    i16, idf, m0, m1 = _consts()

    if "thr" not in _CACHE:
        _CACHE["thr"] = build_thr_program()
    thr_maps = [dict(search=a, kk=k, identf=idf) for a, k in _thr_inputs(inputs)]
    res_t = run_bass_kernel_spmd(_CACHE["thr"], thr_maps, core_ids=list(range(NCORE)))
    thr8 = np.ascontiguousarray(np.asarray(res_t.results[0]["thr8"], np.float32))

    if "l0" not in _CACHE:
        _CACHE["l0"] = build_l0()
    src = np.ascontiguousarray(np.asarray(inputs["L0_attn_src"], np.float32).reshape(32, S, D))
    common0 = dict(
        Wq=np.ascontiguousarray(np.asarray(inputs["L0_attn_Wq"], np.float32)),
        bq=np.ascontiguousarray(np.asarray(inputs["L0_attn_bq"], np.float32)),
        Wk=np.ascontiguousarray(np.asarray(inputs["L0_attn_Wk"], np.float32)),
        bk=np.ascontiguousarray(np.asarray(inputs["L0_attn_bk"], np.float32)),
        Wv=np.ascontiguousarray(np.asarray(inputs["L0_attn_Wv"], np.float32)),
        bv=np.ascontiguousarray(np.asarray(inputs["L0_attn_bv"], np.float32)),
        Wo=np.ascontiguousarray(np.asarray(inputs["L0_attn_Wo"], np.float32)),
        bo=np.ascontiguousarray(np.asarray(inputs["L0_attn_bo"], np.float32)),
        gamma_x=np.ascontiguousarray(np.asarray(inputs["L0_ffn_gamma_x"], np.float32)),
        gamma_slot=np.ascontiguousarray(np.asarray(inputs["L0_ffn_gamma_slot"], np.float32)),
        semb=np.ascontiguousarray(np.asarray(inputs["L0_ffn_slot_embeds"], np.float32).reshape(ET, D)),
        w1=np.ascontiguousarray(np.asarray(inputs["L0_ffn_w1"], np.float32)),
        w2=np.ascontiguousarray(np.asarray(inputs["L0_ffn_w2"], np.float32)),
        ident16=i16, identf=idf, mask0=m0,
    )
    maps0 = [dict(common0, src=np.ascontiguousarray(
        src[c * BPC:(c + 1) * BPC].reshape(BPC * S, D))) for c in range(NCORE)]
    res0 = run_bass_kernel_spmd(_CACHE["l0"], maps0, core_ids=list(range(NCORE)))

    if "l1" not in _CACHE:
        _CACHE["l1"] = build_l1()
    common1 = dict(
        Wq=np.ascontiguousarray(np.asarray(inputs["L1_attn_Wq"], np.float32)),
        Wk=np.ascontiguousarray(np.asarray(inputs["L1_attn_Wk"], np.float32)),
        Wv=np.ascontiguousarray(np.asarray(inputs["L1_attn_Wv"], np.float32)),
        Wf=np.ascontiguousarray(np.asarray(inputs["L1_attn_Wf"], np.float32)),
        bf=np.ascontiguousarray(np.asarray(inputs["L1_attn_bf"], np.float32)),
        W1=np.ascontiguousarray(np.asarray(inputs["L1_ffn_W1"], np.float32)),
        b1=np.ascontiguousarray(np.asarray(inputs["L1_ffn_b1"], np.float32)),
        W2=np.ascontiguousarray(np.asarray(inputs["L1_ffn_W2"], np.float32)),
        b2=np.ascontiguousarray(np.asarray(inputs["L1_ffn_b2"], np.float32)),
        ln_g=np.ascontiguousarray(np.asarray(inputs["L1_ffn_ln_gamma"], np.float32)),
        ln_b=np.ascontiguousarray(np.asarray(inputs["L1_ffn_ln_beta"], np.float32)),
        Wr=np.ascontiguousarray(np.asarray(inputs["L1_ffn_Wr"], np.float32)),
        br=np.ascontiguousarray(np.asarray(inputs["L1_ffn_br"], np.float32)),
        thr8=thr8, ident16=i16, identf=idf, mask1=m1,
    )
    maps1 = [dict(common1, xin=np.ascontiguousarray(
        np.asarray(res0.results[c]["x1"], np.float32))) for c in range(NCORE)]
    res1 = run_bass_kernel_spmd(_CACHE["l1"], maps1, core_ids=list(range(NCORE)))
    out = np.concatenate([np.asarray(r["y"], np.float32).reshape(BPC, S, D)
                          for r in res1.results], axis=0)
    return out



def build_fused(taps=()):
    nc = bass.Bass("TRN2", target_bir_lowering=False)
    src_d = nc.dram_tensor("src", [BPC * S, D], F32, kind="ExternalInput")
    n0 = dict(Wq=[D, D], bq=[D], Wk=[D, D], bk=[D], Wv=[D, D], bv=[D],
              Wo=[D, D], bo=[D], gamma_x=[D], gamma_slot=[D],
              semb=[ET, D], w1=[D, FF], w2=[FF, D])
    t0 = {k: nc.dram_tensor(k, v, F32, kind="ExternalInput") for k, v in n0.items()}
    n1 = dict(Wq1=[D, D], Wk1=[64, D], Wv1=[64, D], Wf1=[D, D], bf1=[D],
              W1=[D, FF], b1=[FF], W2=[FF, D], b2=[D],
              ln_g=[D], ln_b=[D], Wr=[D, 8], br=[8])
    t1 = {k: nc.dram_tensor(k, v, F32, kind="ExternalInput") for k, v in n1.items()}
    search = nc.dram_tensor("search", [128, 32768], F32, kind="ExternalInput")
    kk = nc.dram_tensor("kk", [1, 1], F32, kind="ExternalInput")
    ident16_d = nc.dram_tensor("ident16", [128, 128], BF16, kind="ExternalInput")
    identf_d = nc.dram_tensor("identf", [128, 128], F32, kind="ExternalInput")
    mask0_d = nc.dram_tensor("mask0", [128, 128], BF16, kind="ExternalInput")
    mask1_d = nc.dram_tensor("mask1", [128, 3 * 384], BF16, kind="ExternalInput")
    y_out = nc.dram_tensor("y", [BPC * S, D], F32, kind="ExternalOutput")
    x1_d = nc.dram_tensor("x1scr", [BPC * S, D], F32)
    xpark_d = nc.dram_tensor("xpark", [S, D], F32)
    xnT_park = nc.dram_tensor("xnT_park", [D, S], BF16)
    slots_park = [nc.dram_tensor(f"slots_park{i}", [D, S], BF16) for i in range(2)]
    sepT_d = nc.dram_tensor("sepT", [D, ET], BF16)
    btmp_d = nc.dram_tensor("btmp", [128, 8], F32)
    w1p_d = nc.dram_tensor("w1p", [D, FF], BF16)
    w2p_d = nc.dram_tensor("w2p", [FF, D], BF16)
    cc_in = nc.dram_tensor("cc_in", [1, 128], F32)
    cc_out = nc.dram_tensor("cc_out", [8, 128], F32)

    with ExitStack() as st:
        block = st.enter_context(nc.Block())
        cx = Cx(nc, block, st)
        cx.t_sum = cx.sb("t_sum", [128, 1], F32)
        cx.t_ssq = cx.sb("t_ssq", [128, 1], F32)
        cx.t_mean = cx.sb("t_mean", [128, 1], F32)
        cx.t_m2 = cx.sb("t_m2", [128, 1], F32)
        cx.t_var = cx.sb("t_var", [128, 1], F32)
        cx.t_sd = cx.sb("t_sd", [128, 1], F32)
        cx.t_sdv = cx.sb("t_sdv", [128, 1], F32)
        cx.t_rstd = cx.sb("t_rstd", [128, 1], F32)
        cx.t_nrm = cx.sb("t_nrm", [128, 1], F32)
        cx.t_tau = cx.sb("t_tau", [128, 1], F32)
        cx.t_rt = cx.sb("t_rt", [128, 1], F32)
        cx.t_neg = cx.sb("t_neg", [128, 1], F32)
        cx.t_asum = cx.sb("t_asum", [128, 1], F32)
        cx.t_ra = cx.sb("t_ra", [128, 1], F32)
        cx.ident16 = cx.sb("ident16_s", [128, 128], BF16)
        cx.identf = cx.sb("identf_s", [128, 128], F32)
        cx.onesf = cx.sb("onesf_s", [1, 128], F32)
        cx.epsb = cx.sb("epsb_s", [128, 1], F32)
        cx.ones16 = cx.sb("ones16_s", [1, 128], BF16)
        cx.xch = cx.sb("xch", [128, 2 * D], F32)
        cx.dchunk = cx.sb("dchunk", [128, S], BF16)
        cx.t_dump = cx.dchunk[:, 0:D]
        cx.t_xn = cx.dchunk[:, D:2 * D]
        cx.PA = cx.ps("PA", [128, 2048], F32)
        cx.PB = cx.ps("PB", [128, 1024], F32)
        cx.PTa = cx.ps("PTa", [128, 1024], BF16)
        cx.PTb = cx.ps("PTb", [128, 1024], BF16)
        thr8_s = cx.sb("thr8_s", [1, 8], F32)
        emit_load_consts(cx, ident16_d, identf_d)
        bar = cx.snap()
        with ExitStack() as st2:
            saved = cx.st
            cx.st = st2
            bar = emit_threshold_search(cx, bar, search, kk, cc_in, cc_out, thr8_s)
            cx.st = saved
        def pool_bar(e):
            cx.waits(e, bar)
            e.wait_ge(cx.s["gd"], cx.c["gd"])
            e.sem_inc(cx.s["cc"], 1)
        cx.run("gpsimd", pool_bar)
        cx.c["cc"] += 1
        cx.BIG1 = cx.sb("BIG1", [128, 32768], BF16)
        cx.BIG2 = cx.sb("BIG2", [128, 32768], BF16)
        cx.M3 = cx.sb("M3", [128, 16384], BF16)
        cx.wbuf = cx.sb("wbuf", [128, 8192], BF16)
        bar = cx.snap()
        with ExitStack() as stl0:
            saved = cx.st
            cx.st = stl0
            emit_l0_body(cx, (src_d, t0["Wq"], t0["bq"], t0["Wk"], t0["bk"], t0["Wv"],
                              t0["bv"], t0["Wo"], t0["bo"], t0["gamma_x"],
                              t0["gamma_slot"], t0["semb"], t0["w1"], t0["w2"],
                              x1_d, xpark_d, xnT_park, slots_park, sepT_d,
                              mask0_d), {t: None for t in ()})
            cx.st = saved

        def pool_bar2(e):
            cx.waits(e, cx.snap())
            e.wait_ge(cx.s["gd"], cx.c["gd"])
            e.wait_ge(cx.s["ld"], cx.c["ld"])
            e.sem_inc(cx.s["cc"], 1)
        cx.run("gpsimd", pool_bar2)
        cx.c["cc"] += 1
        emit_l1_body(cx, (x1_d, t1["Wq1"], t1["Wk1"], t1["Wv1"], t1["Wf1"],
                          t1["bf1"], t1["W1"], t1["b1"], t1["W2"], t1["b2"],
                          t1["ln_g"], t1["ln_b"], t1["Wr"], t1["br"], None,
                          y_out, xpark_d, btmp_d, w1p_d, w2p_d, mask1_d),
                     {}, thr8_tile=thr8_s)
    return nc


def kernel_fused(**inputs):
    i16, idf, m0, m1 = _consts()
    if "fused" not in _CACHE:
        _CACHE["fused"] = build_fused()
    src = np.ascontiguousarray(np.asarray(inputs["L0_attn_src"], np.float32).reshape(32, S, D))
    segpos = np.asarray(inputs["L1_attn_segment_pos"])
    common = dict(
        Wq=np.ascontiguousarray(np.asarray(inputs["L0_attn_Wq"], np.float32)),
        bq=np.ascontiguousarray(np.asarray(inputs["L0_attn_bq"], np.float32)),
        Wk=np.ascontiguousarray(np.asarray(inputs["L0_attn_Wk"], np.float32)),
        bk=np.ascontiguousarray(np.asarray(inputs["L0_attn_bk"], np.float32)),
        Wv=np.ascontiguousarray(np.asarray(inputs["L0_attn_Wv"], np.float32)),
        bv=np.ascontiguousarray(np.asarray(inputs["L0_attn_bv"], np.float32)),
        Wo=np.ascontiguousarray(np.asarray(inputs["L0_attn_Wo"], np.float32)),
        bo=np.ascontiguousarray(np.asarray(inputs["L0_attn_bo"], np.float32)),
        gamma_x=np.ascontiguousarray(np.asarray(inputs["L0_ffn_gamma_x"], np.float32)),
        gamma_slot=np.ascontiguousarray(np.asarray(inputs["L0_ffn_gamma_slot"], np.float32)),
        semb=np.ascontiguousarray(np.asarray(inputs["L0_ffn_slot_embeds"], np.float32).reshape(ET, D)),
        w1=np.ascontiguousarray(np.asarray(inputs["L0_ffn_w1"], np.float32)),
        w2=np.ascontiguousarray(np.asarray(inputs["L0_ffn_w2"], np.float32)),
        Wq1=np.ascontiguousarray(np.asarray(inputs["L1_attn_Wq"], np.float32)),
        Wk1=np.ascontiguousarray(np.asarray(inputs["L1_attn_Wk"], np.float32)),
        Wv1=np.ascontiguousarray(np.asarray(inputs["L1_attn_Wv"], np.float32)),
        Wf1=np.ascontiguousarray(np.asarray(inputs["L1_attn_Wf"], np.float32)),
        bf1=np.ascontiguousarray(np.asarray(inputs["L1_attn_bf"], np.float32)),
        W1=np.ascontiguousarray(np.asarray(inputs["L1_ffn_W1"], np.float32)),
        b1=np.ascontiguousarray(np.asarray(inputs["L1_ffn_b1"], np.float32)),
        W2=np.ascontiguousarray(np.asarray(inputs["L1_ffn_W2"], np.float32)),
        b2=np.ascontiguousarray(np.asarray(inputs["L1_ffn_b2"], np.float32)),
        ln_g=np.ascontiguousarray(np.asarray(inputs["L1_ffn_ln_gamma"], np.float32)),
        ln_b=np.ascontiguousarray(np.asarray(inputs["L1_ffn_ln_beta"], np.float32)),
        Wr=np.ascontiguousarray(np.asarray(inputs["L1_ffn_Wr"], np.float32)),
        br=np.ascontiguousarray(np.asarray(inputs["L1_ffn_br"], np.float32)),
        ident16=i16, identf=idf, mask0=m0, mask1=m1,
    )
    thr_in = _thr_inputs(inputs)
    maps = []
    for c in range(NCORE):
        arr, kkv = thr_in[c]
        maps.append(dict(common, src=np.ascontiguousarray(
            src[c * BPC:(c + 1) * BPC].reshape(BPC * S, D)),
            search=arr, kk=kkv))
    res = run_bass_kernel_spmd(_CACHE["fused"], maps, core_ids=list(range(NCORE)))
    return np.concatenate([np.asarray(r["y"], np.float32).reshape(BPC, S, D)
                           for r in res.results], axis=0)


_UNFUSED_KERNEL = kernel
kernel = kernel_fused
